# version 02
# baseline (speedup 1.0000x reference)
import math
import numpy as np
import ml_dtypes
import concourse.bass as bass
import concourse.mybir as mybir
from concourse.bass_utils import run_bass_kernel_spmd

F32 = mybir.dt.float32
BF16 = mybir.dt.bfloat16
I16 = mybir.dt.int16
I32 = mybir.dt.int32
AF = mybir.ActivationFunctionType
ALU = mybir.AluOpType
AX = mybir.AxisListType

BF = ml_dtypes.bfloat16

D = 2048
CP = 512
T = 16384
SL = 4096
B = 4
DH = 128
NH = 16
EPS = 1e-5
ISQ = 1.0 / math.sqrt(128.0)

NCORES = 8
TB = 512


class Sched:
    ENGS = ("sync", "gpsimd", "vector", "scalar", "tensor")

    def __init__(self, nc):
        self.nc = nc
        self.ops = {e: [] for e in self.ENGS}
        self.handles = {}
        self.cnt = {}

    def sem(self, name):
        if name not in self.handles:
            self.handles[name] = self.nc.alloc_semaphore(name)
            self.cnt[name] = 0
        return self.handles[name]

    def wait(self, eng, name, val=None):
        h = self.sem(name)
        v = self.cnt[name] if val is None else val
        if v > 0:
            self.ops[eng].append(lambda e, h=h, v=v: e.wait_ge(h, v))

    def wait_fa(self, eng, free_at):
        for name, v in free_at.items():
            self.wait(eng, name, v)

    def emit(self, eng, fn, inc=None, inc_n=1):
        if inc is not None:
            h = self.sem(inc)
            self.cnt[inc] += inc_n
            self.ops[eng].append(
                lambda e, fn=fn, h=h, n=inc_n: fn(e).then_inc(h, n))
        else:
            self.ops[eng].append(fn)

    def dma(self, eng, out, in_, inc):
        self.emit(eng, lambda e, o=out, i=in_: e.dma_start(out=o, in_=i),
                  inc=inc, inc_n=16)

    def val(self, name):
        return self.cnt.get(name, 0)

    def mark(self, *names):
        return {n: self.val(n) for n in names}

    def n_ops(self):
        return sum(len(v) for v in self.ops.values())

    def build(self):
        with self.nc.Block() as blk:
            for eng in self.ENGS:
                if self.ops[eng]:
                    def runner(e, eng=eng):
                        for f in self.ops[eng]:
                            f(e)
                    getattr(blk, eng)(runner)



def emit_A(nc, S, G):
    NTB = T // TB

    x_fm = G["x_fm"]; wdq = G["wdq"]; wdkv = G["wdkv"]
    wuq_m = G["wuq_m"]; wk_m = G["wk_m"]; wv_m = G["wv_m"]
    qg_i = G["qg_i"]; qb_i = G["qb_i"]; kg_i = G["kg_i"]; kb_i = G["kb_i"]
    ones_i = G["ones_i"]; onesr_i = G["onesr_i"]; trim_i = G["trim_i"]
    a2a_in = G["a2a_in"]
    k_scr = G["k_scr"]; cbounce = G["cbounce"]; v_scr = G["v_scr"]

    ARX = G["ARX"]; ARY = G["ARY"]; ARZW = G["ARZW"]; WU = G["WU"]
    q_sb = ARX
    xab = [ARY[:, 0:8192], ARY[:, 8192:16384]]
    xsq = ARZW[:, 0:8192]
    wdq_sb = ARZW[:, 8192:16384]
    wdkv_sb = WU[:, 0:8192]
    raw_sb = WU[:, 8192:10240]
    sq_sb = WU[:, 10240:12288]
    nrm_sb = WU[:, 12288:14336]
    wuq_sb = WU[:, 14336:15360]
    wk_sb = WU[:, 15360:16384]
    wv_sb = WU[:, 16384:17408]
    trim_sb = WU[:, 17408:19456]
    kev_sb = G["S_sb"]
    vev_sb = G["sT_sb"][:, 0:256]
    o_sb = G["P_sb"]
    rec_sb = G["rt_f"]
    qg_sb = G["qg_sb"]; qb_sb = G["qb_sb"]; kg_sb = G["kg_sb"]; kb_sb = G["kb_sb"]
    ones_sb = G["ones_sb"]; onesr_sb = G["onesr_sb"]
    st_f = G["st_f"]; st_b = G["st_b"]
    qn2 = G["qn2"]; kn2 = G["kn2"]
    sc_sb = G["sc_sb"]; cb_sb = G["cb_sb"]
    pp = G["pp"]; bc = G["bc"]; stp = G["stp"]; sum_ps = G["sum_ps"]

    fa_pp = [dict(), dict(), dict()]
    fa_bc = [dict(), dict()]
    fa_stp = [dict(), dict()]
    fa_sum = dict()
    fa_stf = dict()
    fa_xab = [dict(), dict()]
    fa_xsq = dict()
    fa_raw = dict()
    fa_sq = dict()
    fa_nrm = dict()
    fa_osb = dict()
    fa_kev = dict()
    fa_vev = dict()
    fa_cb = dict()
    fa_scb = dict()

    for dc in range(16):
        S.dma("gpsimd", wdq_sb[:, dc * CP:(dc + 1) * CP],
              wdq[dc * 128:(dc + 1) * 128, :], "g")
        S.dma("gpsimd", wdkv_sb[:, dc * CP:(dc + 1) * CP],
              wdkv[dc * 128:(dc + 1) * 128, :], "g")
    for cc in range(4):
        S.dma("gpsimd", wuq_sb[:, cc * 256:(cc + 1) * 256],
              wuq_m[cc * 128:(cc + 1) * 128, :], "g")
        S.dma("gpsimd", wk_sb[:, cc * 256:(cc + 1) * 256],
              wk_m[cc * 128:(cc + 1) * 128, :], "g")
        S.dma("gpsimd", wv_sb[:, cc * 256:(cc + 1) * 256],
              wv_m[cc * 128:(cc + 1) * 128, :], "g")
    S.dma("sync", qg_sb[:], qg_i, "s")
    S.dma("sync", qb_sb[:], qb_i, "s")
    S.dma("sync", kg_sb[:], kg_i, "s")
    S.dma("sync", kb_sb[:], kb_i, "s")
    S.dma("sync", ones_sb[:], ones_i, "s")
    S.dma("sync", onesr_sb[:], onesr_i, "s")
    for j in range(4):
        S.dma("sync", trim_sb[:, j * TB:(j + 1) * TB], trim_i[j], "s")
    w_g = S.val("g")
    w_s = S.val("s")

    def load_x_tile(tb, buf):
        S.wait_fa("gpsimd", fa_xab[buf])
        for dc in range(16):
            S.dma("gpsimd", xab[buf][:, dc * TB:(dc + 1) * TB],
                  x_fm[dc * 128:(dc + 1) * 128, tb * TB:(tb + 1) * TB], "g")
        return S.val("g")

    g_loaded = {0: load_x_tile(0, 0)}

    S.wait("tensor", "g", w_g)
    S.wait("tensor", "s", w_s)
    S.wait("scalar", "g", w_g)
    S.wait("scalar", "s", w_s)

    def ln_stats(n_inv):
        S.wait("vector", "p")
        S.wait_fa("vector", fa_stf)
        S.emit("vector", lambda e: e.tensor_scalar_mul(st_f[0][:], stp[0][:], n_inv))
        S.emit("vector", lambda e: e.tensor_scalar_mul(st_f[1][:], stp[1][:], n_inv))
        S.emit("vector", lambda e: e.tensor_tensor(st_f[2][:], st_f[0][:], st_f[0][:], ALU.mult),
               inc="v")
        S.wait("vector", "v")
        S.emit("vector", lambda e: e.tensor_tensor(st_f[1][:], st_f[1][:], st_f[2][:], ALU.subtract))
        S.emit("vector", lambda e: e.tensor_scalar_add(st_f[1][:], st_f[1][:], EPS),
               inc="v")
        fa_stp[0]["v"] = S.val("v")
        fa_stp[1]["v"] = S.val("v")
        S.wait("scalar", "v")
        S.emit("scalar", lambda e: e.activation(st_f[1][:], st_f[1][:], AF.Ln))
        S.emit("scalar", lambda e: e.activation(st_f[1][:], st_f[1][:], AF.Exp, scale=-0.5),
               inc="a")
        S.wait("vector", "a")
        S.emit("vector", lambda e: e.tensor_copy(st_b[0][:], st_f[0][:]))
        S.emit("vector", lambda e: e.tensor_copy(st_b[1][:], st_f[1][:]), inc="v")
        S.wait("tensor", "v")
        S.wait_fa("tensor", fa_bc[0])
        S.wait_fa("tensor", fa_bc[1])
        S.emit("tensor", lambda e: e.matmul(bc[0][:], onesr_sb[:], st_b[0][:],
                                            start=True, stop=True))
        S.emit("tensor", lambda e: e.matmul(bc[1][:], onesr_sb[:], st_b[1][:],
                                            start=True, stop=True), inc="p")
        fa_stf["p"] = S.val("p")

    for tb in range(NTB):
        buf = tb % 2
        S.wait("scalar", "g", g_loaded[tb])
        S.wait_fa("scalar", fa_xsq)
        for dc in range(16):
            S.emit("scalar", lambda e, dc=dc, buf=buf: e.activation(
                xsq[:, dc * TB:(dc + 1) * TB],
                xab[buf][:, dc * TB:(dc + 1) * TB], AF.Square),
                inc=("a" if dc == 15 else None))
        a_sq = S.val("a")
        S.wait("tensor", "g", g_loaded[tb])
        S.wait_fa("tensor", fa_stp[0])
        for dc in range(16):
            S.emit("tensor", lambda e, dc=dc, buf=buf: e.matmul(
                stp[0][:], ones_sb[:], xab[buf][:, dc * TB:(dc + 1) * TB],
                start=(dc == 0), stop=(dc == 15)),
                inc=("p" if dc == 15 else None))
        S.wait("tensor", "a", a_sq)
        S.wait_fa("tensor", fa_stp[1])
        for dc in range(16):
            S.emit("tensor", lambda e, dc=dc: e.matmul(
                stp[1][:], ones_sb[:], xsq[:, dc * TB:(dc + 1) * TB],
                start=(dc == 0), stop=(dc == 15)),
                inc=("p" if dc == 15 else None))
        ln_stats(1.0 / D)
        S.wait("vector", "p")
        for dc in range(16):
            S.emit("vector", lambda e, dc=dc, buf=buf: e.tensor_tensor(
                xsq[:, dc * TB:(dc + 1) * TB],
                xab[buf][:, dc * TB:(dc + 1) * TB], bc[0][:], ALU.subtract))
            S.emit("vector", lambda e, dc=dc: e.tensor_tensor(
                xsq[:, dc * TB:(dc + 1) * TB],
                xsq[:, dc * TB:(dc + 1) * TB], bc[1][:], ALU.mult),
                inc=("v" if dc == 15 else None))
        v_xa = S.val("v")
        fa_xab[buf] = {"v": v_xa}
        fa_bc[0] = {"v": v_xa}
        fa_bc[1] = {"v": v_xa}
        if tb + 1 < NTB:
            g_loaded[tb + 1] = load_x_tile(tb + 1, 1 - buf)

        for br in range(2):
            wsb = wdq_sb if br == 0 else wdkv_sb
            gg = qg_sb if br == 0 else kg_sb
            bbv = qb_sb if br == 0 else kb_sb
            S.wait("tensor", "v", v_xa)
            for cc in range(4):
                bank = cc % 2
                S.wait_fa("tensor", fa_pp[bank])
                for dc in range(16):
                    S.emit("tensor", lambda e, dc=dc, cc=cc, wsb=wsb, bank=bank: e.matmul(
                        pp[bank][:],
                        wsb[:, dc * CP + cc * 128: dc * CP + (cc + 1) * 128],
                        xsq[:, dc * TB:(dc + 1) * TB],
                        start=(dc == 0), stop=(dc == 15)),
                        inc=("p" if dc == 15 else None))
                p_proj = S.val("p")
                S.wait("scalar", "p", p_proj)
                if cc < 2:
                    S.wait_fa("scalar", fa_raw)
                S.emit("scalar", lambda e, cc=cc, bank=bank: e.activation(
                    raw_sb[:, cc * TB:(cc + 1) * TB], pp[bank][:], AF.Identity),
                    inc="a")
                fa_pp[bank] = {"a": S.val("a")}
            a_raw = S.val("a")
            S.wait_fa("scalar", fa_sq)
            for cc in range(4):
                S.emit("scalar", lambda e, cc=cc: e.activation(
                    sq_sb[:, cc * TB:(cc + 1) * TB],
                    raw_sb[:, cc * TB:(cc + 1) * TB], AF.Square),
                    inc=("a" if cc == 3 else None))
            a_sq2 = S.val("a")
            S.wait("tensor", "a", a_raw)
            S.wait_fa("tensor", fa_stp[0])
            for cc in range(4):
                S.emit("tensor", lambda e, cc=cc: e.matmul(
                    stp[0][:], ones_sb[:], raw_sb[:, cc * TB:(cc + 1) * TB],
                    start=(cc == 0), stop=(cc == 3)),
                    inc=("p" if cc == 3 else None))
            S.wait("tensor", "a", a_sq2)
            S.wait_fa("tensor", fa_stp[1])
            for cc in range(4):
                S.emit("tensor", lambda e, cc=cc: e.matmul(
                    stp[1][:], ones_sb[:], sq_sb[:, cc * TB:(cc + 1) * TB],
                    start=(cc == 0), stop=(cc == 3)),
                    inc=("p" if cc == 3 else None))
            ln_stats(1.0 / CP)
            S.wait("vector", "p")
            S.wait_fa("vector", fa_sq)
            for cc in range(4):
                S.emit("vector", lambda e, cc=cc: e.tensor_tensor(
                    sq_sb[:, cc * TB:(cc + 1) * TB],
                    raw_sb[:, cc * TB:(cc + 1) * TB], bc[0][:], ALU.subtract))
                S.emit("vector", lambda e, cc=cc: e.tensor_tensor(
                    sq_sb[:, cc * TB:(cc + 1) * TB],
                    sq_sb[:, cc * TB:(cc + 1) * TB], bc[1][:], ALU.mult),
                    inc=("v" if cc == 3 else None))
            v_n0 = S.val("v")
            fa_bc[0] = {"v": v_n0}
            fa_bc[1] = {"v": v_n0}
            fa_raw = {"v": v_n0}
            S.wait("scalar", "v", v_n0)
            S.wait_fa("scalar", fa_nrm)
            for cc in range(4):
                S.emit("scalar", lambda e, cc=cc, gg=gg, bbv=bbv: e.activation(
                    nrm_sb[:, cc * TB:(cc + 1) * TB],
                    sq_sb[:, cc * TB:(cc + 1) * TB], AF.Identity,
                    bias=bbv[:, cc:cc + 1], scale=gg[:, cc:cc + 1]),
                    inc=("a" if cc == 3 else None))
            a_nrm = S.val("a")
            fa_sq["a"] = a_nrm

            if br == 0:
                S.wait("tensor", "a", a_nrm)
                for oc in range(2):
                    S.wait_fa("tensor", fa_pp[2])
                    for cc in range(4):
                        S.emit("tensor", lambda e, cc=cc, oc=oc: e.matmul(
                            pp[2][:],
                            wuq_sb[:, cc * 256 + oc * 128: cc * 256 + (oc + 1) * 128],
                            nrm_sb[:, cc * TB:(cc + 1) * TB],
                            start=(cc == 0), stop=(cc == 3)),
                            inc=("p" if cc == 3 else None))
                    p_q = S.val("p")
                    S.wait("scalar", "p", p_q)
                    S.emit("scalar", lambda e, oc=oc, tb=tb: e.activation(
                        q_sb[:, oc * T + tb * TB: oc * T + (tb + 1) * TB],
                        pp[2][:], AF.Identity), inc="a")
                    S.wait_fa("scalar", fa_sq)
                    S.emit("scalar", lambda e: e.activation(
                        sq_sb[:, 0:TB], pp[2][:], AF.Square), inc="a")
                    a_qs = S.val("a")
                    fa_pp[2] = {"a": a_qs}
                    S.wait("tensor", "a", a_qs)
                    S.wait_fa("tensor", fa_stp[0])
                    S.emit("tensor", lambda e: e.matmul(
                        stp[0][:], ones_sb[:], sq_sb[:, 0:TB],
                        start=True, stop=True), inc="p")
                    p_qn = S.val("p")
                    fa_sq = {"p": p_qn}
                    S.wait("vector", "p", p_qn)
                    S.emit("vector", lambda e, oc=oc, tb=tb: e.reduce_max(
                        qn2[0:1, oc * NTB + tb: oc * NTB + tb + 1], stp[0][:],
                        axis=AX.X), inc="v")
                    fa_stp[0] = {"v": S.val("v")}
                fa_nrm = {"p": S.val("p")}
            else:
                S.wait("tensor", "a", a_nrm)
                for oc in range(2):
                    S.wait_fa("tensor", fa_pp[2])
                    for cc in range(4):
                        S.emit("tensor", lambda e, cc=cc, oc=oc: e.matmul(
                            pp[2][:],
                            wk_sb[:, cc * 256 + oc * 128: cc * 256 + (oc + 1) * 128],
                            nrm_sb[:, cc * TB:(cc + 1) * TB],
                            start=(cc == 0), stop=(cc == 3)),
                            inc=("p" if cc == 3 else None))
                    p_k = S.val("p")
                    S.wait("scalar", "p", p_k)
                    S.wait_fa("scalar", fa_kev)
                    S.emit("scalar", lambda e: e.activation(
                        kev_sb[:], pp[2][:], AF.Identity), inc="a")
                    a_ke = S.val("a")
                    S.wait_fa("scalar", fa_sq)
                    S.emit("scalar", lambda e: e.activation(
                        sq_sb[:, 0:TB], pp[2][:], AF.Square), inc="a")
                    a_ks = S.val("a")
                    fa_pp[2] = {"a": a_ks}
                    S.wait("sync", "a", a_ke)
                    S.dma("sync", k_scr[oc * 128:(oc + 1) * 128, tb * TB:(tb + 1) * TB],
                          kev_sb[:], "ks")
                    fa_kev = {"ks": S.val("ks")}
                    S.wait("tensor", "a", a_ks)
                    S.wait_fa("tensor", fa_stp[0])
                    S.emit("tensor", lambda e: e.matmul(
                        stp[0][:], ones_sb[:], sq_sb[:, 0:TB],
                        start=True, stop=True), inc="p")
                    p_kn = S.val("p")
                    fa_sq = {"p": p_kn}
                    S.wait("vector", "p", p_kn)
                    S.emit("vector", lambda e, oc=oc, tb=tb: e.reduce_max(
                        kn2[0:1, oc * NTB + tb: oc * NTB + tb + 1], stp[0][:],
                        axis=AX.X), inc="v")
                    fa_stp[0] = {"v": S.val("v")}
                for ts_ in range(4):
                    S.wait_fa("tensor", fa_pp[2])
                    for cc in range(4):
                        S.emit("tensor", lambda e, cc=cc, ts_=ts_: e.matmul(
                            pp[2][:, 0:256],
                            nrm_sb[:, cc * TB + ts_ * 128: cc * TB + ts_ * 128 + 128],
                            wv_sb[:, cc * 256:(cc + 1) * 256],
                            start=(cc == 0), stop=(cc == 3)),
                            inc=("p" if cc == 3 else None))
                    p_v = S.val("p")
                    S.wait("scalar", "p", p_v)
                    S.wait_fa("scalar", fa_vev)
                    S.emit("scalar", lambda e: e.activation(
                        vev_sb[:], pp[2][:, 0:256], AF.Identity), inc="a")
                    a_ve = S.val("a")
                    fa_pp[2] = {"a": a_ve}
                    S.wait("sync", "a", a_ve)
                    S.dma("sync",
                          v_scr[tb * TB + ts_ * 128: tb * TB + (ts_ + 1) * 128, :],
                          vev_sb[:], "ks")
                    fa_vev = {"ks": S.val("ks")}
                fa_nrm = {"p": S.val("p")}
        fa_xsq["p"] = S.val("p")

    kb_ = [xab[0][:, 0:4096], xab[1][:, 0:4096]]
    vb_ = [xab[0][:, 4096:8192], xab[1][:, 4096:8192]]
    pt_ = [xsq[:, 0:TB], xsq[:, TB:2 * TB]]
    fa_kvbuf = [dict(fa_xab[0]), dict(fa_xab[1])]
    fa_pt = [dict(fa_xsq), dict(fa_xsq)]

    ks_all = S.val("ks")
    S.wait("sync", "ks", ks_all)

    pairs = [(b, hl) for b in range(B) for hl in range(2)]
    a_exp = {}
    p_cons = {}
    fa_recsb = dict()

    for pi, (b, hl) in enumerate(pairs):
        lb = pi % 2
        S.wait_fa("sync", fa_kvbuf[lb])
        S.dma("sync", kb_[lb], k_scr[hl * 128:(hl + 1) * 128,
                                     b * SL:(b + 1) * SL], "kv")
        vsrc3 = v_scr[b * SL:(b + 1) * SL, hl * 128:(hl + 1) * 128] \
            .rearrange("(kc p) f -> p kc f", p=128)
        vb3 = vb_[lb].rearrange("p (kc f) -> p kc f", f=128)
        for qu in range(4):
            S.dma("sync", vb3[:, qu * 8:(qu + 1) * 8, :],
                  vsrc3[:, qu * 8:(qu + 1) * 8, :], "kv")
        kv_loaded = S.val("kv")

        S.wait("vector", "v")
        S.wait_fa("vector", fa_scb)
        S.emit("vector", lambda e, hl=hl, b=b: e.reduce_max(
            sc_sb[:, 0:1],
            qn2[0:1, hl * NTB + b * 8: hl * NTB + (b + 1) * 8], axis=AX.X))
        S.emit("vector", lambda e, hl=hl, b=b: e.reduce_max(
            sc_sb[:, 1:2],
            kn2[0:1, hl * NTB + b * 8: hl * NTB + (b + 1) * 8], axis=AX.X),
            inc="v")
        S.wait("vector", "v")
        S.emit("vector", lambda e: e.tensor_tensor(
            sc_sb[:, 2:3], sc_sb[:, 0:1], sc_sb[:, 1:2], ALU.mult), inc="v")
        S.wait("scalar", "v")
        S.emit("scalar", lambda e: e.activation(sc_sb[:, 3:4], sc_sb[:, 2:3], AF.Ln))
        S.emit("scalar", lambda e: e.activation(sc_sb[:, 3:4], sc_sb[:, 3:4], AF.Exp, scale=0.5),
               inc="a")
        S.wait("vector", "a")
        S.emit("vector", lambda e: e.tensor_scalar_mul(sc_sb[:, 4:5], sc_sb[:, 3:4],
                                                       -1.02 * ISQ), inc="v")
        v_c2 = S.val("v")
        S.wait("sync", "v", v_c2)
        S.dma("sync", cbounce[:, 0:1], sc_sb[:, 4:5], "cbd")
        S.wait("sync", "cbd")
        S.wait_fa("sync", fa_cb)
        S.dma("sync", cb_sb[:], cbounce[:, 0:1].broadcast_to([128, 1]), "cbd")
        v_cb = S.val("cbd")
        fa_scb = {"cbd": v_cb}

        S.wait("tensor", "kv", kv_loaded)
        S.wait("scalar", "cbd", v_cb)

        job = 0
        a_cb_users_last = None
        for qb in range(8):
            njt = 4 * qb + 4
            qcol = b * SL + qb * TB
            for j in range(njt):
                bank = job % 2
                ptile = pt_[job % 2]
                S.wait_fa("tensor", fa_pp[bank])
                S.emit("tensor", lambda e, j=j, hl=hl, qcol=qcol, bank=bank, lb=lb: e.matmul(
                    pp[bank][:], kb_[lb][:, j * 128:(j + 1) * 128],
                    q_sb[:, hl * T + qcol: hl * T + qcol + TB],
                    start=True, stop=True), inc="p")
                p_s = S.val("p")
                S.wait("scalar", "p", p_s)
                S.wait_fa("scalar", fa_pt[job % 2])
                S.emit("scalar", lambda e, bank=bank, ptile=ptile: e.activation(
                    ptile, pp[bank][:], AF.Exp, bias=cb_sb[:], scale=ISQ), inc="a")
                a_exp[job] = S.val("a")
                a_cb_users_last = a_exp[job]
                fa_pp[bank] = {"a": a_exp[job]}
                diag = j >= 4 * qb
                vm_now = None
                if diag:
                    jr = j - 4 * qb
                    S.wait("vector", "a", a_exp[job])
                    S.emit("vector", lambda e, jr=jr, ptile=ptile: e.tensor_tensor(
                        ptile, ptile, trim_sb[:, jr * TB:(jr + 1) * TB],
                        ALU.mult), inc="vm")
                    vm_now = S.val("vm")
                S.wait("tensor", "a", a_exp[job])
                if diag:
                    S.wait("tensor", "vm", vm_now)
                first = (j == 0)
                last = (j == njt - 1)
                if first:
                    S.wait_fa("tensor", fa_pp[2])
                    S.wait_fa("tensor", fa_sum)
                S.emit("tensor", lambda e, j=j, lb=lb, ptile=ptile, first=first, last=last: e.matmul(
                    pp[2][:], vb_[lb][:, j * 128:(j + 1) * 128], ptile,
                    start=first, stop=last))
                S.emit("tensor", lambda e, ptile=ptile, first=first, last=last: e.matmul(
                    sum_ps[:], ones_sb[:], ptile,
                    start=first, stop=last), inc="p")
                p_cons[job] = S.val("p")
                fa_pt[job % 2] = {"p": p_cons[job]}
                job += 1
            p_qb = S.val("p")
            S.wait("vector", "p", p_qb)
            S.wait_fa("vector", fa_stf)
            S.emit("vector", lambda e: e.tensor_copy(st_f[3][:], sum_ps[:]), inc="v")
            S.wait("vector", "v")
            S.emit("vector", lambda e: e.reciprocal(st_f[3][:], st_f[3][:]))
            S.emit("vector", lambda e: e.tensor_copy(st_b[0][:], st_f[3][:]), inc="v")
            v_rec = S.val("v")
            fa_sum = {"v": v_rec}
            S.wait("tensor", "v", v_rec)
            S.wait_fa("tensor", fa_bc[1])
            S.emit("tensor", lambda e: e.matmul(bc[1][:], onesr_sb[:], st_b[0][:],
                                                start=True, stop=True), inc="p")
            p_recb = S.val("p")
            fa_stf = {"p": p_recb}
            S.wait("vector", "p", p_recb)
            S.wait_fa("vector", fa_recsb)
            S.emit("vector", lambda e: e.tensor_copy(rec_sb[:], bc[1][:]), inc="v")
            v_recb = S.val("v")
            fa_bc[1] = {"v": v_recb}
            S.wait_fa("vector", fa_osb)
            S.emit("vector", lambda e: e.tensor_tensor(o_sb[:], pp[2][:], rec_sb[:],
                                                       ALU.mult), inc="v")
            v_onrm = S.val("v")
            fa_pp[2] = {"v": v_onrm}
            fa_recsb = {"v": v_onrm}
            S.wait("sync", "v", v_onrm)
            S.dma("sync", a2a_in[qb * 256 + hl * 128: qb * 256 + (hl + 1) * 128,
                                 b * TB:(b + 1) * TB],
                  o_sb[:], "os")
            fa_osb = {"os": S.val("os")}
        fa_kvbuf[lb] = {"p": S.val("p")}
        fa_cb = {"a": a_cb_users_last}

    S.wait("sync", "os", S.val("os"))
    S.wait("gpsimd", "os", S.val("os"))



def stage_A(inputs):
    x = np.asarray(inputs["L0_attn_x"], dtype=np.float32)
    x_fm = np.ascontiguousarray(x.transpose(2, 0, 1).reshape(D, T))
    wdq = np.ascontiguousarray(np.asarray(inputs["L0_attn_W_dq"], np.float32))
    wdkv = np.ascontiguousarray(np.asarray(inputs["L0_attn_W_dkv"], np.float32))
    wuq = np.asarray(inputs["L0_attn_W_uq"], np.float32)
    wukv = np.asarray(inputs["L0_attn_W_ukv"], np.float32)
    qg = np.asarray(inputs["L0_attn_q_ln_g"], np.float32)
    qb = np.asarray(inputs["L0_attn_q_ln_b"], np.float32)
    kg = np.asarray(inputs["L0_attn_kv_ln_g"], np.float32)
    kb = np.asarray(inputs["L0_attn_kv_ln_b"], np.float32)

    qg_h = np.ascontiguousarray(qg.reshape(4, 128).T)
    qb_h = np.ascontiguousarray(qb.reshape(4, 128).T)
    kg_h = np.ascontiguousarray(kg.reshape(4, 128).T)
    kb_h = np.ascontiguousarray(kb.reshape(4, 128).T)
    ones = np.ones((128, 1), dtype=BF)
    onesr = np.ones((1, 128), dtype=BF)
    trim = np.zeros((4, 128, TB), dtype=np.float32)
    for jr in range(4):
        for k in range(128):
            trim[jr, k, jr * 128 + k:] = 1.0
    trim = trim.astype(BF)

    in_maps = []
    for i in range(NCORES):
        sl = slice(256 * i, 256 * (i + 1))
        in_maps.append(dict(
            x_fm=x_fm,
            wdq=wdq, wdkv=wdkv,
            wuq_m=np.ascontiguousarray(wuq[:, sl]),
            wk_m=np.ascontiguousarray(wukv[:, sl]),
            wv_m=np.ascontiguousarray(wukv[:, 2048 + 256 * i: 2048 + 256 * (i + 1)]),
            qg_i=qg_h, qb_i=qb_h, kg_i=kg_h, kb_i=kb_h,
            ones_i=ones, onesr_i=onesr, trim_i=trim,
        ))
    return in_maps



GELU_A = 1.5957691216057308
GELU_B = 0.07135481282316932
TT = 2048
NMLP = 16384


def emit_B1(nc, S, G):
    o_my = G["a2a_out"]
    x_my = G["x_my"]
    woT_b = G["woT_b"]
    pk_g = G["pk_g"]; pk_u = G["pk_u"]; pk_d = G["pk_d"]
    sc_g = G["sc_g"]; sc_u = G["sc_u"]; sc_d = G["sc_d"]
    y2_my = G["y2_scr"]
    resid_scr = G["resid_scr"]; xeo_scr = G["xeo_scr"]
    inter_scr = G["inter_scr"]; wd_scr = G["wd_scr"]

    ARX = G["ARX"]; ARY = G["ARY"]; ARZW = G["ARZW"]; WU = G["WU"]
    AR1 = ARX
    AR2 = ARY
    AR3 = ARZW
    wob = [WU[:, 0:2048], WU[:, 2048:4096]]
    wdt = [WU[:, 4096:8192], WU[:, 8192:12288]]
    pkb = WU[:, 12288:12800].bitcast(I16)
    pkb2 = WU[:, 12800:13312].bitcast(I16)
    stl = [WU[:, 13312:14336].bitcast(F32), WU[:, 14336:15360].bitcast(F32)]
    cmi = WU[:, 15360:15872].bitcast(I16)
    sgn = WU[:, 15872:16384].bitcast(I16)
    cmf = WU[:, 16384:16896]
    t1s = WU[:, 16896:17408]
    t2s = WU[:, 17408:17920]
    g2b = WU[:, 17920:18432]
    csb = WU[:, 18432:18944]
    asb = WU[:, 18944:19456]
    ssb = WU[:, 19456:19968]
    hsb = WU[:, 19968:20480]
    gsb = G["sg_sb"]
    isb = G["ic_sb"]
    rt_f = G["rt_f"]; yt_f = G["yt_f"]; rec_dn = G["acc_sb"]
    fl_sb = G["fl_sb"]; ones_sb = G["ones_sb"]; onesr_sb = G["onesr_sb"]
    st_f = G["st_f"]; st_b = G["st_b"]
    pp = G["pp"]; bc = G["bc"]; stp = G["stp"]

    rt32 = ARY.bitcast(F32)
    fa_pp = [dict(), dict()]
    fa_bc = [dict(), dict()]
    fa_stp = [dict(), dict()]
    fa_stf = dict()
    fa_ar1 = dict()
    fa_rt = dict()
    fa_sq = dict()
    fa_wob = [dict(), dict()]
    fa_pkb = [dict(), dict()]
    fa_stl = [dict(), dict()]
    fa_dec = [dict(), dict()]
    fa_chain = dict()
    fa_wdt = [dict(), dict()]
    fa_ytf = dict()
    fa_rtf = dict()
    fa_isb = dict()

    w_s = S.val("s")

    def ln_stats(n_inv):
        S.wait("vector", "p")
        S.wait_fa("vector", fa_stf)
        S.emit("vector", lambda e: e.tensor_scalar_mul(st_f[0][:], stp[0][:], n_inv))
        S.emit("vector", lambda e: e.tensor_scalar_mul(st_f[1][:], stp[1][:], n_inv))
        S.emit("vector", lambda e: e.tensor_tensor(st_f[2][:], st_f[0][:], st_f[0][:], ALU.mult),
               inc="v")
        S.wait("vector", "v")
        S.emit("vector", lambda e: e.tensor_tensor(st_f[1][:], st_f[1][:], st_f[2][:], ALU.subtract))
        S.emit("vector", lambda e: e.tensor_scalar_add(st_f[1][:], st_f[1][:], EPS),
               inc="v")
        fa_stp[0]["v"] = S.val("v")
        fa_stp[1]["v"] = S.val("v")
        S.wait("scalar", "v")
        S.emit("scalar", lambda e: e.activation(st_f[1][:], st_f[1][:], AF.Ln))
        S.emit("scalar", lambda e: e.activation(st_f[1][:], st_f[1][:], AF.Exp, scale=-0.5),
               inc="a")
        S.wait("vector", "a")
        S.emit("vector", lambda e: e.tensor_copy(st_b[0][:], st_f[0][:]))
        S.emit("vector", lambda e: e.tensor_copy(st_b[1][:], st_f[1][:]), inc="v")
        S.wait("tensor", "v")
        S.wait_fa("tensor", fa_bc[0])
        S.wait_fa("tensor", fa_bc[1])
        S.emit("tensor", lambda e: e.matmul(bc[0][:], onesr_sb[:], st_b[0][:],
                                            start=True, stop=True))
        S.emit("tensor", lambda e: e.matmul(bc[1][:], onesr_sb[:], st_b[1][:],
                                            start=True, stop=True), inc="p")
        fa_stf["p"] = S.val("p")

    for oc in range(16):
        S.dma("sync", AR1[:, oc * TT:(oc + 1) * TT],
              o_my[oc * 128:(oc + 1) * 128, :], "s")
    o_loaded = S.val("s")
    S.dma("gpsimd", wob[0].rearrange("p (oc f) -> p oc f", f=128),
          woT_b[0].rearrange("(oc p) f -> p oc f", p=128), "g")

    S.wait("tensor", "s", w_s)

    for tb4 in range(4):
        tcol = tb4 * TB
        for dout in range(16):
            wb = dout % 2
            nxt = dout + 1 if dout < 15 else (0 if tb4 < 3 else None)
            S.wait("tensor", "g", S.val("g"))
            if dout == 0:
                S.wait("tensor", "s", o_loaded)
            bank = dout % 2
            S.wait_fa("tensor", fa_pp[bank])
            for oc in range(16):
                S.emit("tensor", lambda e, oc=oc, dout=dout, wb=wb, bank=bank, tcol=tcol: e.matmul(
                    pp[bank][:],
                    wob[wb][:, oc * 128:(oc + 1) * 128],
                    AR1[:, oc * TT + tcol: oc * TT + tcol + TB],
                    start=(oc == 0), stop=(oc == 15)),
                    inc=("p" if oc == 15 else None))
            p_wo = S.val("p")
            if nxt is not None:
                S.wait_fa("gpsimd", fa_wob[nxt % 2])
                S.dma("gpsimd", wob[nxt % 2].rearrange("p (oc f) -> p oc f", f=128),
                      woT_b[nxt].rearrange("(oc p) f -> p oc f", p=128), "g")
            S.wait_fa("sync", fa_rtf)
            S.dma("sync", rt_f[:], x_my[dout * 128:(dout + 1) * 128, tcol:tcol + TB], "s")
            x_ld = S.val("s")
            S.wait("vector", "p", p_wo)
            S.wait("vector", "s", x_ld)
            S.wait_fa("vector", fa_rt)
            S.emit("vector", lambda e, dout=dout, bank=bank: e.tensor_tensor(
                rt32[:, dout * TB:(dout + 1) * TB], pp[bank][:], rt_f[:], ALU.add),
                inc="v")
            v_r = S.val("v")
            fa_pp[bank] = {"v": v_r}
            fa_rtf = {"v": v_r}
            fa_wob[wb] = {"p": p_wo}
        S.wait("vector", "v")
        S.emit("vector", lambda e: e.tensor_copy(fl_sb[:], AR2[:, 0:8]), inc="v")
        v_rall = S.val("v")
        S.wait("sync", "v", v_rall)
        for dout in range(16):
            S.dma("sync", resid_scr[dout * 128:(dout + 1) * 128, tcol:tcol + TB],
                  rt32[:, dout * TB:(dout + 1) * TB], "rs")
        fa_rt = {"rs": S.val("rs")}
        S.wait("scalar", "v", v_rall)
        S.wait_fa("scalar", fa_sq)
        for c in range(16):
            S.emit("scalar", lambda e, c=c: e.activation(
                AR3[:, c * TB:(c + 1) * TB], rt32[:, c * TB:(c + 1) * TB],
                AF.Square), inc=("a" if c == 15 else None))
        a_sq = S.val("a")
        S.wait("scalar", "v", v_rall)
        for c in range(16):
            S.emit("scalar", lambda e, c=c: e.activation(
                AR3[:, (16 + c) * TB:(17 + c) * TB], rt32[:, c * TB:(c + 1) * TB],
                AF.Identity), inc=("a" if c == 15 else None))
        a_cp = S.val("a")
        S.wait("tensor", "a", a_cp)
        S.wait_fa("tensor", fa_stp[0])
        for c in range(16):
            S.emit("tensor", lambda e, c=c: e.matmul(
                stp[0][:], ones_sb[:], AR3[:, (16 + c) * TB:(17 + c) * TB],
                start=(c == 0), stop=(c == 15)), inc=("p" if c == 15 else None))
        S.wait("tensor", "a", a_sq)
        S.wait_fa("tensor", fa_stp[1])
        for c in range(16):
            S.emit("tensor", lambda e, c=c: e.matmul(
                stp[1][:], ones_sb[:], AR3[:, c * TB:(c + 1) * TB],
                start=(c == 0), stop=(c == 15)), inc=("p" if c == 15 else None))
        ln_stats(1.0 / D)
        S.wait("vector", "p")
        for c in range(16):
            S.emit("vector", lambda e, c=c: e.tensor_tensor(
                AR3[:, c * TB:(c + 1) * TB], rt32[:, c * TB:(c + 1) * TB],
                bc[0][:], ALU.subtract))
            S.emit("vector", lambda e, c=c: e.tensor_tensor(
                AR3[:, c * TB:(c + 1) * TB], AR3[:, c * TB:(c + 1) * TB],
                bc[1][:], ALU.mult), inc=("v" if c == 15 else None))
        S.wait("vector", "v")
        S.emit("vector", lambda e: e.tensor_copy(fl_sb[:], AR3[:, 0:8]), inc="v")
        v_xln = S.val("v")
        fa_bc[0] = {"v": v_xln}
        fa_bc[1] = {"v": v_xln}
        fa_rt["v"] = v_xln
        S.wait("sync", "v", v_xln)
        for c in range(16):
            base = 128 * c
            if c % 2 == 0:
                ev0, od0 = base, base + 128
            else:
                ev0, od0 = base - 64, base + 64
            S.dma("sync", xeo_scr[ev0:ev0 + 64, tcol:tcol + TB],
                  AR3[0:128:2, c * TB:(c + 1) * TB], "rs")
            S.dma("sync", xeo_scr[od0:od0 + 64, tcol:tcol + TB],
                  AR3[1:128:2, c * TB:(c + 1) * TB], "rs")
        fa_sq = {"rs": S.val("rs")}
    ph1_done_rs = S.val("rs")

    def decode_tile(src_i16, stile, out_lo, out_hi):
        S.emit("vector", lambda e: e.tensor_single_scalar(cmi[:], src_i16, 7, ALU.bitwise_and))
        S.emit("vector", lambda e: e.tensor_copy(cmf[:], cmi[:]))
        S.emit("vector", lambda e: e.tensor_scalar(t1s[:], cmf[:], 2.0, -8.0, ALU.mult, ALU.add))
        S.emit("vector", lambda e: e.scalar_tensor_tensor(t2s[:], cmf[:], -2.0, t1s[:], ALU.add, ALU.max))
        S.emit("vector", lambda e: e.scalar_tensor_tensor(t1s[:], cmf[:], 0.5, t2s[:], ALU.mult, ALU.max))
        S.emit("vector", lambda e, o=out_lo: e.tensor_tensor(o, t1s[:], stile, ALU.mult))
        S.emit("vector", lambda e: e.tensor_scalar(sgn[:], src_i16, 8, 12, ALU.bitwise_and, ALU.logical_shift_left))
        S.emit("vector", lambda e, o=out_lo: e.tensor_tensor(o.bitcast(I16), o.bitcast(I16), sgn[:], ALU.bitwise_or))
        S.emit("vector", lambda e: e.tensor_scalar(cmi[:], src_i16, 4, 7, ALU.logical_shift_right, ALU.bitwise_and))
        S.emit("vector", lambda e: e.tensor_copy(cmf[:], cmi[:]))
        S.emit("vector", lambda e: e.tensor_scalar(t1s[:], cmf[:], 2.0, -8.0, ALU.mult, ALU.add))
        S.emit("vector", lambda e: e.scalar_tensor_tensor(t2s[:], cmf[:], -2.0, t1s[:], ALU.add, ALU.max))
        S.emit("vector", lambda e: e.scalar_tensor_tensor(t1s[:], cmf[:], 0.5, t2s[:], ALU.mult, ALU.max))
        S.emit("vector", lambda e, o=out_hi: e.tensor_tensor(o, t1s[:], stile, ALU.mult))
        S.emit("vector", lambda e: e.tensor_scalar(sgn[:], src_i16, 128, 8, ALU.bitwise_and, ALU.logical_shift_left))
        S.emit("vector", lambda e, o=out_hi: e.tensor_tensor(o.bitcast(I16), o.bitcast(I16), sgn[:], ALU.bitwise_or),
               inc="v")

    fa_dec[0] = {"rs": ph1_done_rs}
    lo_t = AR2[:, 0:TB]
    hi_t = AR2[:, TB:2 * TB]
    for jd in range(64):
        for hb in range(4):
            sbuf_i = (jd * 4 + hb) % 2
            S.wait_fa("gpsimd", fa_pkb[sbuf_i])
            S.dma("gpsimd", (pkb if sbuf_i == 0 else pkb2)[:],
                  pk_d[jd * 128:(jd + 1) * 128, hb * TB:(hb + 1) * TB], "g")
            S.wait_fa("gpsimd", fa_stl[sbuf_i])
            for r in range(8):
                S.dma("gpsimd", stl[sbuf_i][16 * r:16 * (r + 1), :],
                      sc_d[8 * jd + r: 8 * jd + r + 1, hb * TB:(hb + 1) * TB]
                      .broadcast_to([16, TB]), "g")
            g_now = S.val("g")
            S.wait("vector", "g", g_now)
            S.wait_fa("vector", fa_dec[0])
            decode_tile((pkb if sbuf_i == 0 else pkb2)[:], stl[sbuf_i][:],
                        lo_t, hi_t)
            v_dec = S.val("v")
            fa_pkb[sbuf_i] = {"v": v_dec}
            fa_stl[sbuf_i] = {"v": v_dec}
            S.wait("vector", "v", v_dec)
            S.emit("vector", lambda e: e.tensor_copy(fl_sb[:], lo_t[:, 0:8]), inc="v")
            v_dec = S.val("v")
            S.wait("sync", "v", v_dec)
            for k in range(4):
                hc = 4 * hb + k
                S.dma("sync", wd_scr[hc, 256 * jd: 256 * jd + 256: 2, :],
                      lo_t[:, k * 128:(k + 1) * 128], "ws")
                S.dma("sync", wd_scr[hc, 256 * jd + 1: 256 * jd + 256: 2, :],
                      hi_t[:, k * 128:(k + 1) * 128], "ws")
            fa_dec[0] = {"ws": S.val("ws")}

    S.wait("sync", "rs", ph1_done_rs)
    S.wait_fa("sync", fa_ar1)
    S.wait("sync", "p")
    for c in range(16):
        S.dma("sync", AR1[:, c * TT:(c + 1) * TT],
              xeo_scr[c * 128:(c + 1) * 128, :], "s")
    xeo_loaded = S.val("s")
    S.wait("tensor", "s", xeo_loaded)

    wdec_g = [AR2[:, 0:8192], AR3[:, 0:8192]]
    wdec_u = [AR2[:, 8192:16384], AR3[:, 8192:16384]]
    fa_w = [{"ws": S.val("ws")}, {"rs": ph1_done_rs}]

    for nb in range(32):
        buf = nb % 2
        for mi, (pk, scm) in enumerate(((pk_g, sc_g), (pk_u, sc_u))):
            wd = (wdec_g if mi == 0 else wdec_u)[buf]
            for jc in range(8):
                sbuf_i = jc % 2
                S.wait_fa("gpsimd", fa_pkb[sbuf_i])
                S.dma("gpsimd", (pkb if sbuf_i == 0 else pkb2)
                      .rearrange("p (s w) -> p s w", w=128),
                      pk[4 * nb:4 * nb + 4, 128 * jc:128 * (jc + 1), :]
                      .rearrange("s p w -> p s w"), "g")
                S.wait_fa("gpsimd", fa_stl[sbuf_i])
                for r in range(8):
                    S.dma("gpsimd", stl[sbuf_i][16 * r:16 * (r + 1), :],
                          scm[8 * jc + r: 8 * jc + r + 1,
                              TB * nb: TB * (nb + 1)].broadcast_to([16, TB]), "g")
                g_now = S.val("g")
                S.wait("vector", "g", g_now)
                if jc == 0 and mi == 0:
                    S.wait_fa("vector", fa_w[buf])
                    S.wait("vector", "p", fa_w[buf].get("p", 0))
                decode_tile((pkb if sbuf_i == 0 else pkb2)[:],
                            stl[sbuf_i][:],
                            wd[:, (2 * jc) * TB:(2 * jc + 1) * TB],
                            wd[:, (2 * jc + 1) * TB:(2 * jc + 2) * TB])
                v_dec = S.val("v")
                fa_pkb[sbuf_i] = {"v": v_dec}
                fa_stl[sbuf_i] = {"v": v_dec}
        v_decoded = S.val("v")
        S.wait("tensor", "v", v_decoded)
        for ns in range(4):
            for tb4 in range(4):
                tcol = tb4 * TB
                nrow = TB * nb + 128 * ns
                S.wait_fa("tensor", fa_pp[0])
                for kc in range(16):
                    S.emit("tensor", lambda e, kc=kc, ns=ns, buf=buf, tcol=tcol: e.matmul(
                        pp[0][:],
                        wdec_g[buf][:, kc * TB + ns * 128: kc * TB + ns * 128 + 128],
                        AR1[:, kc * TT + tcol: kc * TT + tcol + TB],
                        start=(kc == 0), stop=(kc == 15)),
                        inc=("p" if kc == 15 else None))
                p_g = S.val("p")
                S.wait_fa("tensor", fa_pp[1])
                for kc in range(16):
                    S.emit("tensor", lambda e, kc=kc, ns=ns, buf=buf, tcol=tcol: e.matmul(
                        pp[1][:],
                        wdec_u[buf][:, kc * TB + ns * 128: kc * TB + ns * 128 + 128],
                        AR1[:, kc * TT + tcol: kc * TT + tcol + TB],
                        start=(kc == 0), stop=(kc == 15)),
                        inc=("p" if kc == 15 else None))
                p_u = S.val("p")
                S.wait("scalar", "p", p_g)
                S.wait_fa("scalar", fa_chain)
                S.emit("scalar", lambda e: e.activation(gsb[:], pp[0][:], AF.Identity), inc="a")
                a_g = S.val("a")
                fa_pp[0] = {"a": a_g}
                S.emit("scalar", lambda e: e.activation(g2b[:], gsb[:], AF.Square), inc="a")
                a_g2 = S.val("a")
                S.wait("vector", "a", a_g2)
                S.emit("vector", lambda e: e.tensor_scalar(
                    csb[:], g2b[:], GELU_B, GELU_A, ALU.mult, ALU.add))
                S.emit("vector", lambda e: e.tensor_tensor(asb[:], csb[:], gsb[:], ALU.mult),
                       inc="v")
                v_arg = S.val("v")
                S.wait("scalar", "v", v_arg)
                S.emit("scalar", lambda e: e.activation(ssb[:], asb[:], AF.Sigmoid), inc="a")
                a_s = S.val("a")
                S.wait("vector", "a", a_s)
                S.emit("vector", lambda e: e.tensor_tensor(hsb[:], ssb[:], gsb[:], ALU.mult))
                S.wait("vector", "p", p_u)
                S.wait_fa("vector", fa_isb)
                S.emit("vector", lambda e: e.tensor_tensor(isb[:], hsb[:], pp[1][:], ALU.mult),
                       inc="v")
                v_i = S.val("v")
                fa_pp[1] = {"v": v_i}
                fa_chain = {"v": v_i}
                S.wait("vector", "v", v_i)
                S.emit("vector", lambda e: e.tensor_copy(fl_sb[:], isb[:, 0:8]), inc="v")
                v_i2 = S.val("v")
                S.wait("sync", "v", v_i2)
                S.dma("sync", inter_scr[nrow:nrow + 128, tcol:tcol + TB], isb[:], "is")
                fa_chain["is"] = S.val("is")
                fa_isb = {"is": S.val("is")}
        fa_w[buf] = {"p": S.val("p")}

    is_all = S.val("is")
    ws_all = S.val("ws")

    def stripe_tile(nck):
        if nck < 64:
            return AR1[:, nck * TB:(nck + 1) * TB]
        if nck < 96:
            return AR2[:, (nck - 64) * TB:(nck - 63) * TB]
        return AR3[:, (nck - 96) * TB:(nck - 95) * TB]

    for eng in ("sync", "gpsimd", "tensor", "vector", "scalar"):
        S.wait(eng, "is", is_all)
        S.wait(eng, "ws", ws_all)
        S.wait(eng, "p")
        S.wait(eng, "v")
        S.wait(eng, "a")
        S.wait(eng, "rs")
    fa_stripe = dict()
    for tb4 in range(4):
        tcol = tb4 * TB
        S.wait_fa("sync", fa_stripe)
        for nck in range(128):
            S.dma("sync", stripe_tile(nck),
                  inter_scr[nck * 128:(nck + 1) * 128, tcol:tcol + TB], "s")
        stripe_loaded = S.val("s")
        S.wait("tensor", "s", stripe_loaded)
        for hc in range(16):
            bank = hc % 2
            S.wait_fa("tensor", fa_pp[bank])
            for g32 in range(4):
                wbuf = g32 % 2
                S.wait_fa("sync", fa_wdt[wbuf])
                S.dma("sync", wdt[wbuf].rearrange("p (a f) -> p a f", f=128),
                      wd_scr[hc, g32 * 4096:(g32 + 1) * 4096, :]
                      .rearrange("(a p) f -> p a f", p=128), "wl")
                wl_now = S.val("wl")
                S.wait("tensor", "wl", wl_now)
                for a32 in range(32):
                    nck = 32 * g32 + a32
                    S.emit("tensor", lambda e, a32=a32, nck=nck, wbuf=wbuf, bank=bank: e.matmul(
                        pp[bank][:], wdt[wbuf][:, a32 * 128:(a32 + 1) * 128],
                        stripe_tile(nck),
                        start=(nck == 0), stop=(nck == 127)),
                        inc=("p" if a32 == 31 else None))
                fa_wdt[wbuf] = {"p": S.val("p")}
            p_dn = S.val("p")
            S.wait_fa("sync", fa_rtf)
            S.dma("sync", rt_f[:], resid_scr[hc * 128:(hc + 1) * 128, tcol:tcol + TB], "s")
            r_ld = S.val("s")
            S.wait("vector", "p", p_dn)
            S.wait("vector", "s", r_ld)
            S.wait_fa("vector", fa_ytf)
            S.emit("vector", lambda e, bank=bank: e.tensor_tensor(
                yt_f[:], pp[bank][:], rt_f[:], ALU.add), inc="v")
            v_y = S.val("v")
            fa_pp[bank] = {"v": v_y}
            fa_rtf = {"v": v_y}
            S.wait("sync", "v", v_y)
            S.dma("sync", y2_my[hc * 128:(hc + 1) * 128, tcol:tcol + TB],
                  yt_f[:], "ys")
            fa_ytf = {"ys": S.val("ys")}
        fa_stripe = {"p": S.val("p")}

    S.wait("sync", "ys", S.val("ys"))
    S.wait("gpsimd", "ys", S.val("ys"))


def token_cols(i):
    return np.concatenate([np.arange(b * SL + 512 * i, b * SL + 512 * (i + 1))
                           for b in range(B)])


def stage_B1(inputs, x_fm):
    wo = np.asarray(inputs["L0_attn_W_o"], np.float32)
    woT_b = np.ascontiguousarray(wo.T.reshape(D, 16, 128).transpose(1, 0, 2))

    def unpack16(a32):
        return np.ascontiguousarray(np.asarray(a32, np.int32).view(np.int16)[:, ::2])

    g16 = unpack16(inputs["L0_ffn_gate_W_packed"])
    u16 = unpack16(inputs["L0_ffn_up_W_packed"])
    d16 = unpack16(inputs["L0_ffn_down_W_packed"])
    pk_g = np.ascontiguousarray(g16.T.reshape(1024, 128, 128).transpose(1, 0, 2))
    pk_u = np.ascontiguousarray(u16.T.reshape(1024, 128, 128).transpose(1, 0, 2))
    pk_d = np.ascontiguousarray(d16.T)
    sc_g = np.ascontiguousarray(np.asarray(inputs["L0_ffn_gate_scales"], np.float32).T)
    sc_u = np.ascontiguousarray(np.asarray(inputs["L0_ffn_up_scales"], np.float32).T)
    sc_d = np.ascontiguousarray(np.asarray(inputs["L0_ffn_down_scales"], np.float32).T)

    in_maps = []
    for i in range(NCORES):
        cols = token_cols(i)
        in_maps.append(dict(
            x_my=np.ascontiguousarray(x_fm[:, cols]),
            woT_b=woT_b, pk_g=pk_g, pk_u=pk_u, pk_d=pk_d,
            sc_g=sc_g, sc_u=sc_u, sc_d=sc_d,
        ))
    return in_maps



def emit_CD(nc, S, G):
    y2_my = G["y2_scr"]
    wqkvT_b = G["wqkvT_b"]; woT_b = G["wo1T_b"]; rw_i = G["rw_i"]
    wg_b = G["wg_b"]; wu_b = G["wu_b"]; wd_b = G["wd_b"]
    swg_b = G["swg_b"]; swu_b = G["swu_b"]; swd_b = G["swd_b"]
    e2_i = G["e2_i"]; idb_i = G["idb_i"]; idf_i = G["idf_i"]; alib_i = G["alib_i"]
    out_my = G["out_my"]
    y3_scr = G["y3_scr"]; attn_scr = G["attn_scr"]; p_scr = G["p_scr"]
    cb_scr = G["cb_scr"]; ic_scr = G["ic_scr"]; ish_scr = G["ish_scr"]

    ARX = G["ARX"]; ARY = G["ARY"]
    ARZ = G["ARZW"][:, 0:8192]
    ARW = G["ARZW"][:, 8192:16384]
    WU = G["WU"]
    wqb = [WU[:, 0:2048], WU[:, 2048:4096]]
    wgu = [WU[:, 4096:8192], WU[:, 8192:12288]]
    wdn = [WU[:, 0:10240], WU[:, 10240:20480]]
    S_sb = G["S_sb"]; sT_sb = G["sT_sb"]; P_sb = G["P_sb"]
    tmp_sb = G["tmp_sb"]; pb_sb = G["pb_sb"]
    acc_sb = G["acc_sb"]; tmp2_sb = G["tmp2_sb"]
    rt_f = G["rt_f"]; yt_f = G["yt_f"]
    fl_sb = G["fl_sb"]; ones_sb = G["ones_sb"]; onesr_sb = G["onesr_sb"]
    e2_sb = G["e2_sb"]; idb_sb = G["idb_sb"]; idf_sb = G["idf_sb"]
    alib_sb = G["alib_sb"]; rw_sb = G["rw_sb"]
    nmx = G["nmx"]; rec1 = G["rec1"]
    ro_sb = G["ro_sb"]; e_sb = G["e_sb"]
    m1_sb = G["m1_sb"]; m2_sb = G["m2_sb"]; den_sb = G["den_sb"]
    eq_sb = G["eq_sb"]; me_sb = G["me_sb"]; en_sb = G["en_sb"]
    comb_sb = G["comb_sb"]; combT_sb = G["combT_sb"]
    ce_sb = G["ce_sb"]; sg_sb = G["sg_sb"]; ic_sb = G["ic_sb"]
    st_f = G["st_f"]; st_b = G["st_b"]
    sq_sb = G["sq_sb"]; xb_sb = G["xb_sb"]
    pp = G["pp"]; bc = G["bc"]; stp = G["stp"]
    ptb = G["ptb"]; ptf = G["ptf"]
    fa_pp = [dict(), dict()]
    fa_bc = [dict(), dict()]
    fa_stp = [dict(), dict()]
    fa_stf = dict()
    fa_rtf = dict()
    fa_ytf = dict()
    fa_wqb = [dict(), dict()]
    fa_wgu = [dict(), dict()]
    fa_wdn = [dict(), dict()]
    fa_sqxb = dict()
    fa_tmp = [dict(), dict()]
    fa_Ssb = dict()
    fa_sT = dict()
    fa_Psb = dict()
    fa_pb = dict()
    fa_acc = dict()
    fa_icsb = dict()
    fa_sg = dict()
    fa_ptb = dict()
    fa_ptf = dict()
    fa_arx = dict()
    fa_ary = dict()

    S.dma("sync", e2_sb[:], e2_i, "s")
    S.dma("sync", idb_sb[:], idb_i, "s")
    S.dma("sync", idf_sb[:], idf_i, "s")
    S.dma("sync", alib_sb[:], alib_i, "s")
    S.dma("gpsimd", rw_sb.rearrange("p (a f) -> p a f", f=8),
          rw_i.rearrange("(a p) f -> p a f", p=128), "g")
    w_s = S.val("s")
    S.wait("tensor", "s", w_s)
    S.wait("scalar", "s", w_s)
    S.wait("vector", "s", w_s)

    def ln_stats(n_inv):
        S.wait("vector", "p")
        S.wait_fa("vector", fa_stf)
        S.emit("vector", lambda e: e.tensor_scalar_mul(st_f[0][:], stp[0][:], n_inv))
        S.emit("vector", lambda e: e.tensor_scalar_mul(st_f[1][:], stp[1][:], n_inv))
        S.emit("vector", lambda e: e.tensor_tensor(st_f[2][:], st_f[0][:], st_f[0][:], ALU.mult),
               inc="v")
        S.wait("vector", "v")
        S.emit("vector", lambda e: e.tensor_tensor(st_f[1][:], st_f[1][:], st_f[2][:], ALU.subtract))
        S.emit("vector", lambda e: e.tensor_scalar_add(st_f[1][:], st_f[1][:], EPS),
               inc="v")
        fa_stp[0]["v"] = S.val("v")
        fa_stp[1]["v"] = S.val("v")
        S.wait("scalar", "v")
        S.emit("scalar", lambda e: e.activation(st_f[1][:], st_f[1][:], AF.Ln))
        S.emit("scalar", lambda e: e.activation(st_f[1][:], st_f[1][:], AF.Exp, scale=-0.5),
               inc="a")
        S.wait("vector", "a")
        S.emit("vector", lambda e: e.tensor_copy(st_b[0][:], st_f[0][:]))
        S.emit("vector", lambda e: e.tensor_copy(st_b[1][:], st_f[1][:]), inc="v")
        S.wait("tensor", "v")
        S.wait_fa("tensor", fa_bc[0])
        S.wait_fa("tensor", fa_bc[1])
        S.emit("tensor", lambda e: e.matmul(bc[0][:], onesr_sb[:], st_b[0][:],
                                            start=True, stop=True))
        S.emit("tensor", lambda e: e.matmul(bc[1][:], onesr_sb[:], st_b[1][:],
                                            start=True, stop=True), inc="p")
        fa_stf["p"] = S.val("p")

    def ln_to_arx(src_dram, fa_dst_extra=None):
        nonlocal fa_sqxb
        for tb4 in range(4):
            tcol = tb4 * TB
            for c in range(16):
                S.wait_fa("gpsimd", fa_sqxb)
                S.dma("gpsimd", xb_sb[:], src_dram[c * 128:(c + 1) * 128, tcol:tcol + TB], "g")
                g_x = S.val("g")
                S.wait("scalar", "g", g_x)
                S.wait("scalar", "v")
                S.emit("scalar", lambda e: e.activation(sq_sb[:], xb_sb[:], AF.Square),
                       inc="a")
                a_sq = S.val("a")
                S.wait("tensor", "g", g_x)
                if c == 0:
                    S.wait_fa("tensor", fa_stp[0])
                    S.wait_fa("tensor", fa_stp[1])
                S.emit("tensor", lambda e, c=c: e.matmul(
                    stp[0][:], ones_sb[:], xb_sb[:],
                    start=(c == 0), stop=(c == 15)),
                    inc=("p" if c == 15 else None))
                S.wait("tensor", "a", a_sq)
                S.emit("tensor", lambda e, c=c: e.matmul(
                    stp[1][:], ones_sb[:], sq_sb[:],
                    start=(c == 0), stop=(c == 15)),
                    inc=("p" if c == 15 else None))
                fa_sqxb = {"p": S.val("p")}
            ln_stats(1.0 / D)
            for c in range(16):
                S.wait_fa("gpsimd", fa_sqxb)
                S.dma("gpsimd", xb_sb[:], src_dram[c * 128:(c + 1) * 128, tcol:tcol + TB], "g")
                g_x = S.val("g")
                S.wait("vector", "g", g_x)
                S.wait("vector", "p")
                S.emit("vector", lambda e: e.tensor_tensor(sq_sb[:], xb_sb[:], bc[0][:],
                                                           ALU.subtract), inc="v")
                S.wait("vector", "v")
                S.emit("vector", lambda e, c=c, tcol=tcol: e.tensor_tensor(
                    ARX[:, c * 2048 + tcol: c * 2048 + tcol + TB],
                    sq_sb[:], bc[1][:], ALU.mult), inc="v")
                fa_sqxb = {"v": S.val("v")}
            v_ln = S.val("v")
            fa_bc[0] = {"v": v_ln}
            fa_bc[1] = {"v": v_ln}
        return S.val("v")


    v_xln = ln_to_arx(y2_my)
    S.wait("tensor", "v", v_xln)

    pairs10 = [(a, b) for a in range(4) for b in range(a + 1)]
    fa_ary = dict()
    for stk in range(2):
        ocs = list(range(stk * 8, stk * 8 + 8)) + \
              [16 + 2 * g for g in (2 * stk, 2 * stk + 1)] + \
              [17 + 2 * g for g in (2 * stk, 2 * stk + 1)]
        S.wait("gpsimd", "p", fa_ary.get("p", 0))
        S.dma("gpsimd", wqb[0].rearrange("p (a f) -> p a f", f=128),
              wqkvT_b[ocs[0]].rearrange("(a p) f -> p a f", p=128), "g")
        for ii, oc in enumerate(ocs):
            wb = ii % 2
            S.wait("tensor", "g", S.val("g"))
            if ii + 1 < len(ocs):
                S.wait_fa("gpsimd", fa_wqb[(ii + 1) % 2])
                S.dma("gpsimd", wqb[(ii + 1) % 2].rearrange("p (a f) -> p a f", f=128),
                      wqkvT_b[ocs[ii + 1]].rearrange("(a p) f -> p a f", p=128), "g")
            for tb4 in range(4):
                tcol = tb4 * TB
                bank = tb4 % 2
                S.wait_fa("tensor", fa_pp[bank])
                if ii == 0 and tb4 == 0:
                    S.wait_fa("tensor", fa_ary)
                for dc in range(16):
                    S.emit("tensor", lambda e, dc=dc, wb=wb, bank=bank, tcol=tcol: e.matmul(
                        pp[bank][:], wqb[wb][:, dc * 128:(dc + 1) * 128],
                        ARX[:, dc * 2048 + tcol: dc * 2048 + tcol + TB],
                        start=(dc == 0), stop=(dc == 15)),
                        inc=("p" if dc == 15 else None))
                p_q = S.val("p")
                if oc < 16:
                    dst = ARY[:, (oc - stk * 8) * 2048 + tcol:
                              (oc - stk * 8) * 2048 + tcol + TB]
                elif (oc - 16) % 2 == 0:
                    gl = (oc - 16) // 2 - 2 * stk
                    dst = ARZ[:, gl * 2048 + tcol: gl * 2048 + tcol + TB]
                else:
                    gl = (oc - 16) // 2 - 2 * stk
                    dst = ARW[:, gl * 2048 + tcol: gl * 2048 + tcol + TB]
                S.wait("scalar", "p", p_q)
                S.wait("scalar", "v", fa_ary.get("v", 0))
                S.emit("scalar", lambda e, dst=dst, bank=bank: e.activation(
                    dst, pp[bank][:], AF.Identity), inc="a")
                fa_pp[bank] = {"a": S.val("a")}
            fa_wqb[wb] = {"p": S.val("p")}
        a_qkv = S.val("a")

        S.wait("tensor", "a", a_qkv)
        S.wait_fa("tensor", fa_pp[0])
        r = 0
        for hh in range(8):
            h = stk * 8 + hh
            gl = hh // 4
            for (aa, bb) in pairs10:
                tb_ = r % 2
                S.wait("vector", "a", a_qkv)
                S.wait_fa("vector", fa_tmp[tb_])
                S.emit("vector", lambda e, hh=hh, gl=gl, aa=aa, bb=bb, tb_=tb_: e.tensor_tensor(
                    tmp_sb[tb_][:],
                    ARY[:, hh * 2048 + aa * TB: hh * 2048 + (aa + 1) * TB],
                    ARZ[:, gl * 2048 + bb * TB: gl * 2048 + (bb + 1) * TB],
                    ALU.mult), inc="v")
                v_t = S.val("v")
                S.wait("tensor", "v", v_t)
                S.emit("tensor", lambda e, r=r, tb_=tb_: e.matmul(
                    pp[0][:], e2_sb[:, 128 - r:256 - r], tmp_sb[tb_][:],
                    start=(r == 0), stop=(r == 79)),
                    inc="p")
                fa_tmp[tb_] = {"p": S.val("p")}
                r += 1
        p_sc = S.val("p")
        S.wait("scalar", "p", p_sc)
        S.wait_fa("scalar", fa_Ssb)
        S.emit("scalar", lambda e, stk=stk: e.activation(
            S_sb[:], pp[0][:], AF.Identity,
            bias=alib_sb[:, stk:stk + 1], scale=ISQ), inc="a")
        a_S = S.val("a")
        fa_pp[0] = {"a": a_S}
        offs = [0, 1, 3, 6]
        for tsub in range(4):
            S.wait("tensor", "a", a_S)
            S.wait_fa("tensor", fa_ptb)
            S.emit("tensor", lambda e, tsub=tsub: e.transpose(
                ptb[:], S_sb[:, tsub * 128:(tsub + 1) * 128], idb_sb[:]), inc="p")
            p_t = S.val("p")
            S.wait("scalar", "p", p_t)
            S.wait_fa("scalar", fa_sT)
            S.emit("scalar", lambda e: e.activation(sT_sb[:, 0:128], ptb[:], AF.Identity),
                   inc="a")
            a_sT = S.val("a")
            fa_ptb = {"a": a_sT}
            S.wait("vector", "a", a_sT)
            for grp in range(4):
                base = grp * 8
                for k in range(8):
                    idx = base + k
                    hh2, aa2 = idx // 4, idx % 4
                    c2 = hh2 * 10 + offs[aa2]
                    w2 = aa2 + 1
                    S.emit("vector", lambda e, c2=c2, w2=w2, k=k: e.reduce_max(
                        nmx[:, k:k + 1], sT_sb[:, c2:c2 + w2], axis=AX.X, negate=True),
                        inc=("v" if k == 7 else None))
                S.wait("scalar", "v")
                for k in range(8):
                    idx = base + k
                    hh2, aa2 = idx // 4, idx % 4
                    c2 = hh2 * 10 + offs[aa2]
                    w2 = aa2 + 1
                    S.emit("scalar", lambda e, c2=c2, w2=w2, k=k: e.activation(
                        sT_sb[:, c2:c2 + w2], sT_sb[:, c2:c2 + w2], AF.Exp,
                        bias=nmx[:, k:k + 1]),
                        inc=("a" if k == 7 else None))
                S.wait("vector", "a")
                for k in range(8):
                    idx = base + k
                    hh2, aa2 = idx // 4, idx % 4
                    c2 = hh2 * 10 + offs[aa2]
                    w2 = aa2 + 1
                    S.emit("vector", lambda e, c2=c2, w2=w2, k=k: e.reduce_sum(
                        rec1[:, k:k + 1], sT_sb[:, c2:c2 + w2], axis=AX.X),
                        inc=("v" if k == 7 else None))
                S.wait("vector", "v")
                S.emit("vector", lambda e: e.reciprocal(rec1[:], rec1[:]), inc="v")
                S.wait("vector", "v")
                for k in range(8):
                    idx = base + k
                    hh2, aa2 = idx // 4, idx % 4
                    c2 = hh2 * 10 + offs[aa2]
                    w2 = aa2 + 1
                    S.emit("vector", lambda e, c2=c2, w2=w2, k=k: e.tensor_single_scalar(
                        sT_sb[:, c2:c2 + w2], sT_sb[:, c2:c2 + w2],
                        rec1[:, k:k + 1], ALU.mult),
                        inc=("v" if k == 7 else None))
            v_sm = S.val("v")
            S.wait("vector", "v", v_sm)
            S.emit("vector", lambda e: e.tensor_copy(fl_sb[:], sT_sb[:, 0:8]), inc="v")
            v_fl = S.val("v")
            S.wait("tensor", "v", v_fl)
            S.wait_fa("tensor", fa_ptb)
            S.emit("tensor", lambda e: e.transpose(ptb[:], sT_sb[:, 0:128], idb_sb[:]),
                   inc="p")
            p_t2 = S.val("p")
            S.wait("scalar", "p", p_t2)
            S.wait_fa("scalar", fa_Psb)
            S.emit("scalar", lambda e, tsub=tsub: e.activation(
                P_sb[:, tsub * 128:(tsub + 1) * 128], ptb[:], AF.Identity), inc="a")
            a_P = S.val("a")
            fa_ptb = {"a": a_P}
            fa_sT = {"a": a_P}
        a_Pall = S.val("a")
        S.wait("scalar", "a", a_Pall)
        S.emit("scalar", lambda e: e.activation(fl_sb[:], P_sb[:, 0:8], AF.Identity),
               inc="a")
        a_flP = S.val("a")
        S.wait("sync", "a", a_flP)
        S.dma("sync", p_scr[stk], P_sb[:], "ps")
        ps_now = S.val("ps")
        fa_Psb = {"ps": ps_now}
        fa_Ssb = {"p": S.val("p"), "a": a_Pall}
        S.wait("sync", "ps", ps_now)
        at_before = S.val("at")
        for hh in range(8):
            h = stk * 8 + hh
            gl = hh // 4
            for aa in range(4):
                for bb in range(aa + 1):
                    rr = hh * 10 + offs[aa] + bb
                    S.wait_fa("sync", fa_pb)
                    S.dma("sync", pb_sb[:],
                          p_scr[stk, rr:rr + 1, :].broadcast_to([128, TB]), "pb")
                    pb_now = S.val("pb")
                    S.wait("vector", "pb", pb_now)
                    if bb == 0:
                        S.wait_fa("vector", fa_acc)
                    S.emit("vector", lambda e, gl=gl, bb=bb: e.tensor_tensor(
                        tmp2_sb[:], pb_sb[:],
                        ARW[:, gl * 2048 + bb * TB: gl * 2048 + (bb + 1) * TB],
                        ALU.mult), inc="v")
                    S.wait("vector", "v")
                    if bb == 0:
                        S.emit("vector", lambda e: e.tensor_copy(acc_sb[:], tmp2_sb[:]),
                               inc="v")
                    else:
                        S.emit("vector", lambda e: e.tensor_tensor(
                            acc_sb[:], acc_sb[:], tmp2_sb[:], ALU.add), inc="v")
                    fa_pb = {"v": S.val("v")}
                S.wait("vector", "v")
                S.emit("vector", lambda e: e.tensor_copy(ic_sb[:], acc_sb[:]), inc="v")
                S.wait("vector", "v")
                S.emit("vector", lambda e: e.tensor_copy(fl_sb[:], ic_sb[:, 0:8]),
                       inc="v")
                v_at = S.val("v")
                fa_acc = {"v": v_at}
                S.wait("sync", "v", v_at)
                S.wait_fa("sync", fa_icsb)
                S.dma("sync", attn_scr[h * 128:(h + 1) * 128, aa * TB:(aa + 1) * TB],
                      ic_sb[:], "at")
                fa_icsb = {"at": S.val("at")}
        fa_ary = {"v": S.val("v"), "p": S.val("p")}
    at_all = S.val("at")

    S.wait("sync", "at", at_all)
    S.wait("sync", "p")
    S.wait("sync", "v")
    S.wait("sync", "a")
    for c in range(16):
        S.dma("sync", ARX[:, c * 2048:(c + 1) * 2048],
              attn_scr[c * 128:(c + 1) * 128, :], "s")
    arx_ld = S.val("s")
    S.wait("tensor", "s", arx_ld)
    S.dma("gpsimd", wqb[0].rearrange("p (a f) -> p a f", f=128),
          woT_b[0].rearrange("(a p) f -> p a f", p=128), "g")
    for oc in range(16):
        wb = oc % 2
        S.wait("tensor", "g", S.val("g"))
        if oc < 15:
            S.wait_fa("gpsimd", fa_wqb[(oc + 1) % 2])
            S.dma("gpsimd", wqb[(oc + 1) % 2].rearrange("p (a f) -> p a f", f=128),
                  woT_b[oc + 1].rearrange("(a p) f -> p a f", p=128), "g")
        for tb4 in range(4):
            tcol = tb4 * TB
            bank = tb4 % 2
            S.wait_fa("tensor", fa_pp[bank])
            for dc in range(16):
                S.emit("tensor", lambda e, dc=dc, wb=wb, bank=bank, tcol=tcol: e.matmul(
                    pp[bank][:], wqb[wb][:, dc * 128:(dc + 1) * 128],
                    ARX[:, dc * 2048 + tcol: dc * 2048 + tcol + TB],
                    start=(dc == 0), stop=(dc == 15)),
                    inc=("p" if dc == 15 else None))
            p_wo = S.val("p")
            S.wait_fa("sync", fa_rtf)
            S.dma("sync", rt_f[:], y2_my[oc * 128:(oc + 1) * 128, tcol:tcol + TB], "s")
            r_ld = S.val("s")
            S.wait("vector", "p", p_wo)
            S.wait("vector", "s", r_ld)
            S.wait_fa("vector", fa_ytf)
            S.emit("vector", lambda e, bank=bank: e.tensor_tensor(
                yt_f[:], pp[bank][:], rt_f[:], ALU.add), inc="v")
            S.wait("vector", "v")
            S.emit("vector", lambda e: e.tensor_copy(fl_sb[:], yt_f.bitcast(BF16)[:, 0:8]),
                   inc="v")
            v_y = S.val("v")
            fa_pp[bank] = {"v": v_y}
            fa_rtf = {"v": v_y}
            S.wait("sync", "v", v_y)
            S.dma("sync", y3_scr[oc * 128:(oc + 1) * 128, tcol:tcol + TB], yt_f[:], "y3")
            fa_ytf = {"y3": S.val("y3")}
        fa_wqb[wb] = {"p": S.val("p")}
    y3_all = S.val("y3")

    for eng in ("sync", "gpsimd", "tensor", "vector", "scalar"):
        S.wait(eng, "y3", y3_all)
        S.wait(eng, "p")
        S.wait(eng, "v")
        S.wait(eng, "a")
    v_xln2 = ln_to_arx(y3_scr)
    S.wait("tensor", "v", v_xln2)

    for tt_ in range(16):
        c_t = (tt_ % 4)
        S.wait_fa("tensor", fa_pp[0] if tt_ % 2 == 0 else fa_pp[1])
        bank = tt_ % 2
        for dc in range(16):
            S.emit("tensor", lambda e, dc=dc, tt_=tt_, bank=bank: e.matmul(
                pp[bank][:, 0:8],
                ARX[:, dc * 2048 + tt_ * 128: dc * 2048 + (tt_ + 1) * 128],
                rw_sb[:, dc * 8:(dc + 1) * 8],
                start=(dc == 0), stop=(dc == 15)),
                inc=("p" if dc == 15 else None))
        p_r = S.val("p")
        S.wait("scalar", "p", p_r)
        S.emit("scalar", lambda e, tt_=tt_, bank=bank: e.activation(
            ro_sb[:, tt_ * 8:(tt_ + 1) * 8], pp[bank][:, 0:8], AF.Identity), inc="a")
        if tt_ % 2 == 0:
            fa_pp[0] = {"a": S.val("a")}
        else:
            fa_pp[1] = {"a": S.val("a")}
    a_ro = S.val("a")
    S.wait("scalar", "a", a_ro)
    S.emit("scalar", lambda e: e.activation(e_sb[:], ro_sb[:], AF.Exp), inc="a")
    a_e = S.val("a")
    S.wait("vector", "a", a_e)
    for t_ in range(16):
        S.emit("vector", lambda e, t_=t_: e.reduce_max(
            m1_sb[:, t_:t_ + 1], e_sb[:, t_ * 8:(t_ + 1) * 8], axis=AX.X),
            inc=("v" if t_ == 15 else None))
    S.wait("vector", "v")
    for t_ in range(16):
        S.emit("vector", lambda e, t_=t_: e.tensor_single_scalar(
            eq_sb[:, t_ * 8:(t_ + 1) * 8], e_sb[:, t_ * 8:(t_ + 1) * 8],
            m1_sb[:, t_:t_ + 1], ALU.is_ge), inc=("v" if t_ == 15 else None))
    S.wait("vector", "v")
    S.emit("vector", lambda e: e.scalar_tensor_tensor(
        me_sb[:], eq_sb[:], -1e30, e_sb[:], ALU.mult, ALU.add), inc="v")
    S.wait("vector", "v")
    for t_ in range(16):
        S.emit("vector", lambda e, t_=t_: e.reduce_max(
            m2_sb[:, t_:t_ + 1], me_sb[:, t_ * 8:(t_ + 1) * 8], axis=AX.X),
            inc=("v" if t_ == 15 else None))
    S.wait("vector", "v")
    for t_ in range(16):
        S.emit("vector", lambda e, t_=t_: e.tensor_single_scalar(
            en_sb[:, t_ * 8:(t_ + 1) * 8], e_sb[:, t_ * 8:(t_ + 1) * 8],
            m2_sb[:, t_:t_ + 1], ALU.is_ge), inc=("v" if t_ == 15 else None))
    S.emit("vector", lambda e: e.tensor_tensor(den_sb[:], m1_sb[:], m2_sb[:], ALU.add),
           inc="v")
    S.wait("vector", "v")
    S.emit("vector", lambda e: e.reciprocal(den_sb[:], den_sb[:]))
    S.emit("vector", lambda e: e.tensor_tensor(en_sb[:], en_sb[:], e_sb[:], ALU.mult),
           inc="v")
    S.wait("vector", "v")
    for t_ in range(16):
        S.emit("vector", lambda e, t_=t_: e.tensor_single_scalar(
            comb_sb[:, t_ * 8:(t_ + 1) * 8], en_sb[:, t_ * 8:(t_ + 1) * 8],
            den_sb[:, t_:t_ + 1], ALU.mult), inc=("v" if t_ == 15 else None))
    S.wait("vector", "v")
    S.emit("vector", lambda e: e.tensor_copy(fl_sb[:], comb_sb.bitcast(BF16)[:, 0:8]),
           inc="v")
    v_comb = S.val("v")
    S.wait("tensor", "v", v_comb)
    S.wait_fa("tensor", fa_ptf)
    S.emit("tensor", lambda e: e.transpose(ptf[:], comb_sb[:], idf_sb[:]), inc="p")
    p_ct = S.val("p")
    S.wait("vector", "p", p_ct)
    S.emit("vector", lambda e: e.tensor_copy(combT_sb[:], ptf[:]), inc="v")
    S.wait("vector", "v")
    S.emit("vector", lambda e: e.tensor_copy(fl_sb[:], combT_sb.bitcast(BF16)[:, 0:8]),
           inc="v")
    v_cT = S.val("v")
    fa_ptf = {"v": v_cT}
    S.wait("sync", "v", v_cT)
    S.dma("sync", cb_scr[:], combT_sb[:], "cb")
    S.wait("sync", "cb")

    S.dma("gpsimd", wgu[0][:, 0:2048].rearrange("p (a f) -> p a f", f=128),
          wg_b[0, 0].rearrange("(a p) f -> p a f", p=128), "g")
    S.dma("gpsimd", wgu[0][:, 2048:4096].rearrange("p (a f) -> p a f", f=128),
          wu_b[0, 0].rearrange("(a p) f -> p a f", p=128), "g")
    fa_ce = dict()
    cb_done = S.val("cb")
    for ee in range(8):
        S.wait("sync", "cb", cb_done)
        S.wait_fa("sync", fa_ce)
        S.dma("sync", ce_sb.rearrange("p (a t) -> p a t", t=128),
              cb_scr.rearrange("(a e) t -> e a t", e=8)[ee:ee + 1]
              .broadcast_to([128, 16, 128]), "ce")
        ce_now = S.val("ce")
        for hc in range(8):
            wb = (ee * 8 + hc) % 2
            nxt = ee * 8 + hc + 1
            S.wait("tensor", "g", S.val("g"))
            if nxt < 64:
                ne, nh = nxt // 8, nxt % 8
                S.wait_fa("gpsimd", fa_wgu[nxt % 2])
                S.dma("gpsimd", wgu[nxt % 2][:, 0:2048].rearrange("p (a f) -> p a f", f=128),
                      wg_b[ne, nh].rearrange("(a p) f -> p a f", p=128), "g")
                S.dma("gpsimd", wgu[nxt % 2][:, 2048:4096].rearrange("p (a f) -> p a f", f=128),
                      wu_b[ne, nh].rearrange("(a p) f -> p a f", p=128), "g")
            for tb4 in range(4):
                tcol = tb4 * TB
                S.wait_fa("tensor", fa_pp[0])
                for dc in range(16):
                    S.emit("tensor", lambda e, dc=dc, wb=wb, tcol=tcol: e.matmul(
                        pp[0][:], wgu[wb][:, dc * 128:(dc + 1) * 128],
                        ARX[:, dc * 2048 + tcol: dc * 2048 + tcol + TB],
                        start=(dc == 0), stop=(dc == 15)),
                        inc=("p" if dc == 15 else None))
                p_g = S.val("p")
                S.wait_fa("tensor", fa_pp[1])
                for dc in range(16):
                    S.emit("tensor", lambda e, dc=dc, wb=wb, tcol=tcol: e.matmul(
                        pp[1][:], wgu[wb][:, 2048 + dc * 128: 2048 + (dc + 1) * 128],
                        ARX[:, dc * 2048 + tcol: dc * 2048 + tcol + TB],
                        start=(dc == 0), stop=(dc == 15)),
                        inc=("p" if dc == 15 else None))
                p_u = S.val("p")
                S.wait("scalar", "p", p_g)
                S.wait_fa("scalar", fa_sg)
                S.emit("scalar", lambda e: e.activation(sg_sb[:], pp[0][:], AF.Silu),
                       inc="a")
                a_sg = S.val("a")
                fa_pp[0] = {"a": a_sg}
                S.wait("vector", "a", a_sg)
                S.wait("vector", "p", p_u)
                S.wait("vector", "ce", ce_now)
                S.wait_fa("vector", fa_icsb)
                S.emit("vector", lambda e: e.tensor_tensor(tmp2_sb[:], sg_sb[:], pp[1][:],
                                                           ALU.mult), inc="v")
                S.wait("vector", "v")
                S.emit("vector", lambda e, tcol=tcol: e.tensor_tensor(
                    ic_sb[:], tmp2_sb[:], ce_sb[:, tcol:tcol + TB], ALU.mult), inc="v")
                S.wait("vector", "v")
                S.emit("vector", lambda e: e.tensor_copy(fl_sb[:], ic_sb[:, 0:8]), inc="v")
                v_ic = S.val("v")
                fa_pp[1] = {"v": v_ic}
                fa_sg = {"v": v_ic}
                S.wait("sync", "v", v_ic)
                S.dma("sync", ic_scr[ee * 1024 + hc * 128: ee * 1024 + (hc + 1) * 128,
                                     tcol:tcol + TB], ic_sb[:], "ic")
                fa_icsb = {"ic": S.val("ic")}
            fa_wgu[wb] = {"p": S.val("p")}
        fa_ce = {"v": S.val("v")}

    S.dma("gpsimd", wgu[0][:, 0:2048].rearrange("p (a f) -> p a f", f=128),
          swg_b[0].rearrange("(a p) f -> p a f", p=128), "g")
    S.dma("gpsimd", wgu[0][:, 2048:4096].rearrange("p (a f) -> p a f", f=128),
          swu_b[0].rearrange("(a p) f -> p a f", p=128), "g")
    for oc in range(16):
        wb = oc % 2
        S.wait("tensor", "g", S.val("g"))
        if oc < 15:
            S.wait_fa("gpsimd", fa_wgu[(oc + 1) % 2])
            S.dma("gpsimd", wgu[(oc + 1) % 2][:, 0:2048].rearrange("p (a f) -> p a f", f=128),
                  swg_b[oc + 1].rearrange("(a p) f -> p a f", p=128), "g")
            S.dma("gpsimd", wgu[(oc + 1) % 2][:, 2048:4096].rearrange("p (a f) -> p a f", f=128),
                  swu_b[oc + 1].rearrange("(a p) f -> p a f", p=128), "g")
        for tb4 in range(4):
            tcol = tb4 * TB
            S.wait_fa("tensor", fa_pp[0])
            for dc in range(16):
                S.emit("tensor", lambda e, dc=dc, wb=wb, tcol=tcol: e.matmul(
                    pp[0][:], wgu[wb][:, dc * 128:(dc + 1) * 128],
                    ARX[:, dc * 2048 + tcol: dc * 2048 + tcol + TB],
                    start=(dc == 0), stop=(dc == 15)),
                    inc=("p" if dc == 15 else None))
            p_g = S.val("p")
            S.wait_fa("tensor", fa_pp[1])
            for dc in range(16):
                S.emit("tensor", lambda e, dc=dc, wb=wb, tcol=tcol: e.matmul(
                    pp[1][:], wgu[wb][:, 2048 + dc * 128: 2048 + (dc + 1) * 128],
                    ARX[:, dc * 2048 + tcol: dc * 2048 + tcol + TB],
                    start=(dc == 0), stop=(dc == 15)),
                    inc=("p" if dc == 15 else None))
            p_u = S.val("p")
            S.wait("scalar", "p", p_g)
            S.wait_fa("scalar", fa_sg)
            S.emit("scalar", lambda e: e.activation(sg_sb[:], pp[0][:], AF.Silu), inc="a")
            a_sg = S.val("a")
            fa_pp[0] = {"a": a_sg}
            S.wait("vector", "a", a_sg)
            S.wait("vector", "p", p_u)
            S.wait_fa("vector", fa_icsb)
            S.emit("vector", lambda e: e.tensor_tensor(ic_sb[:], sg_sb[:], pp[1][:],
                                                       ALU.mult), inc="v")
            S.wait("vector", "v")
            S.emit("vector", lambda e: e.tensor_copy(fl_sb[:], ic_sb[:, 0:8]), inc="v")
            v_ic = S.val("v")
            fa_pp[1] = {"v": v_ic}
            fa_sg = {"v": v_ic}
            S.wait("sync", "v", v_ic)
            S.dma("sync", ish_scr[oc * 128:(oc + 1) * 128, tcol:tcol + TB], ic_sb[:], "ic")
            fa_icsb = {"ic": S.val("ic")}
        fa_wgu[wb] = {"p": S.val("p")}
    ic_all = S.val("ic")

    for eng in ("sync", "gpsimd", "tensor", "vector", "scalar"):
        S.wait(eng, "ic", ic_all)
        S.wait(eng, "p")
        S.wait(eng, "v")
        S.wait(eng, "a")

    def stripe_ic(k):
        return ARX[:, k * TB:(k + 1) * TB]

    def stripe_sh(k):
        return ARY[:, k * TB:(k + 1) * TB]

    fa_stripe = dict()
    for tb4 in range(4):
        tcol = tb4 * TB
        S.wait_fa("sync", fa_stripe)
        for k in range(64):
            S.dma("sync", stripe_ic(k), ic_scr[k * 128:(k + 1) * 128, tcol:tcol + TB], "s")
        for k in range(16):
            S.dma("sync", stripe_sh(k), ish_scr[k * 128:(k + 1) * 128, tcol:tcol + TB], "s")
        stripe_loaded = S.val("s")
        S.wait("tensor", "s", stripe_loaded)
        for dc in range(16):
            bank = dc % 2
            wb = dc % 2
            S.wait_fa("gpsimd", fa_wdn[wb])
            for ee in range(8):
                S.dma("gpsimd", wdn[wb][:, ee * 1024:(ee + 1) * 1024]
                      .rearrange("p (a f) -> p a f", f=128),
                      wd_b[ee, dc].rearrange("(a p) f -> p a f", p=128), "wl")
            S.dma("gpsimd", wdn[wb][:, 8192:10240].rearrange("p (a f) -> p a f", f=128),
                  swd_b[dc].rearrange("(a p) f -> p a f", p=128), "wl")
            wl_now = S.val("wl")
            S.wait("tensor", "wl", wl_now)
            S.wait_fa("tensor", fa_pp[bank])
            n_mm = 0
            for ee in range(8):
                for hk in range(8):
                    S.emit("tensor", lambda e, ee=ee, hk=hk, wb=wb, bank=bank: e.matmul(
                        pp[bank][:], wdn[wb][:, ee * 1024 + hk * 128: ee * 1024 + (hk + 1) * 128],
                        stripe_ic(ee * 8 + hk),
                        start=(ee == 0 and hk == 0), stop=False))
                    n_mm += 1
            for k in range(16):
                last = (k == 15)
                S.emit("tensor", lambda e, k=k, wb=wb, bank=bank, last=last: e.matmul(
                    pp[bank][:], wdn[wb][:, 8192 + k * 128: 8192 + (k + 1) * 128],
                    stripe_sh(k),
                    start=False, stop=last),
                    inc=("p" if last else None))
            p_dn = S.val("p")
            fa_wdn[wb] = {"p": p_dn}
            S.wait_fa("sync", fa_rtf)
            S.dma("sync", rt_f[:], y3_scr[dc * 128:(dc + 1) * 128, tcol:tcol + TB], "s")
            r_ld = S.val("s")
            S.wait("vector", "p", p_dn)
            S.wait("vector", "s", r_ld)
            S.wait_fa("vector", fa_ytf)
            S.emit("vector", lambda e, bank=bank: e.tensor_tensor(
                yt_f[:], pp[bank][:], rt_f[:], ALU.add), inc="v")
            S.wait("vector", "v")
            S.emit("vector", lambda e: e.tensor_copy(fl_sb[:], yt_f.bitcast(BF16)[:, 0:8]),
                   inc="v")
            v_y = S.val("v")
            fa_pp[bank] = {"v": v_y}
            fa_rtf = {"v": v_y}
            S.wait("sync", "v", v_y)
            S.dma("sync", out_my[dc * 128:(dc + 1) * 128, tcol:tcol + TB], yt_f[:], "ys")
            fa_ytf = {"ys": S.val("ys")}
        fa_stripe = {"p": S.val("p")}

    S.wait("sync", "ys", S.val("ys"))
    S.wait("gpsimd", "ys", S.val("ys"))


def stage_CD(inputs):
    wqkv = np.asarray(inputs["L1_attn_wqkv"], np.float32)
    wo = np.asarray(inputs["L1_attn_wo"], np.float32)
    wqkvT_b = np.ascontiguousarray(wqkv.T.reshape(D, 24, 128).transpose(1, 0, 2))
    woT_b = np.ascontiguousarray(wo.T.reshape(D, 16, 128).transpose(1, 0, 2))
    rw = np.ascontiguousarray(np.asarray(inputs["L1_ffn_router_w"], np.float32))
    wg = np.asarray(inputs["L1_ffn_w_gate"], np.float32)
    wu = np.asarray(inputs["L1_ffn_w_up"], np.float32)
    wdn = np.asarray(inputs["L1_ffn_w_down"], np.float32)
    wg_b = np.ascontiguousarray(wg.reshape(8, D, 8, 128).transpose(0, 2, 1, 3))
    wu_b = np.ascontiguousarray(wu.reshape(8, D, 8, 128).transpose(0, 2, 1, 3))
    wd_b = np.ascontiguousarray(wdn.reshape(8, 1024, 16, 128).transpose(0, 2, 1, 3))
    swg = np.asarray(inputs["L1_ffn_sw_gate"], np.float32)
    swu = np.asarray(inputs["L1_ffn_sw_up"], np.float32)
    swd = np.asarray(inputs["L1_ffn_sw_down"], np.float32)
    swg_b = np.ascontiguousarray(swg.reshape(D, 16, 128).transpose(1, 0, 2))
    swu_b = np.ascontiguousarray(swu.reshape(D, 16, 128).transpose(1, 0, 2))
    swd_b = np.ascontiguousarray(swd.reshape(D, 16, 128).transpose(1, 0, 2))
    ones = np.ones((128, 1), dtype=BF)
    onesr = np.ones((1, 128), dtype=BF)
    e2 = np.zeros((128, 256), dtype=np.float32)
    e2[:, 128] = 1.0
    e2 = e2.astype(BF)
    idb = np.eye(128, dtype=np.float32).astype(BF)
    idf = np.eye(128, dtype=np.float32)
    alib = np.zeros((128, 2), dtype=np.float32)
    n = 16
    m0 = 2.0 ** (-8.0 / n)
    slopes = m0 ** np.arange(1, n + 1, dtype=np.float64)
    pairs10 = [(a, b) for a in range(4) for b in range(a + 1)]
    for stk in range(2):
        for hh in range(8):
            h = stk * 8 + hh
            for pi_, (a, b) in enumerate(pairs10):
                alib[hh * 10 + pi_, stk] = slopes[h] * (-math.sqrt(a - b))

    in_maps = []
    for i in range(NCORES):
        in_maps.append(dict(
            wqkvT_b=wqkvT_b, wo1T_b=woT_b, rw_i=rw,
            wg_b=wg_b, wu_b=wu_b, wd_b=wd_b,
            swg_b=swg_b, swu_b=swu_b, swd_b=swd_b,
            e2_i=e2, idb_i=idb, idf_i=idf,
            alib_i=alib,
        ))
    return in_maps



def build_fused():
    nc = bass.Bass("TRN2", target_bir_lowering=False)
    G = {}
    DT = nc.dram_tensor

    G["x_fm"] = DT("x_fm", [D, T], F32, kind="ExternalInput").ap()
    G["wdq"] = DT("wdq", [D, CP], F32, kind="ExternalInput").ap()
    G["wdkv"] = DT("wdkv", [D, CP], F32, kind="ExternalInput").ap()
    G["wuq_m"] = DT("wuq_m", [CP, 256], F32, kind="ExternalInput").ap()
    G["wk_m"] = DT("wk_m", [CP, 256], F32, kind="ExternalInput").ap()
    G["wv_m"] = DT("wv_m", [CP, 256], F32, kind="ExternalInput").ap()
    G["qg_i"] = DT("qg_i", [128, 4], F32, kind="ExternalInput").ap()
    G["qb_i"] = DT("qb_i", [128, 4], F32, kind="ExternalInput").ap()
    G["kg_i"] = DT("kg_i", [128, 4], F32, kind="ExternalInput").ap()
    G["kb_i"] = DT("kb_i", [128, 4], F32, kind="ExternalInput").ap()
    G["ones_i"] = DT("ones_i", [128, 1], BF16, kind="ExternalInput").ap()
    G["onesr_i"] = DT("onesr_i", [1, 128], BF16, kind="ExternalInput").ap()
    G["trim_i"] = DT("trim_i", [4, 128, TB], BF16, kind="ExternalInput").ap()
    G["x_my"] = DT("x_my", [D, TT], F32, kind="ExternalInput").ap()
    G["woT_b"] = DT("woT_b", [16, D, 128], F32, kind="ExternalInput").ap()
    G["pk_g"] = DT("pk_g", [128, 1024, 128], I16, kind="ExternalInput").ap()
    G["pk_u"] = DT("pk_u", [128, 1024, 128], I16, kind="ExternalInput").ap()
    G["pk_d"] = DT("pk_d", [8192, D], I16, kind="ExternalInput").ap()
    G["sc_g"] = DT("sc_g", [64, NMLP], F32, kind="ExternalInput").ap()
    G["sc_u"] = DT("sc_u", [64, NMLP], F32, kind="ExternalInput").ap()
    G["sc_d"] = DT("sc_d", [512, D], F32, kind="ExternalInput").ap()
    G["wqkvT_b"] = DT("wqkvT_b", [24, D, 128], F32, kind="ExternalInput").ap()
    G["wo1T_b"] = DT("wo1T_b", [16, D, 128], F32, kind="ExternalInput").ap()
    G["rw_i"] = DT("rw_i", [D, 8], F32, kind="ExternalInput").ap()
    G["wg_b"] = DT("wg_b", [8, 8, D, 128], F32, kind="ExternalInput").ap()
    G["wu_b"] = DT("wu_b", [8, 8, D, 128], F32, kind="ExternalInput").ap()
    G["wd_b"] = DT("wd_b", [8, 16, 1024, 128], F32, kind="ExternalInput").ap()
    G["swg_b"] = DT("swg_b", [16, D, 128], F32, kind="ExternalInput").ap()
    G["swu_b"] = DT("swu_b", [16, D, 128], F32, kind="ExternalInput").ap()
    G["swd_b"] = DT("swd_b", [16, D, 128], F32, kind="ExternalInput").ap()
    G["e2_i"] = DT("e2_i", [128, 256], BF16, kind="ExternalInput").ap()
    G["idb_i"] = DT("idb_i", [128, 128], BF16, kind="ExternalInput").ap()
    G["idf_i"] = DT("idf_i", [128, 128], F32, kind="ExternalInput").ap()
    G["alib_i"] = DT("alib_i", [128, 2], F32, kind="ExternalInput").ap()

    G["out_my"] = DT("out_my", [D, TT], F32, kind="ExternalOutput").ap()

    G["k_scr"] = DT("k_scr", [256, T], BF16, kind="Internal").ap()
    G["cbounce"] = DT("cbounce", [1, 8], F32, kind="Internal").ap()
    G["v_scr"] = DT("v_scr", [T, 256], BF16, kind="Internal").ap()
    a2a_in_t = DT("a2a_in", [D, TT], BF16, kind="Internal")
    a2a_out_t = DT("a2a_out", [D, TT], BF16, kind="Internal")
    G["a2a_in"] = a2a_in_t.ap()
    G["a2a_out"] = a2a_out_t.ap()
    G["resid_scr"] = DT("resid_scr", [D, TT], F32, kind="Internal").ap()
    G["xeo_scr"] = DT("xeo_scr", [D, TT], BF16, kind="Internal").ap()
    G["inter_scr"] = DT("inter_scr", [NMLP, TT], BF16, kind="Internal").ap()
    G["wd_scr"] = DT("wd_scr", [16, NMLP, 128], BF16, kind="Internal").ap()
    G["y2_scr"] = DT("y2_scr", [D, TT], F32, kind="Internal").ap()
    G["y3_scr"] = DT("y3_scr", [D, TT], F32, kind="Internal").ap()
    G["attn_scr"] = DT("attn_scr", [D, TT], BF16, kind="Internal").ap()
    G["p_scr"] = DT("p_scr", [2, 128, TB], BF16, kind="Internal").ap()
    G["cb_scr"] = DT("cb_scr", [128, 128], F32, kind="Internal").ap()
    G["ic_scr"] = DT("ic_scr", [8 * 1024, TT], BF16, kind="Internal").ap()
    G["ish_scr"] = DT("ish_scr", [D, TT], BF16, kind="Internal").ap()

    A = nc.alloc_sbuf_tensor
    G["ARX"] = A("ARX", [128, 32768], BF16)
    G["ARY"] = A("ARY", [128, 16384], BF16)
    G["ARZW"] = A("ARZW", [128, 16384], BF16)
    G["WU"] = A("WU", [128, 20480], BF16)
    G["S_sb"] = A("S_sb", [128, TB], BF16)
    G["sT_sb"] = A("sT_sb", [128, TB], BF16)
    G["P_sb"] = A("P_sb", [128, TB], BF16)
    G["tmp_sb"] = [A(f"tmp_sb{i}", [128, TB], BF16) for i in range(2)]
    G["pb_sb"] = A("pb_sb", [128, TB], BF16)
    G["acc_sb"] = A("acc_sb", [128, TB], F32)
    G["tmp2_sb"] = A("tmp2_sb", [128, TB], F32)
    G["rt_f"] = A("rt_f", [128, TB], F32)
    G["yt_f"] = A("yt_f", [128, TB], F32)
    G["fl_sb"] = A("fl_sb", [128, 8], BF16)
    G["ones_sb"] = A("ones_sb", [128, 1], BF16)
    G["onesr_sb"] = A("onesr_sb", [1, 128], BF16)
    G["e2_sb"] = A("e2_sb", [128, 256], BF16)
    G["idb_sb"] = A("idb_sb", [128, 128], BF16)
    G["idf_sb"] = A("idf_sb", [128, 128], F32)
    G["alib_sb"] = A("alib_sb", [128, 2], F32)
    G["rw_sb"] = A("rw_sb", [128, 16 * 8], BF16)
    G["nmx"] = A("nmx", [128, 8], F32)
    G["rec1"] = A("rec1", [128, 8], F32)
    G["ro_sb"] = A("ro_sb", [128, 128], F32)
    G["e_sb"] = A("e_sb", [128, 128], F32)
    G["m1_sb"] = A("m1_sb", [128, 16], F32)
    G["m2_sb"] = A("m2_sb", [128, 16], F32)
    G["den_sb"] = A("den_sb", [128, 16], F32)
    G["eq_sb"] = A("eq_sb", [128, 128], F32)
    G["me_sb"] = A("me_sb", [128, 128], F32)
    G["en_sb"] = A("en_sb", [128, 128], F32)
    G["comb_sb"] = A("comb_sb", [128, 128], F32)
    G["combT_sb"] = A("combT_sb", [128, 128], F32)
    G["ce_sb"] = A("ce_sb", [128, TT], F32)
    G["sg_sb"] = A("sg_sb", [128, TB], BF16)
    G["ic_sb"] = A("ic_sb", [128, TB], BF16)
    G["st_f"] = [A(f"st_f{i}", [1, TB], F32) for i in range(3)]
    G["st_f"].append(G["st_f"][2])
    G["st_b"] = [A(f"st_b{i}", [1, TB], BF16) for i in range(2)]
    G["sq_sb"] = G["sg_sb"]
    G["xb_sb"] = G["ic_sb"]
    G["qg_sb"] = A("qg_sb", [128, 4], F32)
    G["qb_sb"] = A("qb_sb", [128, 4], F32)
    G["kg_sb"] = A("kg_sb", [128, 4], F32)
    G["kb_sb"] = A("kb_sb", [128, 4], F32)
    G["qn2"] = A("qn2", [1, 2 * (T // TB)], F32)
    G["kn2"] = A("kn2", [1, 2 * (T // TB)], F32)
    G["sc_sb"] = A("sc_sb", [1, 8], F32)
    G["cb_sb"] = A("cb_sb", [128, 1], F32)

    PT = nc.alloc_psum_tensor
    G["pp"] = [PT(f"pp{i}", [128, TB], F32) for i in range(3)]
    G["bc"] = [PT(f"bc{i}", [128, TB], F32) for i in range(2)]
    G["stp"] = [PT(f"stp{i}", [1, TB], F32) for i in range(2)]
    G["sum_ps"] = PT("sum_ps", [1, TB], F32)
    G["ptb"] = G["pp"][2].bitcast(BF16)[:, 0:128]
    G["ptf"] = G["pp"][2][:, 128:256]

    S = Sched(nc)

    def barrier():
        for eng in Sched.ENGS:
            for name in list(S.cnt):
                S.wait(eng, name)

    emit_A(nc, S, G)
    barrier()
    ccs = S.sem("cc")
    S.ops["gpsimd"].append(
        lambda e: e.collective_compute(
            "AllToAll", mybir.AluOpType.bypass,
            replica_groups=[list(range(NCORES))],
            ins=[a2a_in_t.ap().opt()],
            outs=[a2a_out_t.ap().opt()],
        ).then_inc(ccs))
    S.cnt["cc"] += 1
    barrier()
    emit_B1(nc, S, G)
    barrier()
    emit_CD(nc, S, G)
    S.build()
    return nc, S.n_ops()


def stage_fused(inputs):
    x = np.asarray(inputs["L0_attn_x"], dtype=np.float32)
    x_fm = np.ascontiguousarray(x.transpose(2, 0, 1).reshape(D, T))
    maps_a = stage_A(inputs)
    maps_b = stage_B1(inputs, x_fm)
    maps_cd = stage_CD(inputs)
    in_maps = []
    for i in range(NCORES):
        m = dict(maps_a[i])
        m.update(maps_b[i])
        m.update(maps_cd[i])
        in_maps.append(m)
    return in_maps


def kernel(**inputs):
    from concourse.aot_env import allow_ici_all_to_all
    with allow_ici_all_to_all():
        nc, _ = build_fused()
        in_maps = stage_fused(inputs)
        res = run_bass_kernel_spmd(nc, in_maps, core_ids=list(range(NCORES)))
    out = np.zeros((D, T), np.float32)
    for i in range(NCORES):
        out[:, token_cols(i)] = res.results[i]["out_my"]
    return np.ascontiguousarray(
        out.reshape(D, B, SL).transpose(1, 2, 0)).astype(np.float32)


# version 03
# speedup vs baseline: 1.1978x; 1.1978x over previous
import math
import numpy as np
import ml_dtypes
import concourse.bass as bass
import concourse.mybir as mybir
from concourse.bass_utils import run_bass_kernel_spmd

F32 = mybir.dt.float32
BF16 = mybir.dt.bfloat16
I16 = mybir.dt.int16
I32 = mybir.dt.int32
AF = mybir.ActivationFunctionType
ALU = mybir.AluOpType
AX = mybir.AxisListType

BF = ml_dtypes.bfloat16

D = 2048
CP = 512
T = 16384
SL = 4096
B = 4
DH = 128
NH = 16
EPS = 1e-5
ISQ = 1.0 / math.sqrt(128.0)

NCORES = 8
TB = 512


class Sched:
    ENGS = ("sync", "gpsimd", "vector", "scalar", "tensor")

    def __init__(self, nc):
        self.nc = nc
        self.ops = {e: [] for e in self.ENGS}
        self.handles = {}
        self.cnt = {}
        self.phase = "init"

    def sem(self, name):
        if name not in self.handles:
            self.handles[name] = self.nc.alloc_semaphore(name)
            self.cnt[name] = 0
        return self.handles[name]

    def wait(self, eng, name, val=None):
        h = self.sem(name)
        v = self.cnt[name] if val is None else val
        if v > 0:
            self.ops[eng].append((self.phase, lambda e, h=h, v=v: e.wait_ge(h, v)))

    def wait_fa(self, eng, free_at):
        for name, v in free_at.items():
            self.wait(eng, name, v)

    def emit(self, eng, fn, inc=None, inc_n=1):
        if inc is not None:
            h = self.sem(inc)
            self.cnt[inc] += inc_n
            self.ops[eng].append(
                (self.phase, lambda e, fn=fn, h=h, n=inc_n: fn(e).then_inc(h, n)))
        else:
            self.ops[eng].append((self.phase, fn))

    def dma(self, eng, out, in_, inc):
        self.emit(eng, lambda e, o=out, i=in_: e.dma_start(out=o, in_=i),
                  inc=inc, inc_n=16)

    def val(self, name):
        return self.cnt.get(name, 0)

    def mark(self, *names):
        return {n: self.val(n) for n in names}

    def n_ops(self):
        return sum(len(v) for v in self.ops.values())

    def build(self):
        with self.nc.Block() as blk:
            for eng in self.ENGS:
                if self.ops[eng]:
                    def runner(e, eng=eng):
                        i, ops, n = 0, self.ops[eng], len(self.ops[eng])
                        while i < n:
                            ph = ops[i][0]
                            with self.nc.named_scope(ph):
                                while i < n and ops[i][0] == ph:
                                    ops[i][1](e)
                                    i += 1
                    getattr(blk, eng)(runner)



def emit_A(nc, S, G):
    NTB = T // TB

    x_fm = G["x_fm"]; wdq = G["wdq"]; wdkv = G["wdkv"]
    wuq_m = G["wuq_m"]; wk_m = G["wk_m"]; wv_m = G["wv_m"]
    qg_i = G["qg_i"]; qb_i = G["qb_i"]; kg_i = G["kg_i"]; kb_i = G["kb_i"]
    ones_i = G["ones_i"]; onesr_i = G["onesr_i"]; trim_i = G["trim_i"]
    a2a_in = G["a2a_in"]
    k_scr = G["k_scr"]; cbounce = G["cbounce"]; v_scr = G["v_scr"]

    ARX = G["ARX"]; ARY = G["ARY"]; ARZW = G["ARZW"]; WU = G["WU"]
    q_sb = ARX
    xab = [ARY[:, 0:8192], ARY[:, 8192:16384]]
    xsq = ARZW[:, 0:8192]
    wdq_sb = ARZW[:, 8192:16384]
    wdkv_sb = WU[:, 0:8192]
    raw_sb = WU[:, 8192:10240]
    sq_sb = WU[:, 10240:12288]
    nrm_sb = WU[:, 12288:14336]
    wuq_sb = WU[:, 14336:15360]
    wk_sb = WU[:, 15360:16384]
    wv_sb = WU[:, 16384:17408]
    trim_sb = WU[:, 17408:19456]
    kev_sb = G["S_sb"]
    vev_sb = G["sT_sb"][:, 0:256]
    o_sb = G["P_sb"]
    rec_sb = G["rt_f"]
    qg_sb = G["qg_sb"]; qb_sb = G["qb_sb"]; kg_sb = G["kg_sb"]; kb_sb = G["kb_sb"]
    ones_sb = G["ones_sb"]; onesr_sb = G["onesr_sb"]
    st_f = G["st_f"]; st_b = G["st_b"]
    qn2 = G["qn2"]; kn2 = G["kn2"]
    sc_sb = G["sc_sb"]; cb_sb = G["cb_sb"]
    pp = G["pp"]; bc = G["bc"]; stp = G["stp"]; sum_ps = G["sum_ps"]

    fa_pp = [dict(), dict(), dict()]
    fa_bc = [dict(), dict()]
    fa_stp = [dict(), dict()]
    fa_sum = dict()
    fa_stf = dict()
    fa_xab = [dict(), dict()]
    fa_xsq = dict()
    fa_raw = dict()
    fa_sq = dict()
    fa_nrm = dict()
    fa_osb = dict()
    fa_kev = dict()
    fa_vev = dict()
    fa_cb = dict()
    fa_scb = dict()

    for dc in range(16):
        S.dma("gpsimd", wdq_sb[:, dc * CP:(dc + 1) * CP],
              wdq[dc * 128:(dc + 1) * 128, :], "g")
        S.dma("gpsimd", wdkv_sb[:, dc * CP:(dc + 1) * CP],
              wdkv[dc * 128:(dc + 1) * 128, :], "g")
    for cc in range(4):
        S.dma("gpsimd", wuq_sb[:, cc * 256:(cc + 1) * 256],
              wuq_m[cc * 128:(cc + 1) * 128, :], "g")
        S.dma("gpsimd", wk_sb[:, cc * 256:(cc + 1) * 256],
              wk_m[cc * 128:(cc + 1) * 128, :], "g")
        S.dma("gpsimd", wv_sb[:, cc * 256:(cc + 1) * 256],
              wv_m[cc * 128:(cc + 1) * 128, :], "g")
    S.dma("sync", qg_sb[:], qg_i, "s")
    S.dma("sync", qb_sb[:], qb_i, "s")
    S.dma("sync", kg_sb[:], kg_i, "s")
    S.dma("sync", kb_sb[:], kb_i, "s")
    S.dma("sync", ones_sb[:], ones_i, "s")
    S.dma("sync", onesr_sb[:], onesr_i, "s")
    for j in range(4):
        S.dma("sync", trim_sb[:, j * TB:(j + 1) * TB], trim_i[j], "s")
    w_g = S.val("g")
    w_s = S.val("s")

    def load_x_tile(tb, buf):
        S.wait_fa("gpsimd", fa_xab[buf])
        for dc in range(16):
            S.dma("gpsimd", xab[buf][:, dc * TB:(dc + 1) * TB],
                  x_fm[dc * 128:(dc + 1) * 128, tb * TB:(tb + 1) * TB], "g")
        return S.val("g")

    g_loaded = {0: load_x_tile(0, 0)}

    S.wait("tensor", "g", w_g)
    S.wait("tensor", "s", w_s)
    S.wait("scalar", "g", w_g)
    S.wait("scalar", "s", w_s)

    def ln_stats(n_inv):
        S.wait("vector", "p")
        S.wait_fa("vector", fa_stf)
        S.emit("vector", lambda e: e.tensor_scalar_mul(st_f[0][:], stp[0][:], n_inv))
        S.emit("vector", lambda e: e.tensor_scalar_mul(st_f[1][:], stp[1][:], n_inv))
        S.emit("vector", lambda e: e.tensor_tensor(st_f[2][:], st_f[0][:], st_f[0][:], ALU.mult),
               inc="v")
        S.wait("vector", "v")
        S.emit("vector", lambda e: e.tensor_tensor(st_f[1][:], st_f[1][:], st_f[2][:], ALU.subtract))
        S.emit("vector", lambda e: e.tensor_scalar_add(st_f[1][:], st_f[1][:], EPS),
               inc="v")
        fa_stp[0]["v"] = S.val("v")
        fa_stp[1]["v"] = S.val("v")
        S.wait("scalar", "v")
        S.emit("scalar", lambda e: e.activation(st_f[1][:], st_f[1][:], AF.Ln))
        S.emit("scalar", lambda e: e.activation(st_f[1][:], st_f[1][:], AF.Exp, scale=-0.5),
               inc="a")
        S.wait("vector", "a")
        S.emit("vector", lambda e: e.tensor_copy(st_b[0][:], st_f[0][:]))
        S.emit("vector", lambda e: e.tensor_copy(st_b[1][:], st_f[1][:]), inc="v")
        S.wait("tensor", "v")
        S.wait_fa("tensor", fa_bc[0])
        S.wait_fa("tensor", fa_bc[1])
        S.emit("tensor", lambda e: e.matmul(bc[0][:], onesr_sb[:], st_b[0][:],
                                            start=True, stop=True))
        S.emit("tensor", lambda e: e.matmul(bc[1][:], onesr_sb[:], st_b[1][:],
                                            start=True, stop=True), inc="p")
        fa_stf["p"] = S.val("p")

    for tb in range(NTB):
        buf = tb % 2
        S.wait("scalar", "g", g_loaded[tb])
        S.wait_fa("scalar", fa_xsq)
        for dc in range(16):
            S.emit("scalar", lambda e, dc=dc, buf=buf: e.activation(
                xsq[:, dc * TB:(dc + 1) * TB],
                xab[buf][:, dc * TB:(dc + 1) * TB], AF.Square),
                inc=("a" if dc == 15 else None))
        a_sq = S.val("a")
        S.wait("tensor", "g", g_loaded[tb])
        S.wait_fa("tensor", fa_stp[0])
        for dc in range(16):
            S.emit("tensor", lambda e, dc=dc, buf=buf: e.matmul(
                stp[0][:], ones_sb[:], xab[buf][:, dc * TB:(dc + 1) * TB],
                start=(dc == 0), stop=(dc == 15)),
                inc=("p" if dc == 15 else None))
        S.wait("tensor", "a", a_sq)
        S.wait_fa("tensor", fa_stp[1])
        for dc in range(16):
            S.emit("tensor", lambda e, dc=dc: e.matmul(
                stp[1][:], ones_sb[:], xsq[:, dc * TB:(dc + 1) * TB],
                start=(dc == 0), stop=(dc == 15)),
                inc=("p" if dc == 15 else None))
        ln_stats(1.0 / D)
        S.wait("vector", "p")
        for dc in range(16):
            S.emit("vector", lambda e, dc=dc, buf=buf: e.tensor_tensor(
                xsq[:, dc * TB:(dc + 1) * TB],
                xab[buf][:, dc * TB:(dc + 1) * TB], bc[0][:], ALU.subtract))
            S.emit("vector", lambda e, dc=dc: e.tensor_tensor(
                xsq[:, dc * TB:(dc + 1) * TB],
                xsq[:, dc * TB:(dc + 1) * TB], bc[1][:], ALU.mult),
                inc=("v" if dc == 15 else None))
        v_xa = S.val("v")
        fa_xab[buf] = {"v": v_xa}
        fa_bc[0] = {"v": v_xa}
        fa_bc[1] = {"v": v_xa}
        if tb + 1 < NTB:
            g_loaded[tb + 1] = load_x_tile(tb + 1, 1 - buf)

        for br in range(2):
            wsb = wdq_sb if br == 0 else wdkv_sb
            gg = qg_sb if br == 0 else kg_sb
            bbv = qb_sb if br == 0 else kb_sb
            S.wait("tensor", "v", v_xa)
            for cc in range(4):
                bank = cc % 2
                S.wait_fa("tensor", fa_pp[bank])
                for dc in range(16):
                    S.emit("tensor", lambda e, dc=dc, cc=cc, wsb=wsb, bank=bank: e.matmul(
                        pp[bank][:],
                        wsb[:, dc * CP + cc * 128: dc * CP + (cc + 1) * 128],
                        xsq[:, dc * TB:(dc + 1) * TB],
                        start=(dc == 0), stop=(dc == 15)),
                        inc=("p" if dc == 15 else None))
                p_proj = S.val("p")
                S.wait("scalar", "p", p_proj)
                if cc < 2:
                    S.wait_fa("scalar", fa_raw)
                S.emit("scalar", lambda e, cc=cc, bank=bank: e.activation(
                    raw_sb[:, cc * TB:(cc + 1) * TB], pp[bank][:], AF.Identity),
                    inc="a")
                fa_pp[bank] = {"a": S.val("a")}
            a_raw = S.val("a")
            S.wait_fa("scalar", fa_sq)
            for cc in range(4):
                S.emit("scalar", lambda e, cc=cc: e.activation(
                    sq_sb[:, cc * TB:(cc + 1) * TB],
                    raw_sb[:, cc * TB:(cc + 1) * TB], AF.Square),
                    inc=("a" if cc == 3 else None))
            a_sq2 = S.val("a")
            S.wait("tensor", "a", a_raw)
            S.wait_fa("tensor", fa_stp[0])
            for cc in range(4):
                S.emit("tensor", lambda e, cc=cc: e.matmul(
                    stp[0][:], ones_sb[:], raw_sb[:, cc * TB:(cc + 1) * TB],
                    start=(cc == 0), stop=(cc == 3)),
                    inc=("p" if cc == 3 else None))
            S.wait("tensor", "a", a_sq2)
            S.wait_fa("tensor", fa_stp[1])
            for cc in range(4):
                S.emit("tensor", lambda e, cc=cc: e.matmul(
                    stp[1][:], ones_sb[:], sq_sb[:, cc * TB:(cc + 1) * TB],
                    start=(cc == 0), stop=(cc == 3)),
                    inc=("p" if cc == 3 else None))
            ln_stats(1.0 / CP)
            S.wait("vector", "p")
            S.wait_fa("vector", fa_sq)
            for cc in range(4):
                S.emit("vector", lambda e, cc=cc: e.tensor_tensor(
                    sq_sb[:, cc * TB:(cc + 1) * TB],
                    raw_sb[:, cc * TB:(cc + 1) * TB], bc[0][:], ALU.subtract))
                S.emit("vector", lambda e, cc=cc: e.tensor_tensor(
                    sq_sb[:, cc * TB:(cc + 1) * TB],
                    sq_sb[:, cc * TB:(cc + 1) * TB], bc[1][:], ALU.mult),
                    inc=("v" if cc == 3 else None))
            v_n0 = S.val("v")
            fa_bc[0] = {"v": v_n0}
            fa_bc[1] = {"v": v_n0}
            fa_raw = {"v": v_n0}
            S.wait("scalar", "v", v_n0)
            S.wait_fa("scalar", fa_nrm)
            for cc in range(4):
                S.emit("scalar", lambda e, cc=cc, gg=gg, bbv=bbv: e.activation(
                    nrm_sb[:, cc * TB:(cc + 1) * TB],
                    sq_sb[:, cc * TB:(cc + 1) * TB], AF.Identity,
                    bias=bbv[:, cc:cc + 1], scale=gg[:, cc:cc + 1]),
                    inc=("a" if cc == 3 else None))
            a_nrm = S.val("a")
            fa_sq["a"] = a_nrm

            if br == 0:
                S.wait("tensor", "a", a_nrm)
                for oc in range(2):
                    S.wait_fa("tensor", fa_pp[2])
                    for cc in range(4):
                        S.emit("tensor", lambda e, cc=cc, oc=oc: e.matmul(
                            pp[2][:],
                            wuq_sb[:, cc * 256 + oc * 128: cc * 256 + (oc + 1) * 128],
                            nrm_sb[:, cc * TB:(cc + 1) * TB],
                            start=(cc == 0), stop=(cc == 3)),
                            inc=("p" if cc == 3 else None))
                    p_q = S.val("p")
                    S.wait("scalar", "p", p_q)
                    S.emit("scalar", lambda e, oc=oc, tb=tb: e.activation(
                        q_sb[:, oc * T + tb * TB: oc * T + (tb + 1) * TB],
                        pp[2][:], AF.Identity), inc="a")
                    S.wait_fa("scalar", fa_sq)
                    S.emit("scalar", lambda e: e.activation(
                        sq_sb[:, 0:TB], pp[2][:], AF.Square), inc="a")
                    a_qs = S.val("a")
                    fa_pp[2] = {"a": a_qs}
                    S.wait("tensor", "a", a_qs)
                    S.wait_fa("tensor", fa_stp[0])
                    S.emit("tensor", lambda e: e.matmul(
                        stp[0][:], ones_sb[:], sq_sb[:, 0:TB],
                        start=True, stop=True), inc="p")
                    p_qn = S.val("p")
                    fa_sq = {"p": p_qn}
                    S.wait("vector", "p", p_qn)
                    S.emit("vector", lambda e, oc=oc, tb=tb: e.reduce_max(
                        qn2[0:1, oc * NTB + tb: oc * NTB + tb + 1], stp[0][:],
                        axis=AX.X), inc="v")
                    fa_stp[0] = {"v": S.val("v")}
                fa_nrm = {"p": S.val("p")}
            else:
                S.wait("tensor", "a", a_nrm)
                for oc in range(2):
                    S.wait_fa("tensor", fa_pp[2])
                    for cc in range(4):
                        S.emit("tensor", lambda e, cc=cc, oc=oc: e.matmul(
                            pp[2][:],
                            wk_sb[:, cc * 256 + oc * 128: cc * 256 + (oc + 1) * 128],
                            nrm_sb[:, cc * TB:(cc + 1) * TB],
                            start=(cc == 0), stop=(cc == 3)),
                            inc=("p" if cc == 3 else None))
                    p_k = S.val("p")
                    S.wait("scalar", "p", p_k)
                    S.wait_fa("scalar", fa_kev)
                    S.emit("scalar", lambda e: e.activation(
                        kev_sb[:], pp[2][:], AF.Identity), inc="a")
                    a_ke = S.val("a")
                    S.wait_fa("scalar", fa_sq)
                    S.emit("scalar", lambda e: e.activation(
                        sq_sb[:, 0:TB], pp[2][:], AF.Square), inc="a")
                    a_ks = S.val("a")
                    fa_pp[2] = {"a": a_ks}
                    S.wait("sync", "a", a_ke)
                    S.dma("sync", k_scr[oc * 128:(oc + 1) * 128, tb * TB:(tb + 1) * TB],
                          kev_sb[:], "ks")
                    fa_kev = {"ks": S.val("ks")}
                    S.wait("tensor", "a", a_ks)
                    S.wait_fa("tensor", fa_stp[0])
                    S.emit("tensor", lambda e: e.matmul(
                        stp[0][:], ones_sb[:], sq_sb[:, 0:TB],
                        start=True, stop=True), inc="p")
                    p_kn = S.val("p")
                    fa_sq = {"p": p_kn}
                    S.wait("vector", "p", p_kn)
                    S.emit("vector", lambda e, oc=oc, tb=tb: e.reduce_max(
                        kn2[0:1, oc * NTB + tb: oc * NTB + tb + 1], stp[0][:],
                        axis=AX.X), inc="v")
                    fa_stp[0] = {"v": S.val("v")}
                for ts_ in range(4):
                    S.wait_fa("tensor", fa_pp[2])
                    for cc in range(4):
                        S.emit("tensor", lambda e, cc=cc, ts_=ts_: e.matmul(
                            pp[2][:, 0:256],
                            nrm_sb[:, cc * TB + ts_ * 128: cc * TB + ts_ * 128 + 128],
                            wv_sb[:, cc * 256:(cc + 1) * 256],
                            start=(cc == 0), stop=(cc == 3)),
                            inc=("p" if cc == 3 else None))
                    p_v = S.val("p")
                    S.wait("scalar", "p", p_v)
                    S.wait_fa("scalar", fa_vev)
                    S.emit("scalar", lambda e: e.activation(
                        vev_sb[:], pp[2][:, 0:256], AF.Identity), inc="a")
                    a_ve = S.val("a")
                    fa_pp[2] = {"a": a_ve}
                    S.wait("sync", "a", a_ve)
                    S.dma("sync",
                          v_scr[tb * TB + ts_ * 128: tb * TB + (ts_ + 1) * 128, :],
                          vev_sb[:], "ks")
                    fa_vev = {"ks": S.val("ks")}
                fa_nrm = {"p": S.val("p")}
        fa_xsq["p"] = S.val("p")

    S.phase = "A_attn"
    kb_ = [xab[0][:, 0:4096], xab[1][:, 0:4096]]
    vb_ = [xab[0][:, 4096:8192], xab[1][:, 4096:8192]]
    pt_ = [xsq[:, 0:TB], xsq[:, TB:2 * TB]]
    fa_kvbuf = [dict(fa_xab[0]), dict(fa_xab[1])]
    fa_pt = [dict(fa_xsq), dict(fa_xsq)]

    ks_all = S.val("ks")
    S.wait("sync", "ks", ks_all)

    pairs = [(b, hl) for b in range(B) for hl in range(2)]
    a_exp = {}
    p_cons = {}
    fa_recsb = dict()

    for pi, (b, hl) in enumerate(pairs):
        lb = pi % 2
        S.wait_fa("sync", fa_kvbuf[lb])
        S.dma("sync", kb_[lb], k_scr[hl * 128:(hl + 1) * 128,
                                     b * SL:(b + 1) * SL], "kv")
        vsrc3 = v_scr[b * SL:(b + 1) * SL, hl * 128:(hl + 1) * 128] \
            .rearrange("(kc p) f -> p kc f", p=128)
        vb3 = vb_[lb].rearrange("p (kc f) -> p kc f", f=128)
        for qu in range(4):
            S.dma("sync", vb3[:, qu * 8:(qu + 1) * 8, :],
                  vsrc3[:, qu * 8:(qu + 1) * 8, :], "kv")
        kv_loaded = S.val("kv")

        S.wait("vector", "v")
        S.wait_fa("vector", fa_scb)
        S.emit("vector", lambda e, hl=hl, b=b: e.reduce_max(
            sc_sb[:, 0:1],
            qn2[0:1, hl * NTB + b * 8: hl * NTB + (b + 1) * 8], axis=AX.X))
        S.emit("vector", lambda e, hl=hl, b=b: e.reduce_max(
            sc_sb[:, 1:2],
            kn2[0:1, hl * NTB + b * 8: hl * NTB + (b + 1) * 8], axis=AX.X),
            inc="v")
        S.wait("vector", "v")
        S.emit("vector", lambda e: e.tensor_tensor(
            sc_sb[:, 2:3], sc_sb[:, 0:1], sc_sb[:, 1:2], ALU.mult), inc="v")
        S.wait("scalar", "v")
        S.emit("scalar", lambda e: e.activation(sc_sb[:, 3:4], sc_sb[:, 2:3], AF.Ln))
        S.emit("scalar", lambda e: e.activation(sc_sb[:, 3:4], sc_sb[:, 3:4], AF.Exp, scale=0.5),
               inc="a")
        S.wait("vector", "a")
        S.emit("vector", lambda e: e.tensor_scalar_mul(sc_sb[:, 4:5], sc_sb[:, 3:4],
                                                       -1.02 * ISQ), inc="v")
        v_c2 = S.val("v")
        S.wait("sync", "v", v_c2)
        S.dma("sync", cbounce[:, 0:1], sc_sb[:, 4:5], "cbd")
        S.wait("sync", "cbd")
        S.wait_fa("sync", fa_cb)
        S.dma("sync", cb_sb[:], cbounce[:, 0:1].broadcast_to([128, 1]), "cbd")
        v_cb = S.val("cbd")
        fa_scb = {"cbd": v_cb}

        S.wait("tensor", "kv", kv_loaded)
        S.wait("scalar", "cbd", v_cb)

        job = 0
        a_cb_users_last = None
        for qb in range(8):
            njt = 4 * qb + 4
            qcol = b * SL + qb * TB
            for j in range(njt):
                bank = job % 2
                ptile = pt_[job % 2]
                S.wait_fa("tensor", fa_pp[bank])
                S.emit("tensor", lambda e, j=j, hl=hl, qcol=qcol, bank=bank, lb=lb: e.matmul(
                    pp[bank][:], kb_[lb][:, j * 128:(j + 1) * 128],
                    q_sb[:, hl * T + qcol: hl * T + qcol + TB],
                    start=True, stop=True), inc="p")
                p_s = S.val("p")
                S.wait("scalar", "p", p_s)
                S.wait_fa("scalar", fa_pt[job % 2])
                S.emit("scalar", lambda e, bank=bank, ptile=ptile: e.activation(
                    ptile, pp[bank][:], AF.Exp, bias=cb_sb[:], scale=ISQ), inc="a")
                a_exp[job] = S.val("a")
                a_cb_users_last = a_exp[job]
                fa_pp[bank] = {"a": a_exp[job]}
                diag = j >= 4 * qb
                vm_now = None
                if diag:
                    jr = j - 4 * qb
                    S.wait("vector", "a", a_exp[job])
                    S.emit("vector", lambda e, jr=jr, ptile=ptile: e.tensor_tensor(
                        ptile, ptile, trim_sb[:, jr * TB:(jr + 1) * TB],
                        ALU.mult), inc="vm")
                    vm_now = S.val("vm")
                S.wait("tensor", "a", a_exp[job])
                if diag:
                    S.wait("tensor", "vm", vm_now)
                first = (j == 0)
                last = (j == njt - 1)
                if first:
                    S.wait_fa("tensor", fa_pp[2])
                    S.wait_fa("tensor", fa_sum)
                S.emit("tensor", lambda e, j=j, lb=lb, ptile=ptile, first=first, last=last: e.matmul(
                    pp[2][:], vb_[lb][:, j * 128:(j + 1) * 128], ptile,
                    start=first, stop=last))
                S.emit("tensor", lambda e, ptile=ptile, first=first, last=last: e.matmul(
                    sum_ps[:], ones_sb[:], ptile,
                    start=first, stop=last), inc="p")
                p_cons[job] = S.val("p")
                fa_pt[job % 2] = {"p": p_cons[job]}
                job += 1
            p_qb = S.val("p")
            S.wait("vector", "p", p_qb)
            S.wait_fa("vector", fa_stf)
            S.emit("vector", lambda e: e.tensor_copy(st_f[3][:], sum_ps[:]), inc="v")
            S.wait("vector", "v")
            S.emit("vector", lambda e: e.reciprocal(st_f[3][:], st_f[3][:]))
            S.emit("vector", lambda e: e.tensor_copy(st_b[0][:], st_f[3][:]), inc="v")
            v_rec = S.val("v")
            fa_sum = {"v": v_rec}
            S.wait("tensor", "v", v_rec)
            S.wait_fa("tensor", fa_bc[1])
            S.emit("tensor", lambda e: e.matmul(bc[1][:], onesr_sb[:], st_b[0][:],
                                                start=True, stop=True), inc="p")
            p_recb = S.val("p")
            fa_stf = {"p": p_recb}
            S.wait("vector", "p", p_recb)
            S.wait_fa("vector", fa_recsb)
            S.emit("vector", lambda e: e.tensor_copy(rec_sb[:], bc[1][:]), inc="v")
            v_recb = S.val("v")
            fa_bc[1] = {"v": v_recb}
            S.wait_fa("vector", fa_osb)
            S.emit("vector", lambda e: e.tensor_tensor(o_sb[:], pp[2][:], rec_sb[:],
                                                       ALU.mult), inc="v")
            v_onrm = S.val("v")
            fa_pp[2] = {"v": v_onrm}
            fa_recsb = {"v": v_onrm}
            S.wait("sync", "v", v_onrm)
            S.dma("sync", a2a_in[qb * 256 + hl * 128: qb * 256 + (hl + 1) * 128,
                                 b * TB:(b + 1) * TB],
                  o_sb[:], "os")
            fa_osb = {"os": S.val("os")}
        fa_kvbuf[lb] = {"p": S.val("p")}
        fa_cb = {"a": a_cb_users_last}

    S.wait("sync", "os", S.val("os"))
    S.wait("gpsimd", "os", S.val("os"))



def stage_A(inputs):
    x = np.asarray(inputs["L0_attn_x"], dtype=np.float32)
    x_fm = np.ascontiguousarray(x.transpose(2, 0, 1).reshape(D, T))
    wdq = np.ascontiguousarray(np.asarray(inputs["L0_attn_W_dq"], np.float32))
    wdkv = np.ascontiguousarray(np.asarray(inputs["L0_attn_W_dkv"], np.float32))
    wuq = np.asarray(inputs["L0_attn_W_uq"], np.float32)
    wukv = np.asarray(inputs["L0_attn_W_ukv"], np.float32)
    qg = np.asarray(inputs["L0_attn_q_ln_g"], np.float32)
    qb = np.asarray(inputs["L0_attn_q_ln_b"], np.float32)
    kg = np.asarray(inputs["L0_attn_kv_ln_g"], np.float32)
    kb = np.asarray(inputs["L0_attn_kv_ln_b"], np.float32)

    qg_h = np.ascontiguousarray(qg.reshape(4, 128).T)
    qb_h = np.ascontiguousarray(qb.reshape(4, 128).T)
    kg_h = np.ascontiguousarray(kg.reshape(4, 128).T)
    kb_h = np.ascontiguousarray(kb.reshape(4, 128).T)
    ones = np.ones((128, 1), dtype=BF)
    onesr = np.ones((1, 128), dtype=BF)
    trim = np.zeros((4, 128, TB), dtype=np.float32)
    for jr in range(4):
        for k in range(128):
            trim[jr, k, jr * 128 + k:] = 1.0
    trim = trim.astype(BF)

    in_maps = []
    for i in range(NCORES):
        sl = slice(256 * i, 256 * (i + 1))
        in_maps.append(dict(
            x_fm=x_fm,
            wdq=wdq, wdkv=wdkv,
            wuq_m=np.ascontiguousarray(wuq[:, sl]),
            wk_m=np.ascontiguousarray(wukv[:, sl]),
            wv_m=np.ascontiguousarray(wukv[:, 2048 + 256 * i: 2048 + 256 * (i + 1)]),
            qg_i=qg_h, qb_i=qb_h, kg_i=kg_h, kb_i=kb_h,
            ones_i=ones, onesr_i=onesr, trim_i=trim,
        ))
    return in_maps



GELU_A = 1.5957691216057308
GELU_B = 0.07135481282316932
TT = 2048
NMLP = 16384


def emit_B1(nc, S, G):
    o_my = G["a2a_out"]
    x_my = G["x_my"]
    woT_b = G["woT_b"]
    pk_g = G["pk_g"]; pk_u = G["pk_u"]; pk_d = G["pk_d"]
    sc_g = G["sc_g"]; sc_u = G["sc_u"]; sc_d = G["sc_d"]
    y2_my = G["y2_scr"]
    resid_scr = G["resid_scr"]; xeo_scr = G["xeo_scr"]
    inter_scr = G["inter_scr"]; wd_scr = G["wd_scr"]

    ARX = G["ARX"]; ARY = G["ARY"]; ARZW = G["ARZW"]; WU = G["WU"]
    AR1 = ARX
    AR2 = ARY
    AR3 = ARZW
    wob = [WU[:, 0:2048], WU[:, 2048:4096]]
    wdt = [WU[:, 4096:8192], WU[:, 8192:12288]]
    pkb = WU[:, 12288:12800].bitcast(I16)
    pkb2 = WU[:, 12800:13312].bitcast(I16)
    stl = [WU[:, 13312:14336].bitcast(F32), WU[:, 14336:15360].bitcast(F32)]
    cmi = WU[:, 15360:15872].bitcast(I16)
    sgn = WU[:, 15872:16384].bitcast(I16)
    cmf = WU[:, 16384:16896]
    t1s = WU[:, 16896:17408]
    t2s = WU[:, 17408:17920]
    g2b = WU[:, 17920:18432]
    csb = WU[:, 18432:18944]
    asb = WU[:, 18944:19456]
    ssb = WU[:, 19456:19968]
    hsb = WU[:, 19968:20480]
    gsb = G["sg_sb"]
    isb = G["ic_sb"]
    rt_f = G["rt_f"]; yt_f = G["yt_f"]; rec_dn = G["acc_sb"]
    fl_sb = G["fl_sb"]; ones_sb = G["ones_sb"]; onesr_sb = G["onesr_sb"]
    st_f = G["st_f"]; st_b = G["st_b"]
    pp = G["pp"]; bc = G["bc"]; stp = G["stp"]

    rt32 = ARY.bitcast(F32)
    fa_pp = [dict(), dict()]
    fa_bc = [dict(), dict()]
    fa_stp = [dict(), dict()]
    fa_stf = dict()
    fa_ar1 = dict()
    fa_rt = dict()
    fa_sq = dict()
    fa_wob = [dict(), dict()]
    fa_pkb = [dict(), dict()]
    fa_stl = [dict(), dict()]
    fa_dec = [dict(), dict()]
    fa_chain = dict()
    fa_wdt = [dict(), dict()]
    fa_ytf = dict()
    fa_rtf = dict()
    fa_isb = dict()

    w_s = S.val("s")

    def ln_stats(n_inv):
        S.wait("vector", "p")
        S.wait_fa("vector", fa_stf)
        S.emit("vector", lambda e: e.tensor_scalar_mul(st_f[0][:], stp[0][:], n_inv))
        S.emit("vector", lambda e: e.tensor_scalar_mul(st_f[1][:], stp[1][:], n_inv))
        S.emit("vector", lambda e: e.tensor_tensor(st_f[2][:], st_f[0][:], st_f[0][:], ALU.mult),
               inc="v")
        S.wait("vector", "v")
        S.emit("vector", lambda e: e.tensor_tensor(st_f[1][:], st_f[1][:], st_f[2][:], ALU.subtract))
        S.emit("vector", lambda e: e.tensor_scalar_add(st_f[1][:], st_f[1][:], EPS),
               inc="v")
        fa_stp[0]["v"] = S.val("v")
        fa_stp[1]["v"] = S.val("v")
        S.wait("scalar", "v")
        S.emit("scalar", lambda e: e.activation(st_f[1][:], st_f[1][:], AF.Ln))
        S.emit("scalar", lambda e: e.activation(st_f[1][:], st_f[1][:], AF.Exp, scale=-0.5),
               inc="a")
        S.wait("vector", "a")
        S.emit("vector", lambda e: e.tensor_copy(st_b[0][:], st_f[0][:]))
        S.emit("vector", lambda e: e.tensor_copy(st_b[1][:], st_f[1][:]), inc="v")
        S.wait("tensor", "v")
        S.wait_fa("tensor", fa_bc[0])
        S.wait_fa("tensor", fa_bc[1])
        S.emit("tensor", lambda e: e.matmul(bc[0][:], onesr_sb[:], st_b[0][:],
                                            start=True, stop=True))
        S.emit("tensor", lambda e: e.matmul(bc[1][:], onesr_sb[:], st_b[1][:],
                                            start=True, stop=True), inc="p")
        fa_stf["p"] = S.val("p")

    S.phase = "B_wo"
    for oc in range(16):
        S.dma("sync", AR1[:, oc * TT:(oc + 1) * TT],
              o_my[oc * 128:(oc + 1) * 128, :], "s")
    o_loaded = S.val("s")
    S.dma("gpsimd", wob[0].rearrange("p (oc f) -> p oc f", f=128),
          woT_b[0].rearrange("(oc p) f -> p oc f", p=128), "g")

    S.wait("tensor", "s", w_s)

    for tb4 in range(4):
        tcol = tb4 * TB
        for dout in range(16):
            wb = dout % 2
            nxt = dout + 1 if dout < 15 else (0 if tb4 < 3 else None)
            S.wait("tensor", "g", S.val("g"))
            if dout == 0:
                S.wait("tensor", "s", o_loaded)
            bank = dout % 2
            S.wait_fa("tensor", fa_pp[bank])
            for oc in range(16):
                S.emit("tensor", lambda e, oc=oc, dout=dout, wb=wb, bank=bank, tcol=tcol: e.matmul(
                    pp[bank][:],
                    wob[wb][:, oc * 128:(oc + 1) * 128],
                    AR1[:, oc * TT + tcol: oc * TT + tcol + TB],
                    start=(oc == 0), stop=(oc == 15)),
                    inc=("p" if oc == 15 else None))
            p_wo = S.val("p")
            if nxt is not None:
                S.wait_fa("gpsimd", fa_wob[nxt % 2])
                S.dma("gpsimd", wob[nxt % 2].rearrange("p (oc f) -> p oc f", f=128),
                      woT_b[nxt].rearrange("(oc p) f -> p oc f", p=128), "g")
            S.wait_fa("sync", fa_rtf)
            S.dma("sync", rt_f[:], x_my[dout * 128:(dout + 1) * 128, tcol:tcol + TB], "s")
            x_ld = S.val("s")
            S.wait("vector", "p", p_wo)
            S.wait("vector", "s", x_ld)
            S.wait_fa("vector", fa_rt)
            S.emit("vector", lambda e, dout=dout, bank=bank: e.tensor_tensor(
                rt32[:, dout * TB:(dout + 1) * TB], pp[bank][:], rt_f[:], ALU.add),
                inc="v")
            v_r = S.val("v")
            fa_pp[bank] = {"v": v_r}
            fa_rtf = {"v": v_r}
            fa_wob[wb] = {"p": p_wo}
        S.wait("vector", "v")
        S.emit("vector", lambda e: e.tensor_copy(fl_sb[:], AR2[:, 0:8]), inc="v")
        v_rall = S.val("v")
        S.wait("sync", "v", v_rall)
        for dout in range(16):
            S.dma("sync", resid_scr[dout * 128:(dout + 1) * 128, tcol:tcol + TB],
                  rt32[:, dout * TB:(dout + 1) * TB], "rs")
        fa_rt = {"rs": S.val("rs")}
        S.wait("scalar", "v", v_rall)
        S.wait_fa("scalar", fa_sq)
        for c in range(16):
            S.emit("scalar", lambda e, c=c: e.activation(
                AR3[:, c * TB:(c + 1) * TB], rt32[:, c * TB:(c + 1) * TB],
                AF.Square), inc=("a" if c == 15 else None))
        a_sq = S.val("a")
        S.wait("scalar", "v", v_rall)
        for c in range(16):
            S.emit("scalar", lambda e, c=c: e.activation(
                AR3[:, (16 + c) * TB:(17 + c) * TB], rt32[:, c * TB:(c + 1) * TB],
                AF.Identity), inc=("a" if c == 15 else None))
        a_cp = S.val("a")
        S.wait("tensor", "a", a_cp)
        S.wait_fa("tensor", fa_stp[0])
        for c in range(16):
            S.emit("tensor", lambda e, c=c: e.matmul(
                stp[0][:], ones_sb[:], AR3[:, (16 + c) * TB:(17 + c) * TB],
                start=(c == 0), stop=(c == 15)), inc=("p" if c == 15 else None))
        S.wait("tensor", "a", a_sq)
        S.wait_fa("tensor", fa_stp[1])
        for c in range(16):
            S.emit("tensor", lambda e, c=c: e.matmul(
                stp[1][:], ones_sb[:], AR3[:, c * TB:(c + 1) * TB],
                start=(c == 0), stop=(c == 15)), inc=("p" if c == 15 else None))
        ln_stats(1.0 / D)
        S.wait("vector", "p")
        for c in range(16):
            S.emit("vector", lambda e, c=c: e.tensor_tensor(
                AR3[:, c * TB:(c + 1) * TB], rt32[:, c * TB:(c + 1) * TB],
                bc[0][:], ALU.subtract))
            S.emit("vector", lambda e, c=c: e.tensor_tensor(
                AR3[:, c * TB:(c + 1) * TB], AR3[:, c * TB:(c + 1) * TB],
                bc[1][:], ALU.mult), inc=("v" if c == 15 else None))
        S.wait("vector", "v")
        S.emit("vector", lambda e: e.tensor_copy(fl_sb[:], AR3[:, 0:8]), inc="v")
        v_xln = S.val("v")
        fa_bc[0] = {"v": v_xln}
        fa_bc[1] = {"v": v_xln}
        fa_rt["v"] = v_xln
        S.wait("sync", "v", v_xln)
        for c in range(16):
            base = 128 * c
            if c % 2 == 0:
                ev0, od0 = base, base + 128
            else:
                ev0, od0 = base - 64, base + 64
            S.dma("sync", xeo_scr[ev0:ev0 + 64, tcol:tcol + TB],
                  AR3[0:128:2, c * TB:(c + 1) * TB], "rs")
            S.dma("sync", xeo_scr[od0:od0 + 64, tcol:tcol + TB],
                  AR3[1:128:2, c * TB:(c + 1) * TB], "rs")
        fa_sq = {"rs": S.val("rs")}
    ph1_done_rs = S.val("rs")

    def decode_tile(src_i16, stile, out_lo, out_hi):
        S.emit("vector", lambda e: e.tensor_single_scalar(cmi[:], src_i16, 7, ALU.bitwise_and))
        S.emit("vector", lambda e: e.tensor_copy(cmf[:], cmi[:]))
        S.emit("vector", lambda e: e.tensor_scalar(t1s[:], cmf[:], 2.0, -8.0, ALU.mult, ALU.add))
        S.emit("vector", lambda e: e.scalar_tensor_tensor(t2s[:], cmf[:], -2.0, t1s[:], ALU.add, ALU.max))
        S.emit("vector", lambda e: e.scalar_tensor_tensor(t1s[:], cmf[:], 0.5, t2s[:], ALU.mult, ALU.max))
        S.emit("vector", lambda e, o=out_lo: e.tensor_tensor(o, t1s[:], stile, ALU.mult))
        S.emit("vector", lambda e: e.tensor_scalar(sgn[:], src_i16, 8, 12, ALU.bitwise_and, ALU.logical_shift_left))
        S.emit("vector", lambda e, o=out_lo: e.tensor_tensor(o.bitcast(I16), o.bitcast(I16), sgn[:], ALU.bitwise_or))
        S.emit("vector", lambda e: e.tensor_scalar(cmi[:], src_i16, 4, 7, ALU.logical_shift_right, ALU.bitwise_and))
        S.emit("vector", lambda e: e.tensor_copy(cmf[:], cmi[:]))
        S.emit("vector", lambda e: e.tensor_scalar(t1s[:], cmf[:], 2.0, -8.0, ALU.mult, ALU.add))
        S.emit("vector", lambda e: e.scalar_tensor_tensor(t2s[:], cmf[:], -2.0, t1s[:], ALU.add, ALU.max))
        S.emit("vector", lambda e: e.scalar_tensor_tensor(t1s[:], cmf[:], 0.5, t2s[:], ALU.mult, ALU.max))
        S.emit("vector", lambda e, o=out_hi: e.tensor_tensor(o, t1s[:], stile, ALU.mult))
        S.emit("vector", lambda e: e.tensor_scalar(sgn[:], src_i16, 128, 8, ALU.bitwise_and, ALU.logical_shift_left))
        S.emit("vector", lambda e, o=out_hi: e.tensor_tensor(o.bitcast(I16), o.bitcast(I16), sgn[:], ALU.bitwise_or),
               inc="v")

    S.phase = "B_gateup"
    pkb_d = [G["S_sb"].bitcast(I16), G["P_sb"].bitcast(I16)]
    ce_sb_f = G["ce_sb"]
    stl_d = [ce_sb_f[:, 0:512], ce_sb_f[:, 512:1024]]
    wdlo_t, wdhi_t = G["tmp_sb"][0], G["tmp_sb"][1]
    fa_pkd = [dict(), dict()]
    fa_stld = [dict(), dict()]
    fa_dec[0] = {"rs": ph1_done_rs}

    def emit_wd_group(idx):
        jd, hb = divmod(idx, 4)
        sbuf_i = idx % 2
        S.wait_fa("gpsimd", fa_pkd[sbuf_i])
        S.dma("gpsimd", pkb_d[sbuf_i][:],
              pk_d[jd * 128:(jd + 1) * 128, hb * TB:(hb + 1) * TB], "g")
        S.wait_fa("gpsimd", fa_stld[sbuf_i])
        for r in range(8):
            S.dma("gpsimd", stl_d[sbuf_i][16 * r:16 * (r + 1), :],
                  sc_d[8 * jd + r: 8 * jd + r + 1, hb * TB:(hb + 1) * TB]
                  .broadcast_to([16, TB]), "g")
        g_now = S.val("g")
        S.wait("vector", "g", g_now)
        S.wait_fa("vector", fa_dec[0])
        decode_tile(pkb_d[sbuf_i][:], stl_d[sbuf_i][:], wdlo_t[:], wdhi_t[:])
        v_dec = S.val("v")
        fa_pkd[sbuf_i] = {"v": v_dec}
        fa_stld[sbuf_i] = {"v": v_dec}
        S.wait("vector", "v", v_dec)
        S.emit("vector", lambda e: e.tensor_copy(fl_sb[:], wdlo_t[:, 0:8]), inc="v")
        v_dec = S.val("v")
        S.wait("sync", "v", v_dec)
        for k in range(4):
            hc = 4 * hb + k
            S.dma("sync", wd_scr[hc, 256 * jd: 256 * jd + 256: 2, :],
                  wdlo_t[:, k * 128:(k + 1) * 128], "ws")
            S.dma("sync", wd_scr[hc, 256 * jd + 1: 256 * jd + 256: 2, :],
                  wdhi_t[:, k * 128:(k + 1) * 128], "ws")
        fa_dec[0] = {"ws": S.val("ws")}

    S.wait("sync", "rs", ph1_done_rs)
    S.wait_fa("sync", fa_ar1)
    S.wait("sync", "p")
    for c in range(16):
        S.dma("sync", AR1[:, c * TT:(c + 1) * TT],
              xeo_scr[c * 128:(c + 1) * 128, :], "s")
    xeo_loaded = S.val("s")
    S.wait("tensor", "s", xeo_loaded)

    wdec_g = [AR2[:, 0:8192], AR3[:, 0:8192]]
    wdec_u = [AR2[:, 8192:16384], AR3[:, 8192:16384]]
    fa_w = [dict(), {"rs": ph1_done_rs}]

    def emit_gu_chunk(nb, ci, war_first):
        mi, jc = divmod(ci, 8)
        pk, scm = ((pk_g, sc_g), (pk_u, sc_u))[mi]
        wd = (wdec_g if mi == 0 else wdec_u)[nb % 2]
        sbuf_i = jc % 2
        S.wait_fa("gpsimd", fa_pkb[sbuf_i])
        S.dma("gpsimd", (pkb if sbuf_i == 0 else pkb2)
              .rearrange("p (s w) -> p s w", w=128),
              pk[4 * nb:4 * nb + 4, 128 * jc:128 * (jc + 1), :]
              .rearrange("s p w -> p s w"), "g")
        S.wait_fa("gpsimd", fa_stl[sbuf_i])
        for r in range(8):
            S.dma("gpsimd", stl[sbuf_i][16 * r:16 * (r + 1), :],
                  scm[8 * jc + r: 8 * jc + r + 1,
                      TB * nb: TB * (nb + 1)].broadcast_to([16, TB]), "g")
        g_now = S.val("g")
        S.wait("vector", "g", g_now)
        if war_first:
            S.wait_fa("vector", fa_w[nb % 2])
            S.wait("vector", "p", fa_w[nb % 2].get("p", 0))
        decode_tile((pkb if sbuf_i == 0 else pkb2)[:],
                    stl[sbuf_i][:],
                    wd[:, (2 * jc) * TB:(2 * jc + 1) * TB],
                    wd[:, (2 * jc + 1) * TB:(2 * jc + 2) * TB])
        v_dec = S.val("v")
        fa_pkb[sbuf_i] = {"v": v_dec}
        fa_stl[sbuf_i] = {"v": v_dec}

    for ci in range(16):
        emit_gu_chunk(0, ci, ci == 0)
    v_mark = S.val("v")
    wd_idx = 0

    for nb in range(32):
        buf = nb % 2
        S.wait("tensor", "v", v_mark)
        gu_ci = 0
        for ns in range(4):
            for tb4 in range(4):
                unit = ns * 4 + tb4
                tcol = tb4 * TB
                nrow = TB * nb + 128 * ns
                S.wait_fa("tensor", fa_pp[0])
                for kc in range(16):
                    S.emit("tensor", lambda e, kc=kc, ns=ns, buf=buf, tcol=tcol: e.matmul(
                        pp[0][:],
                        wdec_g[buf][:, kc * TB + ns * 128: kc * TB + ns * 128 + 128],
                        AR1[:, kc * TT + tcol: kc * TT + tcol + TB],
                        start=(kc == 0), stop=(kc == 15)),
                        inc=("p" if kc == 15 else None))
                p_g = S.val("p")
                S.wait_fa("tensor", fa_pp[1])
                for kc in range(16):
                    S.emit("tensor", lambda e, kc=kc, ns=ns, buf=buf, tcol=tcol: e.matmul(
                        pp[1][:],
                        wdec_u[buf][:, kc * TB + ns * 128: kc * TB + ns * 128 + 128],
                        AR1[:, kc * TT + tcol: kc * TT + tcol + TB],
                        start=(kc == 0), stop=(kc == 15)),
                        inc=("p" if kc == 15 else None))
                p_u = S.val("p")
                S.wait("scalar", "p", p_g)
                S.wait_fa("scalar", fa_chain)
                S.emit("scalar", lambda e: e.activation(gsb[:], pp[0][:], AF.Identity), inc="a")
                a_g = S.val("a")
                fa_pp[0] = {"a": a_g}
                S.emit("scalar", lambda e: e.activation(g2b[:], gsb[:], AF.Square), inc="a")
                a_g2 = S.val("a")
                S.wait("vector", "a", a_g2)
                S.emit("vector", lambda e: e.tensor_scalar(
                    csb[:], g2b[:], GELU_B, GELU_A, ALU.mult, ALU.add))
                S.emit("vector", lambda e: e.tensor_tensor(asb[:], csb[:], gsb[:], ALU.mult),
                       inc="v")
                v_arg = S.val("v")
                S.wait("scalar", "v", v_arg)
                S.emit("scalar", lambda e: e.activation(ssb[:], asb[:], AF.Sigmoid), inc="a")
                a_s = S.val("a")
                S.wait("vector", "a", a_s)
                S.emit("vector", lambda e: e.tensor_tensor(hsb[:], ssb[:], gsb[:], ALU.mult))
                S.wait("vector", "p", p_u)
                S.wait_fa("vector", fa_isb)
                S.emit("vector", lambda e: e.tensor_tensor(isb[:], hsb[:], pp[1][:], ALU.mult),
                       inc="v")
                v_i = S.val("v")
                fa_pp[1] = {"v": v_i}
                fa_chain = {"v": v_i}
                S.wait("vector", "v", v_i)
                S.emit("vector", lambda e: e.tensor_copy(fl_sb[:], isb[:, 0:8]), inc="v")
                v_i2 = S.val("v")
                S.wait("sync", "v", v_i2)
                S.dma("sync", inter_scr[nrow:nrow + 128, tcol:tcol + TB], isb[:], "is")
                fa_chain["is"] = S.val("is")
                fa_isb = {"is": S.val("is")}
                if nb < 31 and gu_ci < 16:
                    emit_gu_chunk(nb + 1, gu_ci, gu_ci == 0)
                    gu_ci += 1
                if unit % 2 == 1 and wd_idx < 256:
                    emit_wd_group(wd_idx)
                    wd_idx += 1
        fa_w[buf] = {"p": S.val("p")}
        v_mark = S.val("v")

    is_all = S.val("is")
    ws_all = S.val("ws")

    S.phase = "B_down"
    def stripe_tile(nck):
        if nck < 64:
            return AR1[:, nck * TB:(nck + 1) * TB]
        if nck < 96:
            return AR2[:, (nck - 64) * TB:(nck - 63) * TB]
        return AR3[:, (nck - 96) * TB:(nck - 95) * TB]

    for eng in ("sync", "gpsimd", "tensor", "vector", "scalar"):
        S.wait(eng, "is", is_all)
        S.wait(eng, "ws", ws_all)
        S.wait(eng, "p")
        S.wait(eng, "v")
        S.wait(eng, "a")
        S.wait(eng, "rs")
    fa_stripe = dict()
    for tb4 in range(4):
        tcol = tb4 * TB
        S.wait_fa("sync", fa_stripe)
        for nck in range(128):
            S.dma("sync", stripe_tile(nck),
                  inter_scr[nck * 128:(nck + 1) * 128, tcol:tcol + TB], "s")
        stripe_loaded = S.val("s")
        S.wait("tensor", "s", stripe_loaded)
        for hc in range(16):
            bank = hc % 2
            S.wait_fa("tensor", fa_pp[bank])
            for g32 in range(4):
                wbuf = g32 % 2
                S.wait_fa("sync", fa_wdt[wbuf])
                S.dma("sync", wdt[wbuf].rearrange("p (a f) -> p a f", f=128),
                      wd_scr[hc, g32 * 4096:(g32 + 1) * 4096, :]
                      .rearrange("(a p) f -> p a f", p=128), "wl")
                wl_now = S.val("wl")
                S.wait("tensor", "wl", wl_now)
                for a32 in range(32):
                    nck = 32 * g32 + a32
                    S.emit("tensor", lambda e, a32=a32, nck=nck, wbuf=wbuf, bank=bank: e.matmul(
                        pp[bank][:], wdt[wbuf][:, a32 * 128:(a32 + 1) * 128],
                        stripe_tile(nck),
                        start=(nck == 0), stop=(nck == 127)),
                        inc=("p" if a32 == 31 else None))
                fa_wdt[wbuf] = {"p": S.val("p")}
            p_dn = S.val("p")
            S.wait_fa("sync", fa_rtf)
            S.dma("sync", rt_f[:], resid_scr[hc * 128:(hc + 1) * 128, tcol:tcol + TB], "s")
            r_ld = S.val("s")
            S.wait("vector", "p", p_dn)
            S.wait("vector", "s", r_ld)
            S.wait_fa("vector", fa_ytf)
            S.emit("vector", lambda e, bank=bank: e.tensor_tensor(
                yt_f[:], pp[bank][:], rt_f[:], ALU.add), inc="v")
            v_y = S.val("v")
            fa_pp[bank] = {"v": v_y}
            fa_rtf = {"v": v_y}
            S.wait("sync", "v", v_y)
            S.dma("sync", y2_my[hc * 128:(hc + 1) * 128, tcol:tcol + TB],
                  yt_f[:], "ys")
            fa_ytf = {"ys": S.val("ys")}
        fa_stripe = {"p": S.val("p")}

    S.wait("sync", "ys", S.val("ys"))
    S.wait("gpsimd", "ys", S.val("ys"))


def token_cols(i):
    return np.concatenate([np.arange(b * SL + 512 * i, b * SL + 512 * (i + 1))
                           for b in range(B)])


def stage_B1(inputs, x_fm):
    wo = np.asarray(inputs["L0_attn_W_o"], np.float32)
    woT_b = np.ascontiguousarray(wo.T.reshape(D, 16, 128).transpose(1, 0, 2))

    def unpack16(a32):
        return np.ascontiguousarray(np.asarray(a32, np.int32).view(np.int16)[:, ::2])

    g16 = unpack16(inputs["L0_ffn_gate_W_packed"])
    u16 = unpack16(inputs["L0_ffn_up_W_packed"])
    d16 = unpack16(inputs["L0_ffn_down_W_packed"])
    pk_g = np.ascontiguousarray(g16.T.reshape(1024, 128, 128).transpose(1, 0, 2))
    pk_u = np.ascontiguousarray(u16.T.reshape(1024, 128, 128).transpose(1, 0, 2))
    pk_d = np.ascontiguousarray(d16.T)
    sc_g = np.ascontiguousarray(np.asarray(inputs["L0_ffn_gate_scales"], np.float32).T)
    sc_u = np.ascontiguousarray(np.asarray(inputs["L0_ffn_up_scales"], np.float32).T)
    sc_d = np.ascontiguousarray(np.asarray(inputs["L0_ffn_down_scales"], np.float32).T)

    in_maps = []
    for i in range(NCORES):
        cols = token_cols(i)
        in_maps.append(dict(
            x_my=np.ascontiguousarray(x_fm[:, cols]),
            woT_b=woT_b, pk_g=pk_g, pk_u=pk_u, pk_d=pk_d,
            sc_g=sc_g, sc_u=sc_u, sc_d=sc_d,
        ))
    return in_maps



def emit_CD(nc, S, G):
    y2_my = G["y2_scr"]
    wqkvT_b = G["wqkvT_b"]; woT_b = G["wo1T_b"]; rw_i = G["rw_i"]
    wg_b = G["wg_b"]; wu_b = G["wu_b"]; wd_b = G["wd_b"]
    swg_b = G["swg_b"]; swu_b = G["swu_b"]; swd_b = G["swd_b"]
    e2_i = G["e2_i"]; idb_i = G["idb_i"]; idf_i = G["idf_i"]; alib_i = G["alib_i"]
    out_my = G["out_my"]
    y3_scr = G["y3_scr"]; attn_scr = G["attn_scr"]; p_scr = G["p_scr"]
    cb_scr = G["cb_scr"]; ic_scr = G["ic_scr"]; ish_scr = G["ish_scr"]

    ARX = G["ARX"]; ARY = G["ARY"]
    ARZ = G["ARZW"][:, 0:8192]
    ARW = G["ARZW"][:, 8192:16384]
    WU = G["WU"]
    wqb = [WU[:, 0:2048], WU[:, 2048:4096]]
    wgu = [WU[:, 4096:8192], WU[:, 8192:12288]]
    wdn = [WU[:, 0:10240], WU[:, 10240:20480]]
    S_sb = G["S_sb"]; sT_sb = G["sT_sb"]; P_sb = G["P_sb"]
    tmp_sb = G["tmp_sb"]; pb_sb = G["pb_sb"]
    acc_sb = G["acc_sb"]; tmp2_sb = G["tmp2_sb"]
    rt_f = G["rt_f"]; yt_f = G["yt_f"]
    fl_sb = G["fl_sb"]; ones_sb = G["ones_sb"]; onesr_sb = G["onesr_sb"]
    e2_sb = G["e2_sb"]; idb_sb = G["idb_sb"]; idf_sb = G["idf_sb"]
    alib_sb = G["alib_sb"]; rw_sb = G["rw_sb"]
    nmx = G["nmx"]; rec1 = G["rec1"]
    ro_sb = G["ro_sb"]; e_sb = G["e_sb"]
    m1_sb = G["m1_sb"]; m2_sb = G["m2_sb"]; den_sb = G["den_sb"]
    eq_sb = G["eq_sb"]; me_sb = G["me_sb"]; en_sb = G["en_sb"]
    comb_sb = G["comb_sb"]; combT_sb = G["combT_sb"]
    ce_sb = G["ce_sb"]; sg_sb = G["sg_sb"]; ic_sb = G["ic_sb"]
    st_f = G["st_f"]; st_b = G["st_b"]
    sq_sb = G["sq_sb"]; xb_sb = G["xb_sb"]
    pp = G["pp"]; bc = G["bc"]; stp = G["stp"]
    ptb = G["ptb"]; ptf = G["ptf"]
    fa_pp = [dict(), dict()]
    fa_bc = [dict(), dict()]
    fa_stp = [dict(), dict()]
    fa_stf = dict()
    fa_rtf = dict()
    fa_ytf = dict()
    fa_wqb = [dict(), dict()]
    fa_wgu = [dict(), dict()]
    fa_wdn = [dict(), dict()]
    fa_sqxb = dict()
    fa_tmp = [dict(), dict()]
    fa_Ssb = dict()
    fa_sT = dict()
    fa_Psb = dict()
    fa_pb = dict()
    fa_acc = dict()
    fa_icsb = dict()
    fa_sg = dict()
    fa_ptb = dict()
    fa_ptf = dict()
    fa_arx = dict()
    fa_ary = dict()

    S.dma("sync", e2_sb[:], e2_i, "s")
    S.dma("sync", idb_sb[:], idb_i, "s")
    S.dma("sync", idf_sb[:], idf_i, "s")
    S.dma("sync", alib_sb[:], alib_i, "s")
    S.dma("gpsimd", rw_sb.rearrange("p (a f) -> p a f", f=8),
          rw_i.rearrange("(a p) f -> p a f", p=128), "g")
    w_s = S.val("s")
    S.wait("tensor", "s", w_s)
    S.wait("scalar", "s", w_s)
    S.wait("vector", "s", w_s)

    def ln_stats(n_inv):
        S.wait("vector", "p")
        S.wait_fa("vector", fa_stf)
        S.emit("vector", lambda e: e.tensor_scalar_mul(st_f[0][:], stp[0][:], n_inv))
        S.emit("vector", lambda e: e.tensor_scalar_mul(st_f[1][:], stp[1][:], n_inv))
        S.emit("vector", lambda e: e.tensor_tensor(st_f[2][:], st_f[0][:], st_f[0][:], ALU.mult),
               inc="v")
        S.wait("vector", "v")
        S.emit("vector", lambda e: e.tensor_tensor(st_f[1][:], st_f[1][:], st_f[2][:], ALU.subtract))
        S.emit("vector", lambda e: e.tensor_scalar_add(st_f[1][:], st_f[1][:], EPS),
               inc="v")
        fa_stp[0]["v"] = S.val("v")
        fa_stp[1]["v"] = S.val("v")
        S.wait("scalar", "v")
        S.emit("scalar", lambda e: e.activation(st_f[1][:], st_f[1][:], AF.Ln))
        S.emit("scalar", lambda e: e.activation(st_f[1][:], st_f[1][:], AF.Exp, scale=-0.5),
               inc="a")
        S.wait("vector", "a")
        S.emit("vector", lambda e: e.tensor_copy(st_b[0][:], st_f[0][:]))
        S.emit("vector", lambda e: e.tensor_copy(st_b[1][:], st_f[1][:]), inc="v")
        S.wait("tensor", "v")
        S.wait_fa("tensor", fa_bc[0])
        S.wait_fa("tensor", fa_bc[1])
        S.emit("tensor", lambda e: e.matmul(bc[0][:], onesr_sb[:], st_b[0][:],
                                            start=True, stop=True))
        S.emit("tensor", lambda e: e.matmul(bc[1][:], onesr_sb[:], st_b[1][:],
                                            start=True, stop=True), inc="p")
        fa_stf["p"] = S.val("p")

    def ln_to_arx(src_dram, fa_dst_extra=None):
        nonlocal fa_sqxb
        for tb4 in range(4):
            tcol = tb4 * TB
            for c in range(16):
                S.wait_fa("gpsimd", fa_sqxb)
                S.dma("gpsimd", xb_sb[:], src_dram[c * 128:(c + 1) * 128, tcol:tcol + TB], "g")
                g_x = S.val("g")
                S.wait("scalar", "g", g_x)
                S.wait("scalar", "v")
                S.emit("scalar", lambda e: e.activation(sq_sb[:], xb_sb[:], AF.Square),
                       inc="a")
                a_sq = S.val("a")
                S.wait("tensor", "g", g_x)
                if c == 0:
                    S.wait_fa("tensor", fa_stp[0])
                    S.wait_fa("tensor", fa_stp[1])
                S.emit("tensor", lambda e, c=c: e.matmul(
                    stp[0][:], ones_sb[:], xb_sb[:],
                    start=(c == 0), stop=(c == 15)),
                    inc=("p" if c == 15 else None))
                S.wait("tensor", "a", a_sq)
                S.emit("tensor", lambda e, c=c: e.matmul(
                    stp[1][:], ones_sb[:], sq_sb[:],
                    start=(c == 0), stop=(c == 15)),
                    inc=("p" if c == 15 else None))
                fa_sqxb = {"p": S.val("p")}
            ln_stats(1.0 / D)
            for c in range(16):
                S.wait_fa("gpsimd", fa_sqxb)
                S.dma("gpsimd", xb_sb[:], src_dram[c * 128:(c + 1) * 128, tcol:tcol + TB], "g")
                g_x = S.val("g")
                S.wait("vector", "g", g_x)
                S.wait("vector", "p")
                S.emit("vector", lambda e: e.tensor_tensor(sq_sb[:], xb_sb[:], bc[0][:],
                                                           ALU.subtract), inc="v")
                S.wait("vector", "v")
                S.emit("vector", lambda e, c=c, tcol=tcol: e.tensor_tensor(
                    ARX[:, c * 2048 + tcol: c * 2048 + tcol + TB],
                    sq_sb[:], bc[1][:], ALU.mult), inc="v")
                fa_sqxb = {"v": S.val("v")}
            v_ln = S.val("v")
            fa_bc[0] = {"v": v_ln}
            fa_bc[1] = {"v": v_ln}
        return S.val("v")


    S.phase = "C_ln"
    v_xln = ln_to_arx(y2_my)
    S.wait("tensor", "v", v_xln)

    S.phase = "C_attn"
    pairs10 = [(a, b) for a in range(4) for b in range(a + 1)]
    fa_ary = dict()
    for stk in range(2):
        ocs = list(range(stk * 8, stk * 8 + 8)) + \
              [16 + 2 * g for g in (2 * stk, 2 * stk + 1)] + \
              [17 + 2 * g for g in (2 * stk, 2 * stk + 1)]
        S.wait("gpsimd", "p", fa_ary.get("p", 0))
        S.dma("gpsimd", wqb[0].rearrange("p (a f) -> p a f", f=128),
              wqkvT_b[ocs[0]].rearrange("(a p) f -> p a f", p=128), "g")
        for ii, oc in enumerate(ocs):
            wb = ii % 2
            S.wait("tensor", "g", S.val("g"))
            if ii + 1 < len(ocs):
                S.wait_fa("gpsimd", fa_wqb[(ii + 1) % 2])
                S.dma("gpsimd", wqb[(ii + 1) % 2].rearrange("p (a f) -> p a f", f=128),
                      wqkvT_b[ocs[ii + 1]].rearrange("(a p) f -> p a f", p=128), "g")
            for tb4 in range(4):
                tcol = tb4 * TB
                bank = tb4 % 2
                S.wait_fa("tensor", fa_pp[bank])
                if ii == 0 and tb4 == 0:
                    S.wait_fa("tensor", fa_ary)
                for dc in range(16):
                    S.emit("tensor", lambda e, dc=dc, wb=wb, bank=bank, tcol=tcol: e.matmul(
                        pp[bank][:], wqb[wb][:, dc * 128:(dc + 1) * 128],
                        ARX[:, dc * 2048 + tcol: dc * 2048 + tcol + TB],
                        start=(dc == 0), stop=(dc == 15)),
                        inc=("p" if dc == 15 else None))
                p_q = S.val("p")
                if oc < 16:
                    dst = ARY[:, (oc - stk * 8) * 2048 + tcol:
                              (oc - stk * 8) * 2048 + tcol + TB]
                elif (oc - 16) % 2 == 0:
                    gl = (oc - 16) // 2 - 2 * stk
                    dst = ARZ[:, gl * 2048 + tcol: gl * 2048 + tcol + TB]
                else:
                    gl = (oc - 16) // 2 - 2 * stk
                    dst = ARW[:, gl * 2048 + tcol: gl * 2048 + tcol + TB]
                S.wait("scalar", "p", p_q)
                S.wait("scalar", "v", fa_ary.get("v", 0))
                S.emit("scalar", lambda e, dst=dst, bank=bank: e.activation(
                    dst, pp[bank][:], AF.Identity), inc="a")
                fa_pp[bank] = {"a": S.val("a")}
            fa_wqb[wb] = {"p": S.val("p")}
        a_qkv = S.val("a")

        S.wait("tensor", "a", a_qkv)
        S.wait_fa("tensor", fa_pp[0])
        r = 0
        for hh in range(8):
            h = stk * 8 + hh
            gl = hh // 4
            for (aa, bb) in pairs10:
                tb_ = r % 2
                S.wait("vector", "a", a_qkv)
                S.wait_fa("vector", fa_tmp[tb_])
                S.emit("vector", lambda e, hh=hh, gl=gl, aa=aa, bb=bb, tb_=tb_: e.tensor_tensor(
                    tmp_sb[tb_][:],
                    ARY[:, hh * 2048 + aa * TB: hh * 2048 + (aa + 1) * TB],
                    ARZ[:, gl * 2048 + bb * TB: gl * 2048 + (bb + 1) * TB],
                    ALU.mult), inc="v")
                v_t = S.val("v")
                S.wait("tensor", "v", v_t)
                S.emit("tensor", lambda e, r=r, tb_=tb_: e.matmul(
                    pp[0][:], e2_sb[:, 128 - r:256 - r], tmp_sb[tb_][:],
                    start=(r == 0), stop=(r == 79)),
                    inc="p")
                fa_tmp[tb_] = {"p": S.val("p")}
                r += 1
        p_sc = S.val("p")
        S.wait("scalar", "p", p_sc)
        S.wait_fa("scalar", fa_Ssb)
        S.emit("scalar", lambda e, stk=stk: e.activation(
            S_sb[:], pp[0][:], AF.Identity,
            bias=alib_sb[:, stk:stk + 1], scale=ISQ), inc="a")
        a_S = S.val("a")
        fa_pp[0] = {"a": a_S}
        offs = [0, 1, 3, 6]
        for tsub in range(4):
            S.wait("tensor", "a", a_S)
            S.wait_fa("tensor", fa_ptb)
            S.emit("tensor", lambda e, tsub=tsub: e.transpose(
                ptb[:], S_sb[:, tsub * 128:(tsub + 1) * 128], idb_sb[:]), inc="p")
            p_t = S.val("p")
            S.wait("scalar", "p", p_t)
            S.wait_fa("scalar", fa_sT)
            S.emit("scalar", lambda e: e.activation(sT_sb[:, 0:128], ptb[:], AF.Identity),
                   inc="a")
            a_sT = S.val("a")
            fa_ptb = {"a": a_sT}
            S.wait("vector", "a", a_sT)
            for grp in range(4):
                base = grp * 8
                for k in range(8):
                    idx = base + k
                    hh2, aa2 = idx // 4, idx % 4
                    c2 = hh2 * 10 + offs[aa2]
                    w2 = aa2 + 1
                    S.emit("vector", lambda e, c2=c2, w2=w2, k=k: e.reduce_max(
                        nmx[:, k:k + 1], sT_sb[:, c2:c2 + w2], axis=AX.X, negate=True),
                        inc=("v" if k == 7 else None))
                S.wait("scalar", "v")
                for k in range(8):
                    idx = base + k
                    hh2, aa2 = idx // 4, idx % 4
                    c2 = hh2 * 10 + offs[aa2]
                    w2 = aa2 + 1
                    S.emit("scalar", lambda e, c2=c2, w2=w2, k=k: e.activation(
                        sT_sb[:, c2:c2 + w2], sT_sb[:, c2:c2 + w2], AF.Exp,
                        bias=nmx[:, k:k + 1]),
                        inc=("a" if k == 7 else None))
                S.wait("vector", "a")
                for k in range(8):
                    idx = base + k
                    hh2, aa2 = idx // 4, idx % 4
                    c2 = hh2 * 10 + offs[aa2]
                    w2 = aa2 + 1
                    S.emit("vector", lambda e, c2=c2, w2=w2, k=k: e.reduce_sum(
                        rec1[:, k:k + 1], sT_sb[:, c2:c2 + w2], axis=AX.X),
                        inc=("v" if k == 7 else None))
                S.wait("vector", "v")
                S.emit("vector", lambda e: e.reciprocal(rec1[:], rec1[:]), inc="v")
                S.wait("vector", "v")
                for k in range(8):
                    idx = base + k
                    hh2, aa2 = idx // 4, idx % 4
                    c2 = hh2 * 10 + offs[aa2]
                    w2 = aa2 + 1
                    S.emit("vector", lambda e, c2=c2, w2=w2, k=k: e.tensor_single_scalar(
                        sT_sb[:, c2:c2 + w2], sT_sb[:, c2:c2 + w2],
                        rec1[:, k:k + 1], ALU.mult),
                        inc=("v" if k == 7 else None))
            v_sm = S.val("v")
            S.wait("vector", "v", v_sm)
            S.emit("vector", lambda e: e.tensor_copy(fl_sb[:], sT_sb[:, 0:8]), inc="v")
            v_fl = S.val("v")
            S.wait("tensor", "v", v_fl)
            S.wait_fa("tensor", fa_ptb)
            S.emit("tensor", lambda e: e.transpose(ptb[:], sT_sb[:, 0:128], idb_sb[:]),
                   inc="p")
            p_t2 = S.val("p")
            S.wait("scalar", "p", p_t2)
            S.wait_fa("scalar", fa_Psb)
            S.emit("scalar", lambda e, tsub=tsub: e.activation(
                P_sb[:, tsub * 128:(tsub + 1) * 128], ptb[:], AF.Identity), inc="a")
            a_P = S.val("a")
            fa_ptb = {"a": a_P}
            fa_sT = {"a": a_P}
        a_Pall = S.val("a")
        S.wait("scalar", "a", a_Pall)
        S.emit("scalar", lambda e: e.activation(fl_sb[:], P_sb[:, 0:8], AF.Identity),
               inc="a")
        a_flP = S.val("a")
        S.wait("sync", "a", a_flP)
        S.dma("sync", p_scr[stk], P_sb[:], "ps")
        ps_now = S.val("ps")
        fa_Psb = {"ps": ps_now}
        fa_Ssb = {"p": S.val("p"), "a": a_Pall}
        S.wait("sync", "ps", ps_now)
        at_before = S.val("at")
        for hh in range(8):
            h = stk * 8 + hh
            gl = hh // 4
            for aa in range(4):
                for bb in range(aa + 1):
                    rr = hh * 10 + offs[aa] + bb
                    S.wait_fa("sync", fa_pb)
                    S.dma("sync", pb_sb[:],
                          p_scr[stk, rr:rr + 1, :].broadcast_to([128, TB]), "pb")
                    pb_now = S.val("pb")
                    S.wait("vector", "pb", pb_now)
                    if bb == 0:
                        S.wait_fa("vector", fa_acc)
                    S.emit("vector", lambda e, gl=gl, bb=bb: e.tensor_tensor(
                        tmp2_sb[:], pb_sb[:],
                        ARW[:, gl * 2048 + bb * TB: gl * 2048 + (bb + 1) * TB],
                        ALU.mult), inc="v")
                    S.wait("vector", "v")
                    if bb == 0:
                        S.emit("vector", lambda e: e.tensor_copy(acc_sb[:], tmp2_sb[:]),
                               inc="v")
                    else:
                        S.emit("vector", lambda e: e.tensor_tensor(
                            acc_sb[:], acc_sb[:], tmp2_sb[:], ALU.add), inc="v")
                    fa_pb = {"v": S.val("v")}
                S.wait("vector", "v")
                S.emit("vector", lambda e: e.tensor_copy(ic_sb[:], acc_sb[:]), inc="v")
                S.wait("vector", "v")
                S.emit("vector", lambda e: e.tensor_copy(fl_sb[:], ic_sb[:, 0:8]),
                       inc="v")
                v_at = S.val("v")
                fa_acc = {"v": v_at}
                S.wait("sync", "v", v_at)
                S.wait_fa("sync", fa_icsb)
                S.dma("sync", attn_scr[h * 128:(h + 1) * 128, aa * TB:(aa + 1) * TB],
                      ic_sb[:], "at")
                fa_icsb = {"at": S.val("at")}
        fa_ary = {"v": S.val("v"), "p": S.val("p")}
    at_all = S.val("at")

    S.phase = "C_wo"
    S.wait("sync", "at", at_all)
    S.wait("sync", "p")
    S.wait("sync", "v")
    S.wait("sync", "a")
    for c in range(16):
        S.dma("sync", ARX[:, c * 2048:(c + 1) * 2048],
              attn_scr[c * 128:(c + 1) * 128, :], "s")
    arx_ld = S.val("s")
    S.wait("tensor", "s", arx_ld)
    S.dma("gpsimd", wqb[0].rearrange("p (a f) -> p a f", f=128),
          woT_b[0].rearrange("(a p) f -> p a f", p=128), "g")
    for oc in range(16):
        wb = oc % 2
        S.wait("tensor", "g", S.val("g"))
        if oc < 15:
            S.wait_fa("gpsimd", fa_wqb[(oc + 1) % 2])
            S.dma("gpsimd", wqb[(oc + 1) % 2].rearrange("p (a f) -> p a f", f=128),
                  woT_b[oc + 1].rearrange("(a p) f -> p a f", p=128), "g")
        for tb4 in range(4):
            tcol = tb4 * TB
            bank = tb4 % 2
            S.wait_fa("tensor", fa_pp[bank])
            for dc in range(16):
                S.emit("tensor", lambda e, dc=dc, wb=wb, bank=bank, tcol=tcol: e.matmul(
                    pp[bank][:], wqb[wb][:, dc * 128:(dc + 1) * 128],
                    ARX[:, dc * 2048 + tcol: dc * 2048 + tcol + TB],
                    start=(dc == 0), stop=(dc == 15)),
                    inc=("p" if dc == 15 else None))
            p_wo = S.val("p")
            S.wait_fa("sync", fa_rtf)
            S.dma("sync", rt_f[:], y2_my[oc * 128:(oc + 1) * 128, tcol:tcol + TB], "s")
            r_ld = S.val("s")
            S.wait("vector", "p", p_wo)
            S.wait("vector", "s", r_ld)
            S.wait_fa("vector", fa_ytf)
            S.emit("vector", lambda e, bank=bank: e.tensor_tensor(
                yt_f[:], pp[bank][:], rt_f[:], ALU.add), inc="v")
            S.wait("vector", "v")
            S.emit("vector", lambda e: e.tensor_copy(fl_sb[:], yt_f.bitcast(BF16)[:, 0:8]),
                   inc="v")
            v_y = S.val("v")
            fa_pp[bank] = {"v": v_y}
            fa_rtf = {"v": v_y}
            S.wait("sync", "v", v_y)
            S.dma("sync", y3_scr[oc * 128:(oc + 1) * 128, tcol:tcol + TB], yt_f[:], "y3")
            fa_ytf = {"y3": S.val("y3")}
        fa_wqb[wb] = {"p": S.val("p")}
    y3_all = S.val("y3")

    S.phase = "D_ln"
    for eng in ("sync", "gpsimd", "tensor", "vector", "scalar"):
        S.wait(eng, "y3", y3_all)
        S.wait(eng, "p")
        S.wait(eng, "v")
        S.wait(eng, "a")
    v_xln2 = ln_to_arx(y3_scr)
    S.wait("tensor", "v", v_xln2)

    S.phase = "D_router"
    for tt_ in range(16):
        c_t = (tt_ % 4)
        S.wait_fa("tensor", fa_pp[0] if tt_ % 2 == 0 else fa_pp[1])
        bank = tt_ % 2
        for dc in range(16):
            S.emit("tensor", lambda e, dc=dc, tt_=tt_, bank=bank: e.matmul(
                pp[bank][:, 0:8],
                ARX[:, dc * 2048 + tt_ * 128: dc * 2048 + (tt_ + 1) * 128],
                rw_sb[:, dc * 8:(dc + 1) * 8],
                start=(dc == 0), stop=(dc == 15)),
                inc=("p" if dc == 15 else None))
        p_r = S.val("p")
        S.wait("scalar", "p", p_r)
        S.emit("scalar", lambda e, tt_=tt_, bank=bank: e.activation(
            ro_sb[:, tt_ * 8:(tt_ + 1) * 8], pp[bank][:, 0:8], AF.Identity), inc="a")
        if tt_ % 2 == 0:
            fa_pp[0] = {"a": S.val("a")}
        else:
            fa_pp[1] = {"a": S.val("a")}
    a_ro = S.val("a")
    S.wait("scalar", "a", a_ro)
    S.emit("scalar", lambda e: e.activation(e_sb[:], ro_sb[:], AF.Exp), inc="a")
    a_e = S.val("a")
    S.wait("vector", "a", a_e)
    for t_ in range(16):
        S.emit("vector", lambda e, t_=t_: e.reduce_max(
            m1_sb[:, t_:t_ + 1], e_sb[:, t_ * 8:(t_ + 1) * 8], axis=AX.X),
            inc=("v" if t_ == 15 else None))
    S.wait("vector", "v")
    for t_ in range(16):
        S.emit("vector", lambda e, t_=t_: e.tensor_single_scalar(
            eq_sb[:, t_ * 8:(t_ + 1) * 8], e_sb[:, t_ * 8:(t_ + 1) * 8],
            m1_sb[:, t_:t_ + 1], ALU.is_ge), inc=("v" if t_ == 15 else None))
    S.wait("vector", "v")
    S.emit("vector", lambda e: e.scalar_tensor_tensor(
        me_sb[:], eq_sb[:], -1e30, e_sb[:], ALU.mult, ALU.add), inc="v")
    S.wait("vector", "v")
    for t_ in range(16):
        S.emit("vector", lambda e, t_=t_: e.reduce_max(
            m2_sb[:, t_:t_ + 1], me_sb[:, t_ * 8:(t_ + 1) * 8], axis=AX.X),
            inc=("v" if t_ == 15 else None))
    S.wait("vector", "v")
    for t_ in range(16):
        S.emit("vector", lambda e, t_=t_: e.tensor_single_scalar(
            en_sb[:, t_ * 8:(t_ + 1) * 8], e_sb[:, t_ * 8:(t_ + 1) * 8],
            m2_sb[:, t_:t_ + 1], ALU.is_ge), inc=("v" if t_ == 15 else None))
    S.emit("vector", lambda e: e.tensor_tensor(den_sb[:], m1_sb[:], m2_sb[:], ALU.add),
           inc="v")
    S.wait("vector", "v")
    S.emit("vector", lambda e: e.reciprocal(den_sb[:], den_sb[:]))
    S.emit("vector", lambda e: e.tensor_tensor(en_sb[:], en_sb[:], e_sb[:], ALU.mult),
           inc="v")
    S.wait("vector", "v")
    for t_ in range(16):
        S.emit("vector", lambda e, t_=t_: e.tensor_single_scalar(
            comb_sb[:, t_ * 8:(t_ + 1) * 8], en_sb[:, t_ * 8:(t_ + 1) * 8],
            den_sb[:, t_:t_ + 1], ALU.mult), inc=("v" if t_ == 15 else None))
    S.wait("vector", "v")
    S.emit("vector", lambda e: e.tensor_copy(fl_sb[:], comb_sb.bitcast(BF16)[:, 0:8]),
           inc="v")
    v_comb = S.val("v")
    S.wait("tensor", "v", v_comb)
    S.wait_fa("tensor", fa_ptf)
    S.emit("tensor", lambda e: e.transpose(ptf[:], comb_sb[:], idf_sb[:]), inc="p")
    p_ct = S.val("p")
    S.wait("vector", "p", p_ct)
    S.emit("vector", lambda e: e.tensor_copy(combT_sb[:], ptf[:]), inc="v")
    S.wait("vector", "v")
    S.emit("vector", lambda e: e.tensor_copy(fl_sb[:], combT_sb.bitcast(BF16)[:, 0:8]),
           inc="v")
    v_cT = S.val("v")
    fa_ptf = {"v": v_cT}
    S.wait("sync", "v", v_cT)
    S.dma("sync", cb_scr[:], combT_sb[:], "cb")
    S.wait("sync", "cb")

    S.phase = "D_exp"
    S.dma("gpsimd", wgu[0][:, 0:2048].rearrange("p (a f) -> p a f", f=128),
          wg_b[0, 0].rearrange("(a p) f -> p a f", p=128), "g")
    S.dma("gpsimd", wgu[0][:, 2048:4096].rearrange("p (a f) -> p a f", f=128),
          wu_b[0, 0].rearrange("(a p) f -> p a f", p=128), "g")
    fa_ce = dict()
    cb_done = S.val("cb")
    for ee in range(8):
        S.wait("sync", "cb", cb_done)
        S.wait_fa("sync", fa_ce)
        S.dma("sync", ce_sb.rearrange("p (a t) -> p a t", t=128),
              cb_scr.rearrange("(a e) t -> e a t", e=8)[ee:ee + 1]
              .broadcast_to([128, 16, 128]), "ce")
        ce_now = S.val("ce")
        for hc in range(8):
            wb = (ee * 8 + hc) % 2
            nxt = ee * 8 + hc + 1
            S.wait("tensor", "g", S.val("g"))
            if nxt < 64:
                ne, nh = nxt // 8, nxt % 8
                S.wait_fa("gpsimd", fa_wgu[nxt % 2])
                S.dma("gpsimd", wgu[nxt % 2][:, 0:2048].rearrange("p (a f) -> p a f", f=128),
                      wg_b[ne, nh].rearrange("(a p) f -> p a f", p=128), "g")
                S.dma("gpsimd", wgu[nxt % 2][:, 2048:4096].rearrange("p (a f) -> p a f", f=128),
                      wu_b[ne, nh].rearrange("(a p) f -> p a f", p=128), "g")
            for tb4 in range(4):
                tcol = tb4 * TB
                S.wait_fa("tensor", fa_pp[0])
                for dc in range(16):
                    S.emit("tensor", lambda e, dc=dc, wb=wb, tcol=tcol: e.matmul(
                        pp[0][:], wgu[wb][:, dc * 128:(dc + 1) * 128],
                        ARX[:, dc * 2048 + tcol: dc * 2048 + tcol + TB],
                        start=(dc == 0), stop=(dc == 15)),
                        inc=("p" if dc == 15 else None))
                p_g = S.val("p")
                S.wait_fa("tensor", fa_pp[1])
                for dc in range(16):
                    S.emit("tensor", lambda e, dc=dc, wb=wb, tcol=tcol: e.matmul(
                        pp[1][:], wgu[wb][:, 2048 + dc * 128: 2048 + (dc + 1) * 128],
                        ARX[:, dc * 2048 + tcol: dc * 2048 + tcol + TB],
                        start=(dc == 0), stop=(dc == 15)),
                        inc=("p" if dc == 15 else None))
                p_u = S.val("p")
                S.wait("scalar", "p", p_g)
                S.wait_fa("scalar", fa_sg)
                S.emit("scalar", lambda e: e.activation(sg_sb[:], pp[0][:], AF.Silu),
                       inc="a")
                a_sg = S.val("a")
                fa_pp[0] = {"a": a_sg}
                S.wait("vector", "a", a_sg)
                S.wait("vector", "p", p_u)
                S.wait("vector", "ce", ce_now)
                S.wait_fa("vector", fa_icsb)
                S.emit("vector", lambda e: e.tensor_tensor(tmp2_sb[:], sg_sb[:], pp[1][:],
                                                           ALU.mult), inc="v")
                S.wait("vector", "v")
                S.emit("vector", lambda e, tcol=tcol: e.tensor_tensor(
                    ic_sb[:], tmp2_sb[:], ce_sb[:, tcol:tcol + TB], ALU.mult), inc="v")
                S.wait("vector", "v")
                S.emit("vector", lambda e: e.tensor_copy(fl_sb[:], ic_sb[:, 0:8]), inc="v")
                v_ic = S.val("v")
                fa_pp[1] = {"v": v_ic}
                fa_sg = {"v": v_ic}
                S.wait("sync", "v", v_ic)
                S.dma("sync", ic_scr[ee * 1024 + hc * 128: ee * 1024 + (hc + 1) * 128,
                                     tcol:tcol + TB], ic_sb[:], "ic")
                fa_icsb = {"ic": S.val("ic")}
            fa_wgu[wb] = {"p": S.val("p")}
        fa_ce = {"v": S.val("v")}

    S.phase = "D_shared"
    S.dma("gpsimd", wgu[0][:, 0:2048].rearrange("p (a f) -> p a f", f=128),
          swg_b[0].rearrange("(a p) f -> p a f", p=128), "g")
    S.dma("gpsimd", wgu[0][:, 2048:4096].rearrange("p (a f) -> p a f", f=128),
          swu_b[0].rearrange("(a p) f -> p a f", p=128), "g")
    for oc in range(16):
        wb = oc % 2
        S.wait("tensor", "g", S.val("g"))
        if oc < 15:
            S.wait_fa("gpsimd", fa_wgu[(oc + 1) % 2])
            S.dma("gpsimd", wgu[(oc + 1) % 2][:, 0:2048].rearrange("p (a f) -> p a f", f=128),
                  swg_b[oc + 1].rearrange("(a p) f -> p a f", p=128), "g")
            S.dma("gpsimd", wgu[(oc + 1) % 2][:, 2048:4096].rearrange("p (a f) -> p a f", f=128),
                  swu_b[oc + 1].rearrange("(a p) f -> p a f", p=128), "g")
        for tb4 in range(4):
            tcol = tb4 * TB
            S.wait_fa("tensor", fa_pp[0])
            for dc in range(16):
                S.emit("tensor", lambda e, dc=dc, wb=wb, tcol=tcol: e.matmul(
                    pp[0][:], wgu[wb][:, dc * 128:(dc + 1) * 128],
                    ARX[:, dc * 2048 + tcol: dc * 2048 + tcol + TB],
                    start=(dc == 0), stop=(dc == 15)),
                    inc=("p" if dc == 15 else None))
            p_g = S.val("p")
            S.wait_fa("tensor", fa_pp[1])
            for dc in range(16):
                S.emit("tensor", lambda e, dc=dc, wb=wb, tcol=tcol: e.matmul(
                    pp[1][:], wgu[wb][:, 2048 + dc * 128: 2048 + (dc + 1) * 128],
                    ARX[:, dc * 2048 + tcol: dc * 2048 + tcol + TB],
                    start=(dc == 0), stop=(dc == 15)),
                    inc=("p" if dc == 15 else None))
            p_u = S.val("p")
            S.wait("scalar", "p", p_g)
            S.wait_fa("scalar", fa_sg)
            S.emit("scalar", lambda e: e.activation(sg_sb[:], pp[0][:], AF.Silu), inc="a")
            a_sg = S.val("a")
            fa_pp[0] = {"a": a_sg}
            S.wait("vector", "a", a_sg)
            S.wait("vector", "p", p_u)
            S.wait_fa("vector", fa_icsb)
            S.emit("vector", lambda e: e.tensor_tensor(ic_sb[:], sg_sb[:], pp[1][:],
                                                       ALU.mult), inc="v")
            S.wait("vector", "v")
            S.emit("vector", lambda e: e.tensor_copy(fl_sb[:], ic_sb[:, 0:8]), inc="v")
            v_ic = S.val("v")
            fa_pp[1] = {"v": v_ic}
            fa_sg = {"v": v_ic}
            S.wait("sync", "v", v_ic)
            S.dma("sync", ish_scr[oc * 128:(oc + 1) * 128, tcol:tcol + TB], ic_sb[:], "ic")
            fa_icsb = {"ic": S.val("ic")}
        fa_wgu[wb] = {"p": S.val("p")}
    ic_all = S.val("ic")

    S.phase = "D_down"
    for eng in ("sync", "gpsimd", "tensor", "vector", "scalar"):
        S.wait(eng, "ic", ic_all)
        S.wait(eng, "p")
        S.wait(eng, "v")
        S.wait(eng, "a")

    def stripe_ic(k):
        return ARX[:, k * TB:(k + 1) * TB]

    def stripe_sh(k):
        return ARY[:, k * TB:(k + 1) * TB]

    fa_stripe = dict()
    for tb4 in range(4):
        tcol = tb4 * TB
        S.wait_fa("sync", fa_stripe)
        for k in range(64):
            S.dma("sync", stripe_ic(k), ic_scr[k * 128:(k + 1) * 128, tcol:tcol + TB], "s")
        for k in range(16):
            S.dma("sync", stripe_sh(k), ish_scr[k * 128:(k + 1) * 128, tcol:tcol + TB], "s")
        stripe_loaded = S.val("s")
        S.wait("tensor", "s", stripe_loaded)
        for dc in range(16):
            bank = dc % 2
            wb = dc % 2
            S.wait_fa("gpsimd", fa_wdn[wb])
            for ee in range(8):
                S.dma("gpsimd", wdn[wb][:, ee * 1024:(ee + 1) * 1024]
                      .rearrange("p (a f) -> p a f", f=128),
                      wd_b[ee, dc].rearrange("(a p) f -> p a f", p=128), "wl")
            S.dma("gpsimd", wdn[wb][:, 8192:10240].rearrange("p (a f) -> p a f", f=128),
                  swd_b[dc].rearrange("(a p) f -> p a f", p=128), "wl")
            wl_now = S.val("wl")
            S.wait("tensor", "wl", wl_now)
            S.wait_fa("tensor", fa_pp[bank])
            n_mm = 0
            for ee in range(8):
                for hk in range(8):
                    S.emit("tensor", lambda e, ee=ee, hk=hk, wb=wb, bank=bank: e.matmul(
                        pp[bank][:], wdn[wb][:, ee * 1024 + hk * 128: ee * 1024 + (hk + 1) * 128],
                        stripe_ic(ee * 8 + hk),
                        start=(ee == 0 and hk == 0), stop=False))
                    n_mm += 1
            for k in range(16):
                last = (k == 15)
                S.emit("tensor", lambda e, k=k, wb=wb, bank=bank, last=last: e.matmul(
                    pp[bank][:], wdn[wb][:, 8192 + k * 128: 8192 + (k + 1) * 128],
                    stripe_sh(k),
                    start=False, stop=last),
                    inc=("p" if last else None))
            p_dn = S.val("p")
            fa_wdn[wb] = {"p": p_dn}
            S.wait_fa("sync", fa_rtf)
            S.dma("sync", rt_f[:], y3_scr[dc * 128:(dc + 1) * 128, tcol:tcol + TB], "s")
            r_ld = S.val("s")
            S.wait("vector", "p", p_dn)
            S.wait("vector", "s", r_ld)
            S.wait_fa("vector", fa_ytf)
            S.emit("vector", lambda e, bank=bank: e.tensor_tensor(
                yt_f[:], pp[bank][:], rt_f[:], ALU.add), inc="v")
            S.wait("vector", "v")
            S.emit("vector", lambda e: e.tensor_copy(fl_sb[:], yt_f.bitcast(BF16)[:, 0:8]),
                   inc="v")
            v_y = S.val("v")
            fa_pp[bank] = {"v": v_y}
            fa_rtf = {"v": v_y}
            S.wait("sync", "v", v_y)
            S.dma("sync", out_my[dc * 128:(dc + 1) * 128, tcol:tcol + TB], yt_f[:], "ys")
            fa_ytf = {"ys": S.val("ys")}
        fa_stripe = {"p": S.val("p")}

    S.wait("sync", "ys", S.val("ys"))
    S.wait("gpsimd", "ys", S.val("ys"))


def stage_CD(inputs):
    wqkv = np.asarray(inputs["L1_attn_wqkv"], np.float32)
    wo = np.asarray(inputs["L1_attn_wo"], np.float32)
    wqkvT_b = np.ascontiguousarray(wqkv.T.reshape(D, 24, 128).transpose(1, 0, 2))
    woT_b = np.ascontiguousarray(wo.T.reshape(D, 16, 128).transpose(1, 0, 2))
    rw = np.ascontiguousarray(np.asarray(inputs["L1_ffn_router_w"], np.float32))
    wg = np.asarray(inputs["L1_ffn_w_gate"], np.float32)
    wu = np.asarray(inputs["L1_ffn_w_up"], np.float32)
    wdn = np.asarray(inputs["L1_ffn_w_down"], np.float32)
    wg_b = np.ascontiguousarray(wg.reshape(8, D, 8, 128).transpose(0, 2, 1, 3))
    wu_b = np.ascontiguousarray(wu.reshape(8, D, 8, 128).transpose(0, 2, 1, 3))
    wd_b = np.ascontiguousarray(wdn.reshape(8, 1024, 16, 128).transpose(0, 2, 1, 3))
    swg = np.asarray(inputs["L1_ffn_sw_gate"], np.float32)
    swu = np.asarray(inputs["L1_ffn_sw_up"], np.float32)
    swd = np.asarray(inputs["L1_ffn_sw_down"], np.float32)
    swg_b = np.ascontiguousarray(swg.reshape(D, 16, 128).transpose(1, 0, 2))
    swu_b = np.ascontiguousarray(swu.reshape(D, 16, 128).transpose(1, 0, 2))
    swd_b = np.ascontiguousarray(swd.reshape(D, 16, 128).transpose(1, 0, 2))
    ones = np.ones((128, 1), dtype=BF)
    onesr = np.ones((1, 128), dtype=BF)
    e2 = np.zeros((128, 256), dtype=np.float32)
    e2[:, 128] = 1.0
    e2 = e2.astype(BF)
    idb = np.eye(128, dtype=np.float32).astype(BF)
    idf = np.eye(128, dtype=np.float32)
    alib = np.zeros((128, 2), dtype=np.float32)
    n = 16
    m0 = 2.0 ** (-8.0 / n)
    slopes = m0 ** np.arange(1, n + 1, dtype=np.float64)
    pairs10 = [(a, b) for a in range(4) for b in range(a + 1)]
    for stk in range(2):
        for hh in range(8):
            h = stk * 8 + hh
            for pi_, (a, b) in enumerate(pairs10):
                alib[hh * 10 + pi_, stk] = slopes[h] * (-math.sqrt(a - b))

    in_maps = []
    for i in range(NCORES):
        in_maps.append(dict(
            wqkvT_b=wqkvT_b, wo1T_b=woT_b, rw_i=rw,
            wg_b=wg_b, wu_b=wu_b, wd_b=wd_b,
            swg_b=swg_b, swu_b=swu_b, swd_b=swd_b,
            e2_i=e2, idb_i=idb, idf_i=idf,
            alib_i=alib,
        ))
    return in_maps



def build_fused():
    nc = bass.Bass("TRN2", target_bir_lowering=False)
    G = {}
    DT = nc.dram_tensor

    G["x_fm"] = DT("x_fm", [D, T], F32, kind="ExternalInput").ap()
    G["wdq"] = DT("wdq", [D, CP], F32, kind="ExternalInput").ap()
    G["wdkv"] = DT("wdkv", [D, CP], F32, kind="ExternalInput").ap()
    G["wuq_m"] = DT("wuq_m", [CP, 256], F32, kind="ExternalInput").ap()
    G["wk_m"] = DT("wk_m", [CP, 256], F32, kind="ExternalInput").ap()
    G["wv_m"] = DT("wv_m", [CP, 256], F32, kind="ExternalInput").ap()
    G["qg_i"] = DT("qg_i", [128, 4], F32, kind="ExternalInput").ap()
    G["qb_i"] = DT("qb_i", [128, 4], F32, kind="ExternalInput").ap()
    G["kg_i"] = DT("kg_i", [128, 4], F32, kind="ExternalInput").ap()
    G["kb_i"] = DT("kb_i", [128, 4], F32, kind="ExternalInput").ap()
    G["ones_i"] = DT("ones_i", [128, 1], BF16, kind="ExternalInput").ap()
    G["onesr_i"] = DT("onesr_i", [1, 128], BF16, kind="ExternalInput").ap()
    G["trim_i"] = DT("trim_i", [4, 128, TB], BF16, kind="ExternalInput").ap()
    G["x_my"] = DT("x_my", [D, TT], F32, kind="ExternalInput").ap()
    G["woT_b"] = DT("woT_b", [16, D, 128], F32, kind="ExternalInput").ap()
    G["pk_g"] = DT("pk_g", [128, 1024, 128], I16, kind="ExternalInput").ap()
    G["pk_u"] = DT("pk_u", [128, 1024, 128], I16, kind="ExternalInput").ap()
    G["pk_d"] = DT("pk_d", [8192, D], I16, kind="ExternalInput").ap()
    G["sc_g"] = DT("sc_g", [64, NMLP], F32, kind="ExternalInput").ap()
    G["sc_u"] = DT("sc_u", [64, NMLP], F32, kind="ExternalInput").ap()
    G["sc_d"] = DT("sc_d", [512, D], F32, kind="ExternalInput").ap()
    G["wqkvT_b"] = DT("wqkvT_b", [24, D, 128], F32, kind="ExternalInput").ap()
    G["wo1T_b"] = DT("wo1T_b", [16, D, 128], F32, kind="ExternalInput").ap()
    G["rw_i"] = DT("rw_i", [D, 8], F32, kind="ExternalInput").ap()
    G["wg_b"] = DT("wg_b", [8, 8, D, 128], F32, kind="ExternalInput").ap()
    G["wu_b"] = DT("wu_b", [8, 8, D, 128], F32, kind="ExternalInput").ap()
    G["wd_b"] = DT("wd_b", [8, 16, 1024, 128], F32, kind="ExternalInput").ap()
    G["swg_b"] = DT("swg_b", [16, D, 128], F32, kind="ExternalInput").ap()
    G["swu_b"] = DT("swu_b", [16, D, 128], F32, kind="ExternalInput").ap()
    G["swd_b"] = DT("swd_b", [16, D, 128], F32, kind="ExternalInput").ap()
    G["e2_i"] = DT("e2_i", [128, 256], BF16, kind="ExternalInput").ap()
    G["idb_i"] = DT("idb_i", [128, 128], BF16, kind="ExternalInput").ap()
    G["idf_i"] = DT("idf_i", [128, 128], F32, kind="ExternalInput").ap()
    G["alib_i"] = DT("alib_i", [128, 2], F32, kind="ExternalInput").ap()

    G["out_my"] = DT("out_my", [D, TT], F32, kind="ExternalOutput").ap()

    G["k_scr"] = DT("k_scr", [256, T], BF16, kind="Internal").ap()
    G["cbounce"] = DT("cbounce", [1, 8], F32, kind="Internal").ap()
    G["v_scr"] = DT("v_scr", [T, 256], BF16, kind="Internal").ap()
    a2a_in_t = DT("a2a_in", [D, TT], BF16, kind="Internal")
    a2a_out_t = DT("a2a_out", [D, TT], BF16, kind="Internal")
    G["a2a_in"] = a2a_in_t.ap()
    G["a2a_out"] = a2a_out_t.ap()
    G["resid_scr"] = DT("resid_scr", [D, TT], F32, kind="Internal").ap()
    G["xeo_scr"] = DT("xeo_scr", [D, TT], BF16, kind="Internal").ap()
    G["inter_scr"] = DT("inter_scr", [NMLP, TT], BF16, kind="Internal").ap()
    G["wd_scr"] = DT("wd_scr", [16, NMLP, 128], BF16, kind="Internal").ap()
    G["y2_scr"] = DT("y2_scr", [D, TT], F32, kind="Internal").ap()
    G["y3_scr"] = DT("y3_scr", [D, TT], F32, kind="Internal").ap()
    G["attn_scr"] = DT("attn_scr", [D, TT], BF16, kind="Internal").ap()
    G["p_scr"] = DT("p_scr", [2, 128, TB], BF16, kind="Internal").ap()
    G["cb_scr"] = DT("cb_scr", [128, 128], F32, kind="Internal").ap()
    G["ic_scr"] = DT("ic_scr", [8 * 1024, TT], BF16, kind="Internal").ap()
    G["ish_scr"] = DT("ish_scr", [D, TT], BF16, kind="Internal").ap()

    A = nc.alloc_sbuf_tensor
    G["ARX"] = A("ARX", [128, 32768], BF16)
    G["ARY"] = A("ARY", [128, 16384], BF16)
    G["ARZW"] = A("ARZW", [128, 16384], BF16)
    G["WU"] = A("WU", [128, 20480], BF16)
    G["S_sb"] = A("S_sb", [128, TB], BF16)
    G["sT_sb"] = A("sT_sb", [128, TB], BF16)
    G["P_sb"] = A("P_sb", [128, TB], BF16)
    G["tmp_sb"] = [A(f"tmp_sb{i}", [128, TB], BF16) for i in range(2)]
    G["pb_sb"] = A("pb_sb", [128, TB], BF16)
    G["acc_sb"] = A("acc_sb", [128, TB], F32)
    G["tmp2_sb"] = A("tmp2_sb", [128, TB], F32)
    G["rt_f"] = A("rt_f", [128, TB], F32)
    G["yt_f"] = A("yt_f", [128, TB], F32)
    G["fl_sb"] = A("fl_sb", [128, 8], BF16)
    G["ones_sb"] = A("ones_sb", [128, 1], BF16)
    G["onesr_sb"] = A("onesr_sb", [1, 128], BF16)
    G["e2_sb"] = A("e2_sb", [128, 256], BF16)
    G["idb_sb"] = A("idb_sb", [128, 128], BF16)
    G["idf_sb"] = A("idf_sb", [128, 128], F32)
    G["alib_sb"] = A("alib_sb", [128, 2], F32)
    G["rw_sb"] = A("rw_sb", [128, 16 * 8], BF16)
    G["nmx"] = A("nmx", [128, 8], F32)
    G["rec1"] = A("rec1", [128, 8], F32)
    G["ro_sb"] = A("ro_sb", [128, 128], F32)
    G["e_sb"] = A("e_sb", [128, 128], F32)
    G["m1_sb"] = A("m1_sb", [128, 16], F32)
    G["m2_sb"] = A("m2_sb", [128, 16], F32)
    G["den_sb"] = A("den_sb", [128, 16], F32)
    G["eq_sb"] = A("eq_sb", [128, 128], F32)
    G["me_sb"] = A("me_sb", [128, 128], F32)
    G["en_sb"] = A("en_sb", [128, 128], F32)
    G["comb_sb"] = A("comb_sb", [128, 128], F32)
    G["combT_sb"] = A("combT_sb", [128, 128], F32)
    G["ce_sb"] = A("ce_sb", [128, TT], F32)
    G["sg_sb"] = A("sg_sb", [128, TB], BF16)
    G["ic_sb"] = A("ic_sb", [128, TB], BF16)
    G["st_f"] = [A(f"st_f{i}", [1, TB], F32) for i in range(3)]
    G["st_f"].append(G["st_f"][2])
    G["st_b"] = [A(f"st_b{i}", [1, TB], BF16) for i in range(2)]
    G["sq_sb"] = G["sg_sb"]
    G["xb_sb"] = G["ic_sb"]
    G["qg_sb"] = A("qg_sb", [128, 4], F32)
    G["qb_sb"] = A("qb_sb", [128, 4], F32)
    G["kg_sb"] = A("kg_sb", [128, 4], F32)
    G["kb_sb"] = A("kb_sb", [128, 4], F32)
    G["qn2"] = A("qn2", [1, 2 * (T // TB)], F32)
    G["kn2"] = A("kn2", [1, 2 * (T // TB)], F32)
    G["sc_sb"] = A("sc_sb", [1, 8], F32)
    G["cb_sb"] = A("cb_sb", [128, 1], F32)

    PT = nc.alloc_psum_tensor
    G["pp"] = [PT(f"pp{i}", [128, TB], F32) for i in range(3)]
    G["bc"] = [PT(f"bc{i}", [128, TB], F32) for i in range(2)]
    G["stp"] = [PT(f"stp{i}", [1, TB], F32) for i in range(2)]
    G["sum_ps"] = PT("sum_ps", [1, TB], F32)
    G["ptb"] = G["pp"][2].bitcast(BF16)[:, 0:128]
    G["ptf"] = G["pp"][2][:, 128:256]

    S = Sched(nc)

    def barrier():
        for eng in Sched.ENGS:
            for name in list(S.cnt):
                S.wait(eng, name)

    S.phase = "A_qkv"
    emit_A(nc, S, G)
    S.phase = "a2a"
    barrier()
    ccs = S.sem("cc")
    S.ops["gpsimd"].append(("a2a",
        lambda e: e.collective_compute(
            "AllToAll", mybir.AluOpType.bypass,
            replica_groups=[list(range(NCORES))],
            ins=[a2a_in_t.ap().opt()],
            outs=[a2a_out_t.ap().opt()],
        ).then_inc(ccs)))
    S.cnt["cc"] += 1
    barrier()
    emit_B1(nc, S, G)
    barrier()
    emit_CD(nc, S, G)
    S.build()
    return nc, S.n_ops()


def stage_fused(inputs):
    x = np.asarray(inputs["L0_attn_x"], dtype=np.float32)
    x_fm = np.ascontiguousarray(x.transpose(2, 0, 1).reshape(D, T))
    maps_a = stage_A(inputs)
    maps_b = stage_B1(inputs, x_fm)
    maps_cd = stage_CD(inputs)
    in_maps = []
    for i in range(NCORES):
        m = dict(maps_a[i])
        m.update(maps_b[i])
        m.update(maps_cd[i])
        in_maps.append(m)
    return in_maps


def kernel(**inputs):
    from concourse.aot_env import allow_ici_all_to_all
    with allow_ici_all_to_all():
        nc, _ = build_fused()
        in_maps = stage_fused(inputs)
        res = run_bass_kernel_spmd(nc, in_maps, core_ids=list(range(NCORES)))
    out = np.zeros((D, T), np.float32)
    for i in range(NCORES):
        out[:, token_cols(i)] = res.results[i]["out_my"]
    return np.ascontiguousarray(
        out.reshape(D, B, SL).transpose(1, 2, 0)).astype(np.float32)


# version 04 (final)
# speedup vs baseline: 1.3373x; 1.1165x over previous
import math
import numpy as np
import ml_dtypes
import concourse.bass as bass
import concourse.mybir as mybir
from concourse.bass_utils import run_bass_kernel_spmd

F32 = mybir.dt.float32
BF16 = mybir.dt.bfloat16
I16 = mybir.dt.int16
I32 = mybir.dt.int32
AF = mybir.ActivationFunctionType
ALU = mybir.AluOpType
AX = mybir.AxisListType

BF = ml_dtypes.bfloat16

D = 2048
CP = 512
T = 16384
SL = 4096
B = 4
DH = 128
NH = 16
EPS = 1e-5
ISQ = 1.0 / math.sqrt(128.0)

NCORES = 8
TB = 512


class Sched:
    ENGS = ("sync", "gpsimd", "vector", "scalar", "tensor")

    def __init__(self, nc):
        self.nc = nc
        self.ops = {e: [] for e in self.ENGS}
        self.handles = {}
        self.cnt = {}
        self.phase = "init"

    def sem(self, name):
        if name not in self.handles:
            self.handles[name] = self.nc.alloc_semaphore(name)
            self.cnt[name] = 0
        return self.handles[name]

    def wait(self, eng, name, val=None):
        h = self.sem(name)
        v = self.cnt[name] if val is None else val
        if v > 0:
            self.ops[eng].append((self.phase, lambda e, h=h, v=v: e.wait_ge(h, v)))

    def wait_fa(self, eng, free_at):
        for name, v in free_at.items():
            self.wait(eng, name, v)

    def emit(self, eng, fn, inc=None, inc_n=1):
        if inc is not None:
            h = self.sem(inc)
            self.cnt[inc] += inc_n
            self.ops[eng].append(
                (self.phase, lambda e, fn=fn, h=h, n=inc_n: fn(e).then_inc(h, n)))
        else:
            self.ops[eng].append((self.phase, fn))

    def dma(self, eng, out, in_, inc):
        self.emit(eng, lambda e, o=out, i=in_: e.dma_start(out=o, in_=i),
                  inc=inc, inc_n=16)

    def val(self, name):
        return self.cnt.get(name, 0)

    def mark(self, *names):
        return {n: self.val(n) for n in names}

    def n_ops(self):
        return sum(len(v) for v in self.ops.values())

    def build(self):
        with self.nc.Block() as blk:
            for eng in self.ENGS:
                if self.ops[eng]:
                    def runner(e, eng=eng):
                        i, ops, n = 0, self.ops[eng], len(self.ops[eng])
                        while i < n:
                            ph = ops[i][0]
                            with self.nc.named_scope(ph):
                                while i < n and ops[i][0] == ph:
                                    ops[i][1](e)
                                    i += 1
                    getattr(blk, eng)(runner)



def emit_A(nc, S, G):
    NTB = T // TB

    x_my = G["x_my"]; wdq = G["wdq"]; wdkv = G["wdkv"]
    agq_in = G["agq_in"]; agq_out = G["agq_out"]
    wuq_m = G["wuq_m"]; wk_m = G["wk_m"]; wv_m = G["wv_m"]
    qg_i = G["qg_i"]; qb_i = G["qb_i"]; kg_i = G["kg_i"]; kb_i = G["kb_i"]
    ones_i = G["ones_i"]; onesr_i = G["onesr_i"]; trim_i = G["trim_i"]
    a2a_in = G["a2a_in"]
    k_scr = G["k_scr"]; cbounce = G["cbounce"]; v_scr = G["v_scr"]

    ARX = G["ARX"]; ARY = G["ARY"]; ARZW = G["ARZW"]; WU = G["WU"]
    q_sb = ARX
    xab = [ARY[:, 0:8192], ARY[:, 8192:16384]]
    xsq = ARZW[:, 0:8192]
    wdq_sb = ARZW[:, 8192:16384]
    wdkv_sb = WU[:, 0:8192]
    raw_sb = WU[:, 8192:10240]
    sq_sb = WU[:, 10240:12288]
    nrm_sb = WU[:, 12288:14336]
    wuq_sb = WU[:, 14336:15360]
    wk_sb = WU[:, 15360:16384]
    wv_sb = WU[:, 16384:17408]
    trim_sb = WU[:, 17408:19456]
    kev_sb = G["S_sb"]
    vev_sb = G["sT_sb"][:, 0:256]
    o_sb = G["P_sb"]
    rec_sb = G["rt_f"]
    qg_sb = G["qg_sb"]; qb_sb = G["qb_sb"]; kg_sb = G["kg_sb"]; kb_sb = G["kb_sb"]
    ones_sb = G["ones_sb"]; onesr_sb = G["onesr_sb"]
    st_f = G["st_f"]; st_b = G["st_b"]
    qn2 = G["qn2"]; kn2 = G["kn2"]
    sc_sb = G["sc_sb"]; cb_sb = G["cb_sb"]
    pp = G["pp"]; bc = G["bc"]; stp = G["stp"]; sum_ps = G["sum_ps"]

    fa_pp = [dict(), dict(), dict()]
    fa_bc = [dict(), dict()]
    fa_stp = [dict(), dict()]
    fa_sum = dict()
    fa_stf = dict()
    fa_xab = [dict(), dict()]
    fa_xsq = dict()
    fa_raw = dict()
    fa_sq = dict()
    fa_nrm = dict()
    fa_osb = dict()
    fa_kev = dict()
    fa_vev = dict()
    fa_cb = dict()
    fa_scb = dict()

    for dc in range(16):
        S.dma("gpsimd", wdq_sb[:, dc * CP:(dc + 1) * CP],
              wdq[dc * 128:(dc + 1) * 128, :], "g")
        S.dma("gpsimd", wdkv_sb[:, dc * CP:(dc + 1) * CP],
              wdkv[dc * 128:(dc + 1) * 128, :], "g")
    for cc in range(4):
        S.dma("gpsimd", wuq_sb[:, cc * 256:(cc + 1) * 256],
              wuq_m[cc * 128:(cc + 1) * 128, :], "g")
        S.dma("gpsimd", wk_sb[:, cc * 256:(cc + 1) * 256],
              wk_m[cc * 128:(cc + 1) * 128, :], "g")
        S.dma("gpsimd", wv_sb[:, cc * 256:(cc + 1) * 256],
              wv_m[cc * 128:(cc + 1) * 128, :], "g")
    S.dma("sync", qg_sb[:], qg_i, "s")
    S.dma("sync", qb_sb[:], qb_i, "s")
    S.dma("sync", kg_sb[:], kg_i, "s")
    S.dma("sync", kb_sb[:], kb_i, "s")
    S.dma("sync", ones_sb[:], ones_i, "s")
    S.dma("sync", onesr_sb[:], onesr_i, "s")
    for j in range(4):
        S.dma("sync", trim_sb[:, j * TB:(j + 1) * TB], trim_i[j], "s")
    w_g = S.val("g")
    w_s = S.val("s")

    NTT = 4

    def load_x_tile(tb, buf):
        S.wait_fa("gpsimd", fa_xab[buf])
        for dc in range(16):
            S.dma("gpsimd", xab[buf][:, dc * TB:(dc + 1) * TB],
                  x_my[dc * 128:(dc + 1) * 128, tb * TB:(tb + 1) * TB], "g")
        return S.val("g")

    g_loaded = {0: load_x_tile(0, 0)}

    S.wait("tensor", "g", w_g)
    S.wait("tensor", "s", w_s)
    S.wait("scalar", "g", w_g)
    S.wait("scalar", "s", w_s)

    def ln_stats(n_inv):
        S.wait("vector", "p")
        S.wait_fa("vector", fa_stf)
        S.emit("vector", lambda e: e.tensor_scalar_mul(st_f[0][:], stp[0][:], n_inv))
        S.emit("vector", lambda e: e.tensor_scalar_mul(st_f[1][:], stp[1][:], n_inv))
        S.emit("vector", lambda e: e.tensor_tensor(st_f[2][:], st_f[0][:], st_f[0][:], ALU.mult),
               inc="v")
        S.wait("vector", "v")
        S.emit("vector", lambda e: e.tensor_tensor(st_f[1][:], st_f[1][:], st_f[2][:], ALU.subtract))
        S.emit("vector", lambda e: e.tensor_scalar_add(st_f[1][:], st_f[1][:], EPS),
               inc="v")
        fa_stp[0]["v"] = S.val("v")
        fa_stp[1]["v"] = S.val("v")
        S.wait("scalar", "v")
        S.emit("scalar", lambda e: e.activation(st_f[1][:], st_f[1][:], AF.Ln))
        S.emit("scalar", lambda e: e.activation(st_f[1][:], st_f[1][:], AF.Exp, scale=-0.5),
               inc="a")
        S.wait("vector", "a")
        S.emit("vector", lambda e: e.tensor_copy(st_b[0][:], st_f[0][:]))
        S.emit("vector", lambda e: e.tensor_copy(st_b[1][:], st_f[1][:]), inc="v")
        S.wait("tensor", "v")
        S.wait_fa("tensor", fa_bc[0])
        S.wait_fa("tensor", fa_bc[1])
        S.emit("tensor", lambda e: e.matmul(bc[0][:], onesr_sb[:], st_b[0][:],
                                            start=True, stop=True))
        S.emit("tensor", lambda e: e.matmul(bc[1][:], onesr_sb[:], st_b[1][:],
                                            start=True, stop=True), inc="p")
        fa_stf["p"] = S.val("p")

    for tb in range(NTT):
        buf = tb % 2
        S.wait("scalar", "g", g_loaded[tb])
        S.wait_fa("scalar", fa_xsq)
        for dc in range(16):
            S.emit("scalar", lambda e, dc=dc, buf=buf: e.activation(
                xsq[:, dc * TB:(dc + 1) * TB],
                xab[buf][:, dc * TB:(dc + 1) * TB], AF.Square),
                inc=("a" if dc == 15 else None))
        a_sq = S.val("a")
        S.wait("tensor", "g", g_loaded[tb])
        S.wait_fa("tensor", fa_stp[0])
        for dc in range(16):
            S.emit("tensor", lambda e, dc=dc, buf=buf: e.matmul(
                stp[0][:], ones_sb[:], xab[buf][:, dc * TB:(dc + 1) * TB],
                start=(dc == 0), stop=(dc == 15)),
                inc=("p" if dc == 15 else None))
        S.wait("tensor", "a", a_sq)
        S.wait_fa("tensor", fa_stp[1])
        for dc in range(16):
            S.emit("tensor", lambda e, dc=dc: e.matmul(
                stp[1][:], ones_sb[:], xsq[:, dc * TB:(dc + 1) * TB],
                start=(dc == 0), stop=(dc == 15)),
                inc=("p" if dc == 15 else None))
        ln_stats(1.0 / D)
        S.wait("vector", "p")
        for dc in range(16):
            S.emit("vector", lambda e, dc=dc, buf=buf: e.tensor_tensor(
                xsq[:, dc * TB:(dc + 1) * TB],
                xab[buf][:, dc * TB:(dc + 1) * TB], bc[0][:], ALU.subtract))
            S.emit("vector", lambda e, dc=dc: e.tensor_tensor(
                xsq[:, dc * TB:(dc + 1) * TB],
                xsq[:, dc * TB:(dc + 1) * TB], bc[1][:], ALU.mult),
                inc=("v" if dc == 15 else None))
        v_xa = S.val("v")
        fa_xab[buf] = {"v": v_xa}
        fa_bc[0] = {"v": v_xa}
        fa_bc[1] = {"v": v_xa}
        if tb + 1 < NTT:
            g_loaded[tb + 1] = load_x_tile(tb + 1, 1 - buf)

        for br in range(2):
            wsb = wdq_sb if br == 0 else wdkv_sb
            gg = qg_sb if br == 0 else kg_sb
            bbv = qb_sb if br == 0 else kb_sb
            S.wait("tensor", "v", v_xa)
            for cc in range(4):
                bank = cc % 2
                S.wait_fa("tensor", fa_pp[bank])
                for dc in range(16):
                    S.emit("tensor", lambda e, dc=dc, cc=cc, wsb=wsb, bank=bank: e.matmul(
                        pp[bank][:],
                        wsb[:, dc * CP + cc * 128: dc * CP + (cc + 1) * 128],
                        xsq[:, dc * TB:(dc + 1) * TB],
                        start=(dc == 0), stop=(dc == 15)),
                        inc=("p" if dc == 15 else None))
                p_proj = S.val("p")
                S.wait("scalar", "p", p_proj)
                if cc < 2:
                    S.wait_fa("scalar", fa_raw)
                S.emit("scalar", lambda e, cc=cc, bank=bank: e.activation(
                    raw_sb[:, cc * TB:(cc + 1) * TB], pp[bank][:], AF.Identity),
                    inc="a")
                fa_pp[bank] = {"a": S.val("a")}
            a_raw = S.val("a")
            S.wait_fa("scalar", fa_sq)
            for cc in range(4):
                S.emit("scalar", lambda e, cc=cc: e.activation(
                    sq_sb[:, cc * TB:(cc + 1) * TB],
                    raw_sb[:, cc * TB:(cc + 1) * TB], AF.Square),
                    inc=("a" if cc == 3 else None))
            a_sq2 = S.val("a")
            S.wait("tensor", "a", a_raw)
            S.wait_fa("tensor", fa_stp[0])
            for cc in range(4):
                S.emit("tensor", lambda e, cc=cc: e.matmul(
                    stp[0][:], ones_sb[:], raw_sb[:, cc * TB:(cc + 1) * TB],
                    start=(cc == 0), stop=(cc == 3)),
                    inc=("p" if cc == 3 else None))
            S.wait("tensor", "a", a_sq2)
            S.wait_fa("tensor", fa_stp[1])
            for cc in range(4):
                S.emit("tensor", lambda e, cc=cc: e.matmul(
                    stp[1][:], ones_sb[:], sq_sb[:, cc * TB:(cc + 1) * TB],
                    start=(cc == 0), stop=(cc == 3)),
                    inc=("p" if cc == 3 else None))
            ln_stats(1.0 / CP)
            S.wait("vector", "p")
            S.wait_fa("vector", fa_sq)
            for cc in range(4):
                S.emit("vector", lambda e, cc=cc: e.tensor_tensor(
                    sq_sb[:, cc * TB:(cc + 1) * TB],
                    raw_sb[:, cc * TB:(cc + 1) * TB], bc[0][:], ALU.subtract))
                S.emit("vector", lambda e, cc=cc: e.tensor_tensor(
                    sq_sb[:, cc * TB:(cc + 1) * TB],
                    sq_sb[:, cc * TB:(cc + 1) * TB], bc[1][:], ALU.mult),
                    inc=("v" if cc == 3 else None))
            v_n0 = S.val("v")
            fa_bc[0] = {"v": v_n0}
            fa_bc[1] = {"v": v_n0}
            fa_raw = {"v": v_n0}
            S.wait("scalar", "v", v_n0)
            S.wait_fa("scalar", fa_nrm)
            for cc in range(4):
                S.emit("scalar", lambda e, cc=cc, gg=gg, bbv=bbv: e.activation(
                    nrm_sb[:, cc * TB:(cc + 1) * TB],
                    sq_sb[:, cc * TB:(cc + 1) * TB], AF.Identity,
                    bias=bbv[:, cc:cc + 1], scale=gg[:, cc:cc + 1]),
                    inc=("a" if cc == 3 else None))
            a_nrm = S.val("a")
            fa_sq["a"] = a_nrm

            S.wait("sync", "a", a_nrm)
            for cc in range(4):
                S.dma("sync",
                      agq_in[br * CP + cc * 128: br * CP + (cc + 1) * 128,
                             tb * TB:(tb + 1) * TB],
                      nrm_sb[:, cc * TB:(cc + 1) * TB], "ag")
            fa_nrm = {"ag": S.val("ag")}
        fa_xsq["p"] = S.val("p")

    S.wait("gpsimd", "ag", S.val("ag"))
    cqs = S.sem("cq")
    S.ops["gpsimd"].append((S.phase,
        lambda e: e.collective_compute(
            "AllGather", mybir.AluOpType.bypass,
            replica_groups=[list(range(NCORES))],
            ins=[G["agq_in_t"].ap().opt()],
            outs=[G["agq_out_t"].ap().opt()],
        ).then_inc(cqs)))
    S.cnt["cq"] += 1
    for eng in Sched.ENGS:
        S.wait(eng, "cq", 1)

    fa_cqb = dict()
    for r in range(8):
        S.wait_fa("gpsimd", fa_cqb)
        for cc in range(4):
            S.dma("gpsimd", xab[0][:, cc * 2048:(cc + 1) * 2048],
                  agq_out[r * 1024 + cc * 128: r * 1024 + (cc + 1) * 128, :], "g")
            S.dma("gpsimd", xab[1][:, cc * 2048:(cc + 1) * 2048],
                  agq_out[r * 1024 + CP + cc * 128: r * 1024 + CP + (cc + 1) * 128, :],
                  "g")
        g_blk = S.val("g")
        S.wait("tensor", "g", g_blk)
        for oc in range(2):
            for tcol in range(4):
                S.wait_fa("tensor", fa_pp[2])
                for cc in range(4):
                    S.emit("tensor", lambda e, cc=cc, oc=oc, tcol=tcol: e.matmul(
                        pp[2][:],
                        wuq_sb[:, cc * 256 + oc * 128: cc * 256 + (oc + 1) * 128],
                        xab[0][:, cc * 2048 + tcol * TB: cc * 2048 + (tcol + 1) * TB],
                        start=(cc == 0), stop=(cc == 3)),
                        inc=("p" if cc == 3 else None))
                p_q = S.val("p")
                S.wait("scalar", "p", p_q)
                S.emit("scalar", lambda e, oc=oc, r=r, tcol=tcol: e.activation(
                    q_sb[:, oc * T + r * 2048 + tcol * TB:
                         oc * T + r * 2048 + (tcol + 1) * TB],
                    pp[2][:], AF.Identity), inc="a")
                S.wait_fa("scalar", fa_sq)
                S.emit("scalar", lambda e: e.activation(
                    sq_sb[:, 0:TB], pp[2][:], AF.Square), inc="a")
                a_qs = S.val("a")
                fa_pp[2] = {"a": a_qs}
                S.wait("tensor", "a", a_qs)
                S.wait_fa("tensor", fa_stp[0])
                S.emit("tensor", lambda e: e.matmul(
                    stp[0][:], ones_sb[:], sq_sb[:, 0:TB],
                    start=True, stop=True), inc="p")
                p_qn = S.val("p")
                fa_sq = {"p": p_qn}
                S.wait("vector", "p", p_qn)
                S.emit("vector", lambda e, oc=oc, r=r, tcol=tcol: e.reduce_max(
                    qn2[0:1, oc * NTB + tcol * 8 + r: oc * NTB + tcol * 8 + r + 1],
                    stp[0][:], axis=AX.X), inc="v")
                fa_stp[0] = {"v": S.val("v")}
        for oc in range(2):
            for tcol in range(4):
                S.wait_fa("tensor", fa_pp[2])
                for cc in range(4):
                    S.emit("tensor", lambda e, cc=cc, oc=oc, tcol=tcol: e.matmul(
                        pp[2][:],
                        wk_sb[:, cc * 256 + oc * 128: cc * 256 + (oc + 1) * 128],
                        xab[1][:, cc * 2048 + tcol * TB: cc * 2048 + (tcol + 1) * TB],
                        start=(cc == 0), stop=(cc == 3)),
                        inc=("p" if cc == 3 else None))
                p_k = S.val("p")
                S.wait("scalar", "p", p_k)
                S.wait_fa("scalar", fa_kev)
                S.emit("scalar", lambda e: e.activation(
                    kev_sb[:], pp[2][:], AF.Identity), inc="a")
                a_ke = S.val("a")
                S.wait_fa("scalar", fa_sq)
                S.emit("scalar", lambda e: e.activation(
                    sq_sb[:, 0:TB], pp[2][:], AF.Square), inc="a")
                a_ks = S.val("a")
                fa_pp[2] = {"a": a_ks}
                S.wait("sync", "a", a_ke)
                S.dma("sync", k_scr[oc * 128:(oc + 1) * 128,
                                    r * 2048 + tcol * TB: r * 2048 + (tcol + 1) * TB],
                      kev_sb[:], "ks")
                fa_kev = {"ks": S.val("ks")}
                S.wait("tensor", "a", a_ks)
                S.wait_fa("tensor", fa_stp[0])
                S.emit("tensor", lambda e: e.matmul(
                    stp[0][:], ones_sb[:], sq_sb[:, 0:TB],
                    start=True, stop=True), inc="p")
                p_kn = S.val("p")
                fa_sq = {"p": p_kn}
                S.wait("vector", "p", p_kn)
                S.emit("vector", lambda e, oc=oc, r=r, tcol=tcol: e.reduce_max(
                    kn2[0:1, oc * NTB + tcol * 8 + r: oc * NTB + tcol * 8 + r + 1],
                    stp[0][:], axis=AX.X), inc="v")
                fa_stp[0] = {"v": S.val("v")}
        for ts_ in range(16):
            S.wait_fa("tensor", fa_pp[2])
            for cc in range(4):
                S.emit("tensor", lambda e, cc=cc, ts_=ts_: e.matmul(
                    pp[2][:, 0:256],
                    xab[1][:, cc * 2048 + ts_ * 128: cc * 2048 + (ts_ + 1) * 128],
                    wv_sb[:, cc * 256:(cc + 1) * 256],
                    start=(cc == 0), stop=(cc == 3)),
                    inc=("p" if cc == 3 else None))
            p_v = S.val("p")
            S.wait("scalar", "p", p_v)
            S.wait_fa("scalar", fa_vev)
            S.emit("scalar", lambda e: e.activation(
                vev_sb[:], pp[2][:, 0:256], AF.Identity), inc="a")
            a_ve = S.val("a")
            fa_pp[2] = {"a": a_ve}
            S.wait("sync", "a", a_ve)
            S.dma("sync",
                  v_scr[r * 2048 + ts_ * 128: r * 2048 + (ts_ + 1) * 128, :],
                  vev_sb[:], "ks")
            fa_vev = {"ks": S.val("ks")}
        fa_cqb = {"p": S.val("p")}
    fa_xab[0] = {"p": S.val("p")}
    fa_xab[1] = {"p": S.val("p")}

    S.phase = "A_attn"
    kb_ = [xab[0][:, 0:4096], xab[1][:, 0:4096]]
    vb_ = [xab[0][:, 4096:8192], xab[1][:, 4096:8192]]
    pt_ = [xsq[:, 0:TB], xsq[:, TB:2 * TB]]
    fa_kvbuf = [dict(fa_xab[0]), dict(fa_xab[1])]
    fa_pt = [dict(fa_xsq), dict(fa_xsq)]

    ks_all = S.val("ks")
    S.wait("sync", "ks", ks_all)

    pairs = [(b, hl) for b in range(B) for hl in range(2)]
    a_exp = {}
    p_cons = {}
    fa_recsb = dict()

    for pi, (b, hl) in enumerate(pairs):
        lb = pi % 2
        S.wait_fa("sync", fa_kvbuf[lb])
        for j in range(8):
            S.dma("sync", kb_[lb][:, j * TB:(j + 1) * TB],
                  k_scr[hl * 128:(hl + 1) * 128,
                        j * 2048 + b * TB: j * 2048 + (b + 1) * TB], "kv")
        vb3 = vb_[lb].rearrange("p (kc f) -> p kc f", f=128)
        for j in range(8):
            vsrc_j = v_scr[j * 2048 + b * TB: j * 2048 + (b + 1) * TB,
                           hl * 128:(hl + 1) * 128] \
                .rearrange("(kc p) f -> p kc f", p=128)
            S.dma("sync", vb3[:, j * 4:(j + 1) * 4, :], vsrc_j[:], "kv")
        kv_loaded = S.val("kv")

        S.wait("vector", "v")
        S.wait_fa("vector", fa_scb)
        S.emit("vector", lambda e, hl=hl, b=b: e.reduce_max(
            sc_sb[:, 0:1],
            qn2[0:1, hl * NTB + b * 8: hl * NTB + (b + 1) * 8], axis=AX.X))
        S.emit("vector", lambda e, hl=hl, b=b: e.reduce_max(
            sc_sb[:, 1:2],
            kn2[0:1, hl * NTB + b * 8: hl * NTB + (b + 1) * 8], axis=AX.X),
            inc="v")
        S.wait("vector", "v")
        S.emit("vector", lambda e: e.tensor_tensor(
            sc_sb[:, 2:3], sc_sb[:, 0:1], sc_sb[:, 1:2], ALU.mult), inc="v")
        S.wait("scalar", "v")
        S.emit("scalar", lambda e: e.activation(sc_sb[:, 3:4], sc_sb[:, 2:3], AF.Ln))
        S.emit("scalar", lambda e: e.activation(sc_sb[:, 3:4], sc_sb[:, 3:4], AF.Exp, scale=0.5),
               inc="a")
        S.wait("vector", "a")
        S.emit("vector", lambda e: e.tensor_scalar_mul(sc_sb[:, 4:5], sc_sb[:, 3:4],
                                                       -1.02 * ISQ), inc="v")
        v_c2 = S.val("v")
        S.wait("sync", "v", v_c2)
        S.dma("sync", cbounce[:, 0:1], sc_sb[:, 4:5], "cbd")
        S.wait("sync", "cbd")
        S.wait_fa("sync", fa_cb)
        S.dma("sync", cb_sb[:], cbounce[:, 0:1].broadcast_to([128, 1]), "cbd")
        v_cb = S.val("cbd")
        fa_scb = {"cbd": v_cb}

        S.wait("tensor", "kv", kv_loaded)
        S.wait("scalar", "cbd", v_cb)

        job = 0
        a_cb_users_last = None
        for qb in range(8):
            njt = 4 * qb + 4
            qcol = qb * 2048 + b * TB
            for j in range(njt):
                bank = job % 2
                ptile = pt_[job % 2]
                S.wait_fa("tensor", fa_pp[bank])
                S.emit("tensor", lambda e, j=j, hl=hl, qcol=qcol, bank=bank, lb=lb: e.matmul(
                    pp[bank][:], kb_[lb][:, j * 128:(j + 1) * 128],
                    q_sb[:, hl * T + qcol: hl * T + qcol + TB],
                    start=True, stop=True), inc="p")
                p_s = S.val("p")
                S.wait("scalar", "p", p_s)
                S.wait_fa("scalar", fa_pt[job % 2])
                S.emit("scalar", lambda e, bank=bank, ptile=ptile: e.activation(
                    ptile, pp[bank][:], AF.Exp, bias=cb_sb[:], scale=ISQ), inc="a")
                a_exp[job] = S.val("a")
                a_cb_users_last = a_exp[job]
                fa_pp[bank] = {"a": a_exp[job]}
                diag = j >= 4 * qb
                vm_now = None
                if diag:
                    jr = j - 4 * qb
                    S.wait("vector", "a", a_exp[job])
                    S.emit("vector", lambda e, jr=jr, ptile=ptile: e.tensor_tensor(
                        ptile, ptile, trim_sb[:, jr * TB:(jr + 1) * TB],
                        ALU.mult), inc="vm")
                    vm_now = S.val("vm")
                S.wait("tensor", "a", a_exp[job])
                if diag:
                    S.wait("tensor", "vm", vm_now)
                first = (j == 0)
                last = (j == njt - 1)
                if first:
                    S.wait_fa("tensor", fa_pp[2])
                    S.wait_fa("tensor", fa_sum)
                S.emit("tensor", lambda e, j=j, lb=lb, ptile=ptile, first=first, last=last: e.matmul(
                    pp[2][:], vb_[lb][:, j * 128:(j + 1) * 128], ptile,
                    start=first, stop=last))
                S.emit("tensor", lambda e, ptile=ptile, first=first, last=last: e.matmul(
                    sum_ps[:], ones_sb[:], ptile,
                    start=first, stop=last), inc="p")
                p_cons[job] = S.val("p")
                fa_pt[job % 2] = {"p": p_cons[job]}
                job += 1
            p_qb = S.val("p")
            S.wait("vector", "p", p_qb)
            S.wait_fa("vector", fa_stf)
            S.emit("vector", lambda e: e.tensor_copy(st_f[3][:], sum_ps[:]), inc="v")
            S.wait("vector", "v")
            S.emit("vector", lambda e: e.reciprocal(st_f[3][:], st_f[3][:]))
            S.emit("vector", lambda e: e.tensor_copy(st_b[0][:], st_f[3][:]), inc="v")
            v_rec = S.val("v")
            fa_sum = {"v": v_rec}
            S.wait("tensor", "v", v_rec)
            S.wait_fa("tensor", fa_bc[1])
            S.emit("tensor", lambda e: e.matmul(bc[1][:], onesr_sb[:], st_b[0][:],
                                                start=True, stop=True), inc="p")
            p_recb = S.val("p")
            fa_stf = {"p": p_recb}
            S.wait("vector", "p", p_recb)
            S.wait_fa("vector", fa_recsb)
            S.emit("vector", lambda e: e.tensor_copy(rec_sb[:], bc[1][:]), inc="v")
            v_recb = S.val("v")
            fa_bc[1] = {"v": v_recb}
            S.wait_fa("vector", fa_osb)
            S.emit("vector", lambda e: e.tensor_tensor(o_sb[:], pp[2][:], rec_sb[:],
                                                       ALU.mult), inc="v")
            v_onrm = S.val("v")
            fa_pp[2] = {"v": v_onrm}
            fa_recsb = {"v": v_onrm}
            S.wait("sync", "v", v_onrm)
            S.dma("sync", a2a_in[qb * 256 + hl * 128: qb * 256 + (hl + 1) * 128,
                                 b * TB:(b + 1) * TB],
                  o_sb[:], "os")
            fa_osb = {"os": S.val("os")}
        fa_kvbuf[lb] = {"p": S.val("p")}
        fa_cb = {"a": a_cb_users_last}

    S.wait("sync", "os", S.val("os"))
    S.wait("gpsimd", "os", S.val("os"))



def stage_A(inputs):
    wdq = np.ascontiguousarray(np.asarray(inputs["L0_attn_W_dq"], np.float32))
    wdkv = np.ascontiguousarray(np.asarray(inputs["L0_attn_W_dkv"], np.float32))
    wuq = np.asarray(inputs["L0_attn_W_uq"], np.float32)
    wukv = np.asarray(inputs["L0_attn_W_ukv"], np.float32)
    qg = np.asarray(inputs["L0_attn_q_ln_g"], np.float32)
    qb = np.asarray(inputs["L0_attn_q_ln_b"], np.float32)
    kg = np.asarray(inputs["L0_attn_kv_ln_g"], np.float32)
    kb = np.asarray(inputs["L0_attn_kv_ln_b"], np.float32)

    qg_h = np.ascontiguousarray(qg.reshape(4, 128).T)
    qb_h = np.ascontiguousarray(qb.reshape(4, 128).T)
    kg_h = np.ascontiguousarray(kg.reshape(4, 128).T)
    kb_h = np.ascontiguousarray(kb.reshape(4, 128).T)
    ones = np.ones((128, 1), dtype=BF)
    onesr = np.ones((1, 128), dtype=BF)
    trim = np.zeros((4, 128, TB), dtype=np.float32)
    for jr in range(4):
        for k in range(128):
            trim[jr, k, jr * 128 + k:] = 1.0
    trim = trim.astype(BF)

    in_maps = []
    for i in range(NCORES):
        sl = slice(256 * i, 256 * (i + 1))
        in_maps.append(dict(
            wdq=wdq, wdkv=wdkv,
            wuq_m=np.ascontiguousarray(wuq[:, sl]),
            wk_m=np.ascontiguousarray(wukv[:, sl]),
            wv_m=np.ascontiguousarray(wukv[:, 2048 + 256 * i: 2048 + 256 * (i + 1)]),
            qg_i=qg_h, qb_i=qb_h, kg_i=kg_h, kb_i=kb_h,
            ones_i=ones, onesr_i=onesr, trim_i=trim,
        ))
    return in_maps



GELU_A = 1.5957691216057308
GELU_B = 0.07135481282316932
TT = 2048
NMLP = 16384


def emit_B1(nc, S, G):
    o_my = G["a2a_out"]
    x_my = G["x_my"]
    woT_b = G["woT_b"]
    pk_g = G["pk_g"]; pk_u = G["pk_u"]; pk_d = G["pk_d"]
    sc_g = G["sc_g"]; sc_u = G["sc_u"]; sc_d = G["sc_d"]
    y2_my = G["y2_scr"]
    resid_scr = G["resid_scr"]; xeo_scr = G["xeo_scr"]
    inter_scr = G["inter_scr"]; wd_scr = G["wd_scr"]

    ARX = G["ARX"]; ARY = G["ARY"]; ARZW = G["ARZW"]; WU = G["WU"]
    AR1 = ARX
    AR2 = ARY
    AR3 = ARZW
    wob = [WU[:, 0:2048], WU[:, 2048:4096]]
    wdt = [WU[:, 4096:8192], WU[:, 8192:12288]]
    pkb = WU[:, 12288:12800].bitcast(I16)
    pkb2 = WU[:, 12800:13312].bitcast(I16)
    stl = [WU[:, 13312:14336].bitcast(F32), WU[:, 14336:15360].bitcast(F32)]
    cmi = WU[:, 15360:15872].bitcast(I16)
    sgn = WU[:, 15872:16384].bitcast(I16)
    cmf = WU[:, 16384:16896]
    t1s = WU[:, 16896:17408]
    t2s = WU[:, 17408:17920]
    g2b = WU[:, 17920:18432]
    csb = WU[:, 18432:18944]
    asb = WU[:, 18944:19456]
    ssb = WU[:, 19456:19968]
    hsb = WU[:, 19968:20480]
    gsb = G["sg_sb"]
    isb = G["ic_sb"]
    rt_f = G["rt_f"]; yt_f = G["yt_f"]; rec_dn = G["acc_sb"]
    fl_sb = G["fl_sb"]; ones_sb = G["ones_sb"]; onesr_sb = G["onesr_sb"]
    st_f = G["st_f"]; st_b = G["st_b"]
    pp = G["pp"]; bc = G["bc"]; stp = G["stp"]

    rt32 = ARY.bitcast(F32)
    fa_pp = [dict(), dict()]
    fa_bc = [dict(), dict()]
    fa_stp = [dict(), dict()]
    fa_stf = dict()
    fa_ar1 = dict()
    fa_rt = dict()
    fa_sq = dict()
    fa_wob = [dict(), dict()]
    fa_pkb = [dict(), dict()]
    fa_stl = [dict(), dict()]
    fa_dec = [dict(), dict()]
    fa_chain = dict()
    fa_wdt = [dict(), dict()]
    fa_ytf = dict()
    fa_rtf = dict()
    fa_isb = dict()

    w_s = S.val("s")

    def ln_stats(n_inv):
        S.wait("vector", "p")
        S.wait_fa("vector", fa_stf)
        S.emit("vector", lambda e: e.tensor_scalar_mul(st_f[0][:], stp[0][:], n_inv))
        S.emit("vector", lambda e: e.tensor_scalar_mul(st_f[1][:], stp[1][:], n_inv))
        S.emit("vector", lambda e: e.tensor_tensor(st_f[2][:], st_f[0][:], st_f[0][:], ALU.mult),
               inc="v")
        S.wait("vector", "v")
        S.emit("vector", lambda e: e.tensor_tensor(st_f[1][:], st_f[1][:], st_f[2][:], ALU.subtract))
        S.emit("vector", lambda e: e.tensor_scalar_add(st_f[1][:], st_f[1][:], EPS),
               inc="v")
        fa_stp[0]["v"] = S.val("v")
        fa_stp[1]["v"] = S.val("v")
        S.wait("scalar", "v")
        S.emit("scalar", lambda e: e.activation(st_f[1][:], st_f[1][:], AF.Ln))
        S.emit("scalar", lambda e: e.activation(st_f[1][:], st_f[1][:], AF.Exp, scale=-0.5),
               inc="a")
        S.wait("vector", "a")
        S.emit("vector", lambda e: e.tensor_copy(st_b[0][:], st_f[0][:]))
        S.emit("vector", lambda e: e.tensor_copy(st_b[1][:], st_f[1][:]), inc="v")
        S.wait("tensor", "v")
        S.wait_fa("tensor", fa_bc[0])
        S.wait_fa("tensor", fa_bc[1])
        S.emit("tensor", lambda e: e.matmul(bc[0][:], onesr_sb[:], st_b[0][:],
                                            start=True, stop=True))
        S.emit("tensor", lambda e: e.matmul(bc[1][:], onesr_sb[:], st_b[1][:],
                                            start=True, stop=True), inc="p")
        fa_stf["p"] = S.val("p")

    S.phase = "B_wo"
    for oc in range(16):
        S.dma("sync", AR1[:, oc * TT:(oc + 1) * TT],
              o_my[oc * 128:(oc + 1) * 128, :], "s")
    o_loaded = S.val("s")
    S.dma("gpsimd", wob[0].rearrange("p (oc f) -> p oc f", f=128),
          woT_b[0].rearrange("(oc p) f -> p oc f", p=128), "g")

    S.wait("tensor", "s", w_s)

    for tb4 in range(4):
        tcol = tb4 * TB
        for dout in range(16):
            wb = dout % 2
            nxt = dout + 1 if dout < 15 else (0 if tb4 < 3 else None)
            S.wait("tensor", "g", S.val("g"))
            if dout == 0:
                S.wait("tensor", "s", o_loaded)
            bank = dout % 2
            S.wait_fa("tensor", fa_pp[bank])
            for oc in range(16):
                S.emit("tensor", lambda e, oc=oc, dout=dout, wb=wb, bank=bank, tcol=tcol: e.matmul(
                    pp[bank][:],
                    wob[wb][:, oc * 128:(oc + 1) * 128],
                    AR1[:, oc * TT + tcol: oc * TT + tcol + TB],
                    start=(oc == 0), stop=(oc == 15)),
                    inc=("p" if oc == 15 else None))
            p_wo = S.val("p")
            if nxt is not None:
                S.wait_fa("gpsimd", fa_wob[nxt % 2])
                S.dma("gpsimd", wob[nxt % 2].rearrange("p (oc f) -> p oc f", f=128),
                      woT_b[nxt].rearrange("(oc p) f -> p oc f", p=128), "g")
            S.wait_fa("sync", fa_rtf)
            S.dma("sync", rt_f[:], x_my[dout * 128:(dout + 1) * 128, tcol:tcol + TB], "s")
            x_ld = S.val("s")
            S.wait("vector", "p", p_wo)
            S.wait("vector", "s", x_ld)
            S.wait_fa("vector", fa_rt)
            S.emit("vector", lambda e, dout=dout, bank=bank: e.tensor_tensor(
                rt32[:, dout * TB:(dout + 1) * TB], pp[bank][:], rt_f[:], ALU.add),
                inc="v")
            v_r = S.val("v")
            fa_pp[bank] = {"v": v_r}
            fa_rtf = {"v": v_r}
            fa_wob[wb] = {"p": p_wo}
        S.wait("vector", "v")
        S.emit("vector", lambda e: e.tensor_copy(fl_sb[:], AR2[:, 0:8]), inc="v")
        v_rall = S.val("v")
        S.wait("sync", "v", v_rall)
        for dout in range(16):
            S.dma("sync", resid_scr[dout * 128:(dout + 1) * 128, tcol:tcol + TB],
                  rt32[:, dout * TB:(dout + 1) * TB], "rs")
        fa_rt = {"rs": S.val("rs")}
        S.wait("scalar", "v", v_rall)
        S.wait_fa("scalar", fa_sq)
        for c in range(16):
            S.emit("scalar", lambda e, c=c: e.activation(
                AR3[:, c * TB:(c + 1) * TB], rt32[:, c * TB:(c + 1) * TB],
                AF.Square), inc=("a" if c == 15 else None))
        a_sq = S.val("a")
        S.wait("scalar", "v", v_rall)
        for c in range(16):
            S.emit("scalar", lambda e, c=c: e.activation(
                AR3[:, (16 + c) * TB:(17 + c) * TB], rt32[:, c * TB:(c + 1) * TB],
                AF.Identity), inc=("a" if c == 15 else None))
        a_cp = S.val("a")
        S.wait("tensor", "a", a_cp)
        S.wait_fa("tensor", fa_stp[0])
        for c in range(16):
            S.emit("tensor", lambda e, c=c: e.matmul(
                stp[0][:], ones_sb[:], AR3[:, (16 + c) * TB:(17 + c) * TB],
                start=(c == 0), stop=(c == 15)), inc=("p" if c == 15 else None))
        S.wait("tensor", "a", a_sq)
        S.wait_fa("tensor", fa_stp[1])
        for c in range(16):
            S.emit("tensor", lambda e, c=c: e.matmul(
                stp[1][:], ones_sb[:], AR3[:, c * TB:(c + 1) * TB],
                start=(c == 0), stop=(c == 15)), inc=("p" if c == 15 else None))
        ln_stats(1.0 / D)
        S.wait("vector", "p")
        for c in range(16):
            S.emit("vector", lambda e, c=c: e.tensor_tensor(
                AR3[:, c * TB:(c + 1) * TB], rt32[:, c * TB:(c + 1) * TB],
                bc[0][:], ALU.subtract))
            S.emit("vector", lambda e, c=c: e.tensor_tensor(
                AR3[:, c * TB:(c + 1) * TB], AR3[:, c * TB:(c + 1) * TB],
                bc[1][:], ALU.mult), inc=("v" if c == 15 else None))
        S.wait("vector", "v")
        S.emit("vector", lambda e: e.tensor_copy(fl_sb[:], AR3[:, 0:8]), inc="v")
        v_xln = S.val("v")
        fa_bc[0] = {"v": v_xln}
        fa_bc[1] = {"v": v_xln}
        fa_rt["v"] = v_xln
        S.wait("sync", "v", v_xln)
        for c in range(16):
            base = 128 * c
            if c % 2 == 0:
                ev0, od0 = base, base + 128
            else:
                ev0, od0 = base - 64, base + 64
            S.dma("sync", xeo_scr[ev0:ev0 + 64, tcol:tcol + TB],
                  AR3[0:128:2, c * TB:(c + 1) * TB], "rs")
            S.dma("sync", xeo_scr[od0:od0 + 64, tcol:tcol + TB],
                  AR3[1:128:2, c * TB:(c + 1) * TB], "rs")
        fa_sq = {"rs": S.val("rs")}
    ph1_done_rs = S.val("rs")

    def decode_tile(src_i16, stile, out_lo, out_hi):
        S.emit("vector", lambda e: e.tensor_single_scalar(cmi[:], src_i16, 7, ALU.bitwise_and))
        S.emit("vector", lambda e: e.tensor_copy(cmf[:], cmi[:]))
        S.emit("vector", lambda e: e.tensor_scalar(t1s[:], cmf[:], 2.0, -8.0, ALU.mult, ALU.add))
        S.emit("vector", lambda e: e.scalar_tensor_tensor(t2s[:], cmf[:], -2.0, t1s[:], ALU.add, ALU.max))
        S.emit("vector", lambda e: e.scalar_tensor_tensor(t1s[:], cmf[:], 0.5, t2s[:], ALU.mult, ALU.max))
        S.emit("vector", lambda e, o=out_lo: e.tensor_tensor(o, t1s[:], stile, ALU.mult))
        S.emit("vector", lambda e: e.tensor_scalar(sgn[:], src_i16, 8, 12, ALU.bitwise_and, ALU.logical_shift_left))
        S.emit("vector", lambda e, o=out_lo: e.tensor_tensor(o.bitcast(I16), o.bitcast(I16), sgn[:], ALU.bitwise_or))
        S.emit("vector", lambda e: e.tensor_scalar(cmi[:], src_i16, 4, 7, ALU.logical_shift_right, ALU.bitwise_and))
        S.emit("vector", lambda e: e.tensor_copy(cmf[:], cmi[:]))
        S.emit("vector", lambda e: e.tensor_scalar(t1s[:], cmf[:], 2.0, -8.0, ALU.mult, ALU.add))
        S.emit("vector", lambda e: e.scalar_tensor_tensor(t2s[:], cmf[:], -2.0, t1s[:], ALU.add, ALU.max))
        S.emit("vector", lambda e: e.scalar_tensor_tensor(t1s[:], cmf[:], 0.5, t2s[:], ALU.mult, ALU.max))
        S.emit("vector", lambda e, o=out_hi: e.tensor_tensor(o, t1s[:], stile, ALU.mult))
        S.emit("vector", lambda e: e.tensor_scalar(sgn[:], src_i16, 128, 8, ALU.bitwise_and, ALU.logical_shift_left))
        S.emit("vector", lambda e, o=out_hi: e.tensor_tensor(o.bitcast(I16), o.bitcast(I16), sgn[:], ALU.bitwise_or),
               inc="v")

    S.phase = "B_gateup"
    pkb_d = [G["S_sb"].bitcast(I16), G["P_sb"].bitcast(I16)]
    ce_sb_f = G["ce_sb"]
    stl_d = [ce_sb_f[:, 0:512], ce_sb_f[:, 512:1024]]
    wdlo_t, wdhi_t = G["tmp_sb"][0], G["tmp_sb"][1]
    fa_pkd = [dict(), dict()]
    fa_stld = [dict(), dict()]
    fa_dec[0] = {"rs": ph1_done_rs}

    def emit_wd_group(idx):
        jd, hb = divmod(idx, 4)
        sbuf_i = idx % 2
        S.wait_fa("gpsimd", fa_pkd[sbuf_i])
        S.dma("gpsimd", pkb_d[sbuf_i][:],
              pk_d[jd * 128:(jd + 1) * 128, hb * TB:(hb + 1) * TB], "g")
        S.wait_fa("gpsimd", fa_stld[sbuf_i])
        for r in range(8):
            S.dma("gpsimd", stl_d[sbuf_i][16 * r:16 * (r + 1), :],
                  sc_d[8 * jd + r: 8 * jd + r + 1, hb * TB:(hb + 1) * TB]
                  .broadcast_to([16, TB]), "g")
        g_now = S.val("g")
        S.wait("vector", "g", g_now)
        S.wait_fa("vector", fa_dec[0])
        decode_tile(pkb_d[sbuf_i][:], stl_d[sbuf_i][:], wdlo_t[:], wdhi_t[:])
        v_dec = S.val("v")
        fa_pkd[sbuf_i] = {"v": v_dec}
        fa_stld[sbuf_i] = {"v": v_dec}
        S.wait("vector", "v", v_dec)
        S.emit("vector", lambda e: e.tensor_copy(fl_sb[:], wdlo_t[:, 0:8]), inc="v")
        v_dec = S.val("v")
        S.wait("sync", "v", v_dec)
        for k in range(4):
            hc = 4 * hb + k
            S.dma("sync", wd_scr[hc, 256 * jd: 256 * jd + 256: 2, :],
                  wdlo_t[:, k * 128:(k + 1) * 128], "ws")
            S.dma("sync", wd_scr[hc, 256 * jd + 1: 256 * jd + 256: 2, :],
                  wdhi_t[:, k * 128:(k + 1) * 128], "ws")
        fa_dec[0] = {"ws": S.val("ws")}

    S.wait("sync", "rs", ph1_done_rs)
    S.wait_fa("sync", fa_ar1)
    S.wait("sync", "p")
    for c in range(16):
        S.dma("sync", AR1[:, c * TT:(c + 1) * TT],
              xeo_scr[c * 128:(c + 1) * 128, :], "s")
    xeo_loaded = S.val("s")
    S.wait("tensor", "s", xeo_loaded)

    wdec_g = [AR2[:, 0:8192], AR3[:, 0:8192]]
    wdec_u = [AR2[:, 8192:16384], AR3[:, 8192:16384]]
    fa_w = [dict(), {"rs": ph1_done_rs}]

    def emit_gu_chunk(nb, ci, war_first):
        mi, jc = divmod(ci, 8)
        pk, scm = ((pk_g, sc_g), (pk_u, sc_u))[mi]
        wd = (wdec_g if mi == 0 else wdec_u)[nb % 2]
        sbuf_i = jc % 2
        S.wait_fa("gpsimd", fa_pkb[sbuf_i])
        S.dma("gpsimd", (pkb if sbuf_i == 0 else pkb2)
              .rearrange("p (s w) -> p s w", w=128),
              pk[4 * nb:4 * nb + 4, 128 * jc:128 * (jc + 1), :]
              .rearrange("s p w -> p s w"), "g")
        S.wait_fa("gpsimd", fa_stl[sbuf_i])
        for r in range(8):
            S.dma("gpsimd", stl[sbuf_i][16 * r:16 * (r + 1), :],
                  scm[8 * jc + r: 8 * jc + r + 1,
                      TB * nb: TB * (nb + 1)].broadcast_to([16, TB]), "g")
        g_now = S.val("g")
        S.wait("vector", "g", g_now)
        if war_first:
            S.wait_fa("vector", fa_w[nb % 2])
            S.wait("vector", "p", fa_w[nb % 2].get("p", 0))
        decode_tile((pkb if sbuf_i == 0 else pkb2)[:],
                    stl[sbuf_i][:],
                    wd[:, (2 * jc) * TB:(2 * jc + 1) * TB],
                    wd[:, (2 * jc + 1) * TB:(2 * jc + 2) * TB])
        v_dec = S.val("v")
        fa_pkb[sbuf_i] = {"v": v_dec}
        fa_stl[sbuf_i] = {"v": v_dec}

    for ci in range(16):
        emit_gu_chunk(0, ci, ci == 0)
    v_mark = S.val("v")
    wd_idx = 0

    for nb in range(32):
        buf = nb % 2
        S.wait("tensor", "v", v_mark)
        gu_ci = 0
        for ns in range(4):
            for tb4 in range(4):
                unit = ns * 4 + tb4
                tcol = tb4 * TB
                nrow = TB * nb + 128 * ns
                S.wait_fa("tensor", fa_pp[0])
                for kc in range(16):
                    S.emit("tensor", lambda e, kc=kc, ns=ns, buf=buf, tcol=tcol: e.matmul(
                        pp[0][:],
                        wdec_g[buf][:, kc * TB + ns * 128: kc * TB + ns * 128 + 128],
                        AR1[:, kc * TT + tcol: kc * TT + tcol + TB],
                        start=(kc == 0), stop=(kc == 15)),
                        inc=("p" if kc == 15 else None))
                p_g = S.val("p")
                S.wait_fa("tensor", fa_pp[1])
                for kc in range(16):
                    S.emit("tensor", lambda e, kc=kc, ns=ns, buf=buf, tcol=tcol: e.matmul(
                        pp[1][:],
                        wdec_u[buf][:, kc * TB + ns * 128: kc * TB + ns * 128 + 128],
                        AR1[:, kc * TT + tcol: kc * TT + tcol + TB],
                        start=(kc == 0), stop=(kc == 15)),
                        inc=("p" if kc == 15 else None))
                p_u = S.val("p")
                S.wait("scalar", "p", p_g)
                S.wait_fa("scalar", fa_chain)
                S.emit("scalar", lambda e: e.activation(gsb[:], pp[0][:], AF.Identity), inc="a")
                a_g = S.val("a")
                fa_pp[0] = {"a": a_g}
                S.emit("scalar", lambda e: e.activation(g2b[:], gsb[:], AF.Square), inc="a")
                a_g2 = S.val("a")
                S.wait("vector", "a", a_g2)
                S.emit("vector", lambda e: e.tensor_scalar(
                    csb[:], g2b[:], GELU_B, GELU_A, ALU.mult, ALU.add))
                S.emit("vector", lambda e: e.tensor_tensor(asb[:], csb[:], gsb[:], ALU.mult),
                       inc="v")
                v_arg = S.val("v")
                S.wait("scalar", "v", v_arg)
                S.emit("scalar", lambda e: e.activation(ssb[:], asb[:], AF.Sigmoid), inc="a")
                a_s = S.val("a")
                S.wait("vector", "a", a_s)
                S.emit("vector", lambda e: e.tensor_tensor(hsb[:], ssb[:], gsb[:], ALU.mult))
                S.wait("vector", "p", p_u)
                S.wait_fa("vector", fa_isb)
                S.emit("vector", lambda e: e.tensor_tensor(isb[:], hsb[:], pp[1][:], ALU.mult),
                       inc="v")
                v_i = S.val("v")
                fa_pp[1] = {"v": v_i}
                fa_chain = {"v": v_i}
                S.wait("vector", "v", v_i)
                S.emit("vector", lambda e: e.tensor_copy(fl_sb[:], isb[:, 0:8]), inc="v")
                v_i2 = S.val("v")
                S.wait("sync", "v", v_i2)
                S.dma("sync", inter_scr[nrow:nrow + 128, tcol:tcol + TB], isb[:], "is")
                fa_chain["is"] = S.val("is")
                fa_isb = {"is": S.val("is")}
                if nb < 31 and gu_ci < 16:
                    emit_gu_chunk(nb + 1, gu_ci, gu_ci == 0)
                    gu_ci += 1
                if unit % 2 == 1 and wd_idx < 256:
                    emit_wd_group(wd_idx)
                    wd_idx += 1
        fa_w[buf] = {"p": S.val("p")}
        v_mark = S.val("v")

    is_all = S.val("is")
    ws_all = S.val("ws")

    S.phase = "B_down"
    def stripe_tile(nck):
        if nck < 64:
            return AR1[:, nck * TB:(nck + 1) * TB]
        if nck < 96:
            return AR2[:, (nck - 64) * TB:(nck - 63) * TB]
        return AR3[:, (nck - 96) * TB:(nck - 95) * TB]

    for eng in ("sync", "gpsimd", "tensor", "vector", "scalar"):
        S.wait(eng, "is", is_all)
        S.wait(eng, "ws", ws_all)
        S.wait(eng, "p")
        S.wait(eng, "v")
        S.wait(eng, "a")
        S.wait(eng, "rs")
    fa_stripe = dict()
    for tb4 in range(4):
        tcol = tb4 * TB
        S.wait_fa("sync", fa_stripe)
        for nck in range(128):
            S.dma("sync", stripe_tile(nck),
                  inter_scr[nck * 128:(nck + 1) * 128, tcol:tcol + TB], "s")
        stripe_loaded = S.val("s")
        S.wait("tensor", "s", stripe_loaded)
        for hc in range(16):
            bank = hc % 2
            S.wait_fa("tensor", fa_pp[bank])
            for g32 in range(4):
                wbuf = g32 % 2
                S.wait_fa("sync", fa_wdt[wbuf])
                S.dma("sync", wdt[wbuf].rearrange("p (a f) -> p a f", f=128),
                      wd_scr[hc, g32 * 4096:(g32 + 1) * 4096, :]
                      .rearrange("(a p) f -> p a f", p=128), "wl")
                wl_now = S.val("wl")
                S.wait("tensor", "wl", wl_now)
                for a32 in range(32):
                    nck = 32 * g32 + a32
                    S.emit("tensor", lambda e, a32=a32, nck=nck, wbuf=wbuf, bank=bank: e.matmul(
                        pp[bank][:], wdt[wbuf][:, a32 * 128:(a32 + 1) * 128],
                        stripe_tile(nck),
                        start=(nck == 0), stop=(nck == 127)),
                        inc=("p" if a32 == 31 else None))
                fa_wdt[wbuf] = {"p": S.val("p")}
            p_dn = S.val("p")
            S.wait_fa("sync", fa_rtf)
            S.dma("sync", rt_f[:], resid_scr[hc * 128:(hc + 1) * 128, tcol:tcol + TB], "s")
            r_ld = S.val("s")
            S.wait("vector", "p", p_dn)
            S.wait("vector", "s", r_ld)
            S.wait_fa("vector", fa_ytf)
            S.emit("vector", lambda e, bank=bank: e.tensor_tensor(
                yt_f[:], pp[bank][:], rt_f[:], ALU.add), inc="v")
            v_y = S.val("v")
            fa_pp[bank] = {"v": v_y}
            fa_rtf = {"v": v_y}
            S.wait("sync", "v", v_y)
            S.dma("sync", y2_my[hc * 128:(hc + 1) * 128, tcol:tcol + TB],
                  yt_f[:], "ys")
            fa_ytf = {"ys": S.val("ys")}
        fa_stripe = {"p": S.val("p")}

    S.wait("sync", "ys", S.val("ys"))
    S.wait("gpsimd", "ys", S.val("ys"))


def token_cols(i):
    return np.concatenate([np.arange(b * SL + 512 * i, b * SL + 512 * (i + 1))
                           for b in range(B)])


def stage_B1(inputs, x_fm):
    wo = np.asarray(inputs["L0_attn_W_o"], np.float32)
    woT_b = np.ascontiguousarray(wo.T.reshape(D, 16, 128).transpose(1, 0, 2))

    def unpack16(a32):
        return np.ascontiguousarray(np.asarray(a32, np.int32).view(np.int16)[:, ::2])

    g16 = unpack16(inputs["L0_ffn_gate_W_packed"])
    u16 = unpack16(inputs["L0_ffn_up_W_packed"])
    d16 = unpack16(inputs["L0_ffn_down_W_packed"])
    pk_g = np.ascontiguousarray(g16.T.reshape(1024, 128, 128).transpose(1, 0, 2))
    pk_u = np.ascontiguousarray(u16.T.reshape(1024, 128, 128).transpose(1, 0, 2))
    pk_d = np.ascontiguousarray(d16.T)
    sc_g = np.ascontiguousarray(np.asarray(inputs["L0_ffn_gate_scales"], np.float32).T)
    sc_u = np.ascontiguousarray(np.asarray(inputs["L0_ffn_up_scales"], np.float32).T)
    sc_d = np.ascontiguousarray(np.asarray(inputs["L0_ffn_down_scales"], np.float32).T)

    in_maps = []
    for i in range(NCORES):
        cols = token_cols(i)
        in_maps.append(dict(
            x_my=np.ascontiguousarray(x_fm[:, cols]),
            woT_b=woT_b, pk_g=pk_g, pk_u=pk_u, pk_d=pk_d,
            sc_g=sc_g, sc_u=sc_u, sc_d=sc_d,
        ))
    return in_maps



def emit_CD(nc, S, G):
    y2_my = G["y2_scr"]
    wqkvT_b = G["wqkvT_b"]; woT_b = G["wo1T_b"]; rw_i = G["rw_i"]
    wg_b = G["wg_b"]; wu_b = G["wu_b"]; wd_b = G["wd_b"]
    swg_b = G["swg_b"]; swu_b = G["swu_b"]; swd_b = G["swd_b"]
    e2_i = G["e2_i"]; idb_i = G["idb_i"]; idf_i = G["idf_i"]; alib_i = G["alib_i"]
    out_my = G["out_my"]
    y3_scr = G["y3_scr"]; attn_scr = G["attn_scr"]; p_scr = G["p_scr"]
    cb_scr = G["cb_scr"]; ic_scr = G["ic_scr"]; ish_scr = G["ish_scr"]

    ARX = G["ARX"]; ARY = G["ARY"]
    ARZ = G["ARZW"][:, 0:8192]
    ARW = G["ARZW"][:, 8192:16384]
    WU = G["WU"]
    wqb = [WU[:, 0:2048], WU[:, 2048:4096]]
    wgu = [WU[:, 4096:8192], WU[:, 8192:12288]]
    wdn = [WU[:, 0:10240], WU[:, 10240:20480]]
    S_sb = G["S_sb"]; sT_sb = G["sT_sb"]; P_sb = G["P_sb"]
    tmp_sb = G["tmp_sb"]; pb_sb = G["pb_sb"]
    acc_sb = G["acc_sb"]; tmp2_sb = G["tmp2_sb"]
    rt_f = G["rt_f"]; yt_f = G["yt_f"]
    fl_sb = G["fl_sb"]; ones_sb = G["ones_sb"]; onesr_sb = G["onesr_sb"]
    e2_sb = G["e2_sb"]; idb_sb = G["idb_sb"]; idf_sb = G["idf_sb"]
    alib_sb = G["alib_sb"]; rw_sb = G["rw_sb"]
    nmx = G["nmx"]; rec1 = G["rec1"]
    ro_sb = G["ro_sb"]; e_sb = G["e_sb"]
    m1_sb = G["m1_sb"]; m2_sb = G["m2_sb"]; den_sb = G["den_sb"]
    eq_sb = G["eq_sb"]; me_sb = G["me_sb"]; en_sb = G["en_sb"]
    comb_sb = G["comb_sb"]; combT_sb = G["combT_sb"]
    ce_sb = G["ce_sb"]; sg_sb = G["sg_sb"]; ic_sb = G["ic_sb"]
    st_f = G["st_f"]; st_b = G["st_b"]
    sq_sb = G["sq_sb"]; xb_sb = G["xb_sb"]
    pp = G["pp"]; bc = G["bc"]; stp = G["stp"]
    ptb = G["ptb"]; ptf = G["ptf"]
    fa_pp = [dict(), dict()]
    fa_bc = [dict(), dict()]
    fa_stp = [dict(), dict()]
    fa_stf = dict()
    fa_rtf = dict()
    fa_ytf = dict()
    fa_wqb = [dict(), dict()]
    fa_wgu = [dict(), dict()]
    fa_wdn = [dict(), dict()]
    fa_sqxb = dict()
    fa_tmp = [dict(), dict()]
    fa_Ssb = dict()
    fa_sT = dict()
    fa_Psb = dict()
    fa_pb = dict()
    fa_acc = dict()
    fa_icsb = dict()
    fa_sg = dict()
    fa_ptb = dict()
    fa_ptf = dict()
    fa_arx = dict()
    fa_ary = dict()

    S.dma("sync", e2_sb[:], e2_i, "s")
    S.dma("sync", idb_sb[:], idb_i, "s")
    S.dma("sync", idf_sb[:], idf_i, "s")
    S.dma("sync", alib_sb[:], alib_i, "s")
    S.dma("gpsimd", rw_sb.rearrange("p (a f) -> p a f", f=8),
          rw_i.rearrange("(a p) f -> p a f", p=128), "g")
    w_s = S.val("s")
    S.wait("tensor", "s", w_s)
    S.wait("scalar", "s", w_s)
    S.wait("vector", "s", w_s)

    def ln_stats(n_inv):
        S.wait("vector", "p")
        S.wait_fa("vector", fa_stf)
        S.emit("vector", lambda e: e.tensor_scalar_mul(st_f[0][:], stp[0][:], n_inv))
        S.emit("vector", lambda e: e.tensor_scalar_mul(st_f[1][:], stp[1][:], n_inv))
        S.emit("vector", lambda e: e.tensor_tensor(st_f[2][:], st_f[0][:], st_f[0][:], ALU.mult),
               inc="v")
        S.wait("vector", "v")
        S.emit("vector", lambda e: e.tensor_tensor(st_f[1][:], st_f[1][:], st_f[2][:], ALU.subtract))
        S.emit("vector", lambda e: e.tensor_scalar_add(st_f[1][:], st_f[1][:], EPS),
               inc="v")
        fa_stp[0]["v"] = S.val("v")
        fa_stp[1]["v"] = S.val("v")
        S.wait("scalar", "v")
        S.emit("scalar", lambda e: e.activation(st_f[1][:], st_f[1][:], AF.Ln))
        S.emit("scalar", lambda e: e.activation(st_f[1][:], st_f[1][:], AF.Exp, scale=-0.5),
               inc="a")
        S.wait("vector", "a")
        S.emit("vector", lambda e: e.tensor_copy(st_b[0][:], st_f[0][:]))
        S.emit("vector", lambda e: e.tensor_copy(st_b[1][:], st_f[1][:]), inc="v")
        S.wait("tensor", "v")
        S.wait_fa("tensor", fa_bc[0])
        S.wait_fa("tensor", fa_bc[1])
        S.emit("tensor", lambda e: e.matmul(bc[0][:], onesr_sb[:], st_b[0][:],
                                            start=True, stop=True))
        S.emit("tensor", lambda e: e.matmul(bc[1][:], onesr_sb[:], st_b[1][:],
                                            start=True, stop=True), inc="p")
        fa_stf["p"] = S.val("p")

    def ln_to_arx(src_dram, fa_dst_extra=None):
        nonlocal fa_sqxb
        for tb4 in range(4):
            tcol = tb4 * TB
            for c in range(16):
                S.wait_fa("gpsimd", fa_sqxb)
                S.dma("gpsimd", xb_sb[:], src_dram[c * 128:(c + 1) * 128, tcol:tcol + TB], "g")
                g_x = S.val("g")
                S.wait("scalar", "g", g_x)
                S.wait("scalar", "v")
                S.emit("scalar", lambda e: e.activation(sq_sb[:], xb_sb[:], AF.Square),
                       inc="a")
                a_sq = S.val("a")
                S.wait("tensor", "g", g_x)
                if c == 0:
                    S.wait_fa("tensor", fa_stp[0])
                    S.wait_fa("tensor", fa_stp[1])
                S.emit("tensor", lambda e, c=c: e.matmul(
                    stp[0][:], ones_sb[:], xb_sb[:],
                    start=(c == 0), stop=(c == 15)),
                    inc=("p" if c == 15 else None))
                S.wait("tensor", "a", a_sq)
                S.emit("tensor", lambda e, c=c: e.matmul(
                    stp[1][:], ones_sb[:], sq_sb[:],
                    start=(c == 0), stop=(c == 15)),
                    inc=("p" if c == 15 else None))
                fa_sqxb = {"p": S.val("p")}
            ln_stats(1.0 / D)
            for c in range(16):
                S.wait_fa("gpsimd", fa_sqxb)
                S.dma("gpsimd", xb_sb[:], src_dram[c * 128:(c + 1) * 128, tcol:tcol + TB], "g")
                g_x = S.val("g")
                S.wait("vector", "g", g_x)
                S.wait("vector", "p")
                S.emit("vector", lambda e: e.tensor_tensor(sq_sb[:], xb_sb[:], bc[0][:],
                                                           ALU.subtract), inc="v")
                S.wait("vector", "v")
                S.emit("vector", lambda e, c=c, tcol=tcol: e.tensor_tensor(
                    ARX[:, c * 2048 + tcol: c * 2048 + tcol + TB],
                    sq_sb[:], bc[1][:], ALU.mult), inc="v")
                fa_sqxb = {"v": S.val("v")}
            v_ln = S.val("v")
            fa_bc[0] = {"v": v_ln}
            fa_bc[1] = {"v": v_ln}
        return S.val("v")


    S.phase = "C_ln"
    v_xln = ln_to_arx(y2_my)
    S.wait("tensor", "v", v_xln)

    S.phase = "C_attn"
    pairs10 = [(a, b) for a in range(4) for b in range(a + 1)]
    fa_ary = dict()
    for stk in range(2):
        ocs = list(range(stk * 8, stk * 8 + 8)) + \
              [16 + 2 * g for g in (2 * stk, 2 * stk + 1)] + \
              [17 + 2 * g for g in (2 * stk, 2 * stk + 1)]
        S.wait("gpsimd", "p", fa_ary.get("p", 0))
        S.dma("gpsimd", wqb[0].rearrange("p (a f) -> p a f", f=128),
              wqkvT_b[ocs[0]].rearrange("(a p) f -> p a f", p=128), "g")
        for ii, oc in enumerate(ocs):
            wb = ii % 2
            S.wait("tensor", "g", S.val("g"))
            if ii + 1 < len(ocs):
                S.wait_fa("gpsimd", fa_wqb[(ii + 1) % 2])
                S.dma("gpsimd", wqb[(ii + 1) % 2].rearrange("p (a f) -> p a f", f=128),
                      wqkvT_b[ocs[ii + 1]].rearrange("(a p) f -> p a f", p=128), "g")
            for tb4 in range(4):
                tcol = tb4 * TB
                bank = tb4 % 2
                S.wait_fa("tensor", fa_pp[bank])
                if ii == 0 and tb4 == 0:
                    S.wait_fa("tensor", fa_ary)
                for dc in range(16):
                    S.emit("tensor", lambda e, dc=dc, wb=wb, bank=bank, tcol=tcol: e.matmul(
                        pp[bank][:], wqb[wb][:, dc * 128:(dc + 1) * 128],
                        ARX[:, dc * 2048 + tcol: dc * 2048 + tcol + TB],
                        start=(dc == 0), stop=(dc == 15)),
                        inc=("p" if dc == 15 else None))
                p_q = S.val("p")
                if oc < 16:
                    dst = ARY[:, (oc - stk * 8) * 2048 + tcol:
                              (oc - stk * 8) * 2048 + tcol + TB]
                elif (oc - 16) % 2 == 0:
                    gl = (oc - 16) // 2 - 2 * stk
                    dst = ARZ[:, gl * 2048 + tcol: gl * 2048 + tcol + TB]
                else:
                    gl = (oc - 16) // 2 - 2 * stk
                    dst = ARW[:, gl * 2048 + tcol: gl * 2048 + tcol + TB]
                S.wait("scalar", "p", p_q)
                S.wait("scalar", "v", fa_ary.get("v", 0))
                S.emit("scalar", lambda e, dst=dst, bank=bank: e.activation(
                    dst, pp[bank][:], AF.Identity), inc="a")
                fa_pp[bank] = {"a": S.val("a")}
            fa_wqb[wb] = {"p": S.val("p")}
        a_qkv = S.val("a")

        S.wait("tensor", "a", a_qkv)
        S.wait_fa("tensor", fa_pp[0])
        r = 0
        for hh in range(8):
            h = stk * 8 + hh
            gl = hh // 4
            for (aa, bb) in pairs10:
                tb_ = r % 2
                S.wait("vector", "a", a_qkv)
                S.wait_fa("vector", fa_tmp[tb_])
                S.emit("vector", lambda e, hh=hh, gl=gl, aa=aa, bb=bb, tb_=tb_: e.tensor_tensor(
                    tmp_sb[tb_][:],
                    ARY[:, hh * 2048 + aa * TB: hh * 2048 + (aa + 1) * TB],
                    ARZ[:, gl * 2048 + bb * TB: gl * 2048 + (bb + 1) * TB],
                    ALU.mult), inc="v")
                v_t = S.val("v")
                S.wait("tensor", "v", v_t)
                S.emit("tensor", lambda e, r=r, tb_=tb_: e.matmul(
                    pp[0][:], e2_sb[:, 128 - r:256 - r], tmp_sb[tb_][:],
                    start=(r == 0), stop=(r == 79)),
                    inc="p")
                fa_tmp[tb_] = {"p": S.val("p")}
                r += 1
        p_sc = S.val("p")
        S.wait("scalar", "p", p_sc)
        S.wait_fa("scalar", fa_Ssb)
        S.emit("scalar", lambda e, stk=stk: e.activation(
            S_sb[:], pp[0][:], AF.Identity,
            bias=alib_sb[:, stk:stk + 1], scale=ISQ), inc="a")
        a_S = S.val("a")
        fa_pp[0] = {"a": a_S}
        offs = [0, 1, 3, 6]
        for tsub in range(4):
            S.wait("tensor", "a", a_S)
            S.wait_fa("tensor", fa_ptb)
            S.emit("tensor", lambda e, tsub=tsub: e.transpose(
                ptb[:], S_sb[:, tsub * 128:(tsub + 1) * 128], idb_sb[:]), inc="p")
            p_t = S.val("p")
            S.wait("scalar", "p", p_t)
            S.wait_fa("scalar", fa_sT)
            S.emit("scalar", lambda e: e.activation(sT_sb[:, 0:128], ptb[:], AF.Identity),
                   inc="a")
            a_sT = S.val("a")
            fa_ptb = {"a": a_sT}
            S.wait("vector", "a", a_sT)
            for grp in range(4):
                base = grp * 8
                for k in range(8):
                    idx = base + k
                    hh2, aa2 = idx // 4, idx % 4
                    c2 = hh2 * 10 + offs[aa2]
                    w2 = aa2 + 1
                    S.emit("vector", lambda e, c2=c2, w2=w2, k=k: e.reduce_max(
                        nmx[:, k:k + 1], sT_sb[:, c2:c2 + w2], axis=AX.X, negate=True),
                        inc=("v" if k == 7 else None))
                S.wait("scalar", "v")
                for k in range(8):
                    idx = base + k
                    hh2, aa2 = idx // 4, idx % 4
                    c2 = hh2 * 10 + offs[aa2]
                    w2 = aa2 + 1
                    S.emit("scalar", lambda e, c2=c2, w2=w2, k=k: e.activation(
                        sT_sb[:, c2:c2 + w2], sT_sb[:, c2:c2 + w2], AF.Exp,
                        bias=nmx[:, k:k + 1]),
                        inc=("a" if k == 7 else None))
                S.wait("vector", "a")
                for k in range(8):
                    idx = base + k
                    hh2, aa2 = idx // 4, idx % 4
                    c2 = hh2 * 10 + offs[aa2]
                    w2 = aa2 + 1
                    S.emit("vector", lambda e, c2=c2, w2=w2, k=k: e.reduce_sum(
                        rec1[:, k:k + 1], sT_sb[:, c2:c2 + w2], axis=AX.X),
                        inc=("v" if k == 7 else None))
                S.wait("vector", "v")
                S.emit("vector", lambda e: e.reciprocal(rec1[:], rec1[:]), inc="v")
                S.wait("vector", "v")
                for k in range(8):
                    idx = base + k
                    hh2, aa2 = idx // 4, idx % 4
                    c2 = hh2 * 10 + offs[aa2]
                    w2 = aa2 + 1
                    S.emit("vector", lambda e, c2=c2, w2=w2, k=k: e.tensor_single_scalar(
                        sT_sb[:, c2:c2 + w2], sT_sb[:, c2:c2 + w2],
                        rec1[:, k:k + 1], ALU.mult),
                        inc=("v" if k == 7 else None))
            v_sm = S.val("v")
            S.wait("vector", "v", v_sm)
            S.emit("vector", lambda e: e.tensor_copy(fl_sb[:], sT_sb[:, 0:8]), inc="v")
            v_fl = S.val("v")
            S.wait("tensor", "v", v_fl)
            S.wait_fa("tensor", fa_ptb)
            S.emit("tensor", lambda e: e.transpose(ptb[:], sT_sb[:, 0:128], idb_sb[:]),
                   inc="p")
            p_t2 = S.val("p")
            S.wait("scalar", "p", p_t2)
            S.wait_fa("scalar", fa_Psb)
            S.emit("scalar", lambda e, tsub=tsub: e.activation(
                P_sb[:, tsub * 128:(tsub + 1) * 128], ptb[:], AF.Identity), inc="a")
            a_P = S.val("a")
            fa_ptb = {"a": a_P}
            fa_sT = {"a": a_P}
        a_Pall = S.val("a")
        S.wait("scalar", "a", a_Pall)
        S.emit("scalar", lambda e: e.activation(fl_sb[:], P_sb[:, 0:8], AF.Identity),
               inc="a")
        a_flP = S.val("a")
        S.wait("sync", "a", a_flP)
        S.dma("sync", p_scr[stk], P_sb[:], "ps")
        ps_now = S.val("ps")
        fa_Psb = {"ps": ps_now}
        fa_Ssb = {"p": S.val("p"), "a": a_Pall}
        S.wait("sync", "ps", ps_now)
        at_before = S.val("at")
        for hh in range(8):
            h = stk * 8 + hh
            gl = hh // 4
            for aa in range(4):
                for bb in range(aa + 1):
                    rr = hh * 10 + offs[aa] + bb
                    S.wait_fa("sync", fa_pb)
                    S.dma("sync", pb_sb[:],
                          p_scr[stk, rr:rr + 1, :].broadcast_to([128, TB]), "pb")
                    pb_now = S.val("pb")
                    S.wait("vector", "pb", pb_now)
                    if bb == 0:
                        S.wait_fa("vector", fa_acc)
                    S.emit("vector", lambda e, gl=gl, bb=bb: e.tensor_tensor(
                        tmp2_sb[:], pb_sb[:],
                        ARW[:, gl * 2048 + bb * TB: gl * 2048 + (bb + 1) * TB],
                        ALU.mult), inc="v")
                    S.wait("vector", "v")
                    if bb == 0:
                        S.emit("vector", lambda e: e.tensor_copy(acc_sb[:], tmp2_sb[:]),
                               inc="v")
                    else:
                        S.emit("vector", lambda e: e.tensor_tensor(
                            acc_sb[:], acc_sb[:], tmp2_sb[:], ALU.add), inc="v")
                    fa_pb = {"v": S.val("v")}
                S.wait("vector", "v")
                S.emit("vector", lambda e: e.tensor_copy(ic_sb[:], acc_sb[:]), inc="v")
                S.wait("vector", "v")
                S.emit("vector", lambda e: e.tensor_copy(fl_sb[:], ic_sb[:, 0:8]),
                       inc="v")
                v_at = S.val("v")
                fa_acc = {"v": v_at}
                S.wait("sync", "v", v_at)
                S.wait_fa("sync", fa_icsb)
                S.dma("sync", attn_scr[h * 128:(h + 1) * 128, aa * TB:(aa + 1) * TB],
                      ic_sb[:], "at")
                fa_icsb = {"at": S.val("at")}
        fa_ary = {"v": S.val("v"), "p": S.val("p")}
    at_all = S.val("at")

    S.phase = "C_wo"
    S.wait("sync", "at", at_all)
    S.wait("sync", "p")
    S.wait("sync", "v")
    S.wait("sync", "a")
    for c in range(16):
        S.dma("sync", ARX[:, c * 2048:(c + 1) * 2048],
              attn_scr[c * 128:(c + 1) * 128, :], "s")
    arx_ld = S.val("s")
    S.wait("tensor", "s", arx_ld)
    S.dma("gpsimd", wqb[0].rearrange("p (a f) -> p a f", f=128),
          woT_b[0].rearrange("(a p) f -> p a f", p=128), "g")
    for oc in range(16):
        wb = oc % 2
        S.wait("tensor", "g", S.val("g"))
        if oc < 15:
            S.wait_fa("gpsimd", fa_wqb[(oc + 1) % 2])
            S.dma("gpsimd", wqb[(oc + 1) % 2].rearrange("p (a f) -> p a f", f=128),
                  woT_b[oc + 1].rearrange("(a p) f -> p a f", p=128), "g")
        for tb4 in range(4):
            tcol = tb4 * TB
            bank = tb4 % 2
            S.wait_fa("tensor", fa_pp[bank])
            for dc in range(16):
                S.emit("tensor", lambda e, dc=dc, wb=wb, bank=bank, tcol=tcol: e.matmul(
                    pp[bank][:], wqb[wb][:, dc * 128:(dc + 1) * 128],
                    ARX[:, dc * 2048 + tcol: dc * 2048 + tcol + TB],
                    start=(dc == 0), stop=(dc == 15)),
                    inc=("p" if dc == 15 else None))
            p_wo = S.val("p")
            S.wait_fa("sync", fa_rtf)
            S.dma("sync", rt_f[:], y2_my[oc * 128:(oc + 1) * 128, tcol:tcol + TB], "s")
            r_ld = S.val("s")
            S.wait("vector", "p", p_wo)
            S.wait("vector", "s", r_ld)
            S.wait_fa("vector", fa_ytf)
            S.emit("vector", lambda e, bank=bank: e.tensor_tensor(
                yt_f[:], pp[bank][:], rt_f[:], ALU.add), inc="v")
            S.wait("vector", "v")
            S.emit("vector", lambda e: e.tensor_copy(fl_sb[:], yt_f.bitcast(BF16)[:, 0:8]),
                   inc="v")
            v_y = S.val("v")
            fa_pp[bank] = {"v": v_y}
            fa_rtf = {"v": v_y}
            S.wait("sync", "v", v_y)
            S.dma("sync", y3_scr[oc * 128:(oc + 1) * 128, tcol:tcol + TB], yt_f[:], "y3")
            fa_ytf = {"y3": S.val("y3")}
        fa_wqb[wb] = {"p": S.val("p")}
    y3_all = S.val("y3")

    S.phase = "D_ln"
    for eng in ("sync", "gpsimd", "tensor", "vector", "scalar"):
        S.wait(eng, "y3", y3_all)
        S.wait(eng, "p")
        S.wait(eng, "v")
        S.wait(eng, "a")
    v_xln2 = ln_to_arx(y3_scr)
    S.wait("tensor", "v", v_xln2)

    S.phase = "D_router"
    for tt_ in range(16):
        c_t = (tt_ % 4)
        S.wait_fa("tensor", fa_pp[0] if tt_ % 2 == 0 else fa_pp[1])
        bank = tt_ % 2
        for dc in range(16):
            S.emit("tensor", lambda e, dc=dc, tt_=tt_, bank=bank: e.matmul(
                pp[bank][:, 0:8],
                ARX[:, dc * 2048 + tt_ * 128: dc * 2048 + (tt_ + 1) * 128],
                rw_sb[:, dc * 8:(dc + 1) * 8],
                start=(dc == 0), stop=(dc == 15)),
                inc=("p" if dc == 15 else None))
        p_r = S.val("p")
        S.wait("scalar", "p", p_r)
        S.emit("scalar", lambda e, tt_=tt_, bank=bank: e.activation(
            ro_sb[:, tt_ * 8:(tt_ + 1) * 8], pp[bank][:, 0:8], AF.Identity), inc="a")
        if tt_ % 2 == 0:
            fa_pp[0] = {"a": S.val("a")}
        else:
            fa_pp[1] = {"a": S.val("a")}
    a_ro = S.val("a")
    S.wait("scalar", "a", a_ro)
    S.emit("scalar", lambda e: e.activation(e_sb[:], ro_sb[:], AF.Exp), inc="a")
    a_e = S.val("a")
    S.wait("vector", "a", a_e)
    for t_ in range(16):
        S.emit("vector", lambda e, t_=t_: e.reduce_max(
            m1_sb[:, t_:t_ + 1], e_sb[:, t_ * 8:(t_ + 1) * 8], axis=AX.X),
            inc=("v" if t_ == 15 else None))
    S.wait("vector", "v")
    for t_ in range(16):
        S.emit("vector", lambda e, t_=t_: e.tensor_single_scalar(
            eq_sb[:, t_ * 8:(t_ + 1) * 8], e_sb[:, t_ * 8:(t_ + 1) * 8],
            m1_sb[:, t_:t_ + 1], ALU.is_ge), inc=("v" if t_ == 15 else None))
    S.wait("vector", "v")
    S.emit("vector", lambda e: e.scalar_tensor_tensor(
        me_sb[:], eq_sb[:], -1e30, e_sb[:], ALU.mult, ALU.add), inc="v")
    S.wait("vector", "v")
    for t_ in range(16):
        S.emit("vector", lambda e, t_=t_: e.reduce_max(
            m2_sb[:, t_:t_ + 1], me_sb[:, t_ * 8:(t_ + 1) * 8], axis=AX.X),
            inc=("v" if t_ == 15 else None))
    S.wait("vector", "v")
    for t_ in range(16):
        S.emit("vector", lambda e, t_=t_: e.tensor_single_scalar(
            en_sb[:, t_ * 8:(t_ + 1) * 8], e_sb[:, t_ * 8:(t_ + 1) * 8],
            m2_sb[:, t_:t_ + 1], ALU.is_ge), inc=("v" if t_ == 15 else None))
    S.emit("vector", lambda e: e.tensor_tensor(den_sb[:], m1_sb[:], m2_sb[:], ALU.add),
           inc="v")
    S.wait("vector", "v")
    S.emit("vector", lambda e: e.reciprocal(den_sb[:], den_sb[:]))
    S.emit("vector", lambda e: e.tensor_tensor(en_sb[:], en_sb[:], e_sb[:], ALU.mult),
           inc="v")
    S.wait("vector", "v")
    for t_ in range(16):
        S.emit("vector", lambda e, t_=t_: e.tensor_single_scalar(
            comb_sb[:, t_ * 8:(t_ + 1) * 8], en_sb[:, t_ * 8:(t_ + 1) * 8],
            den_sb[:, t_:t_ + 1], ALU.mult), inc=("v" if t_ == 15 else None))
    S.wait("vector", "v")
    S.emit("vector", lambda e: e.tensor_copy(fl_sb[:], comb_sb.bitcast(BF16)[:, 0:8]),
           inc="v")
    v_comb = S.val("v")
    S.wait("tensor", "v", v_comb)
    S.wait_fa("tensor", fa_ptf)
    S.emit("tensor", lambda e: e.transpose(ptf[:], comb_sb[:], idf_sb[:]), inc="p")
    p_ct = S.val("p")
    S.wait("vector", "p", p_ct)
    S.emit("vector", lambda e: e.tensor_copy(combT_sb[:], ptf[:]), inc="v")
    S.wait("vector", "v")
    S.emit("vector", lambda e: e.tensor_copy(fl_sb[:], combT_sb.bitcast(BF16)[:, 0:8]),
           inc="v")
    v_cT = S.val("v")
    fa_ptf = {"v": v_cT}
    S.wait("sync", "v", v_cT)
    S.dma("sync", cb_scr[:], combT_sb[:], "cb")
    S.wait("sync", "cb")

    S.phase = "D_exp"
    S.dma("gpsimd", wgu[0][:, 0:2048].rearrange("p (a f) -> p a f", f=128),
          wg_b[0, 0].rearrange("(a p) f -> p a f", p=128), "g")
    S.dma("gpsimd", wgu[0][:, 2048:4096].rearrange("p (a f) -> p a f", f=128),
          wu_b[0, 0].rearrange("(a p) f -> p a f", p=128), "g")
    fa_ce = dict()
    cb_done = S.val("cb")
    for ee in range(8):
        S.wait("sync", "cb", cb_done)
        S.wait_fa("sync", fa_ce)
        S.dma("sync", ce_sb.rearrange("p (a t) -> p a t", t=128),
              cb_scr.rearrange("(a e) t -> e a t", e=8)[ee:ee + 1]
              .broadcast_to([128, 16, 128]), "ce")
        ce_now = S.val("ce")
        for hc in range(8):
            wb = (ee * 8 + hc) % 2
            nxt = ee * 8 + hc + 1
            S.wait("tensor", "g", S.val("g"))
            if nxt < 64:
                ne, nh = nxt // 8, nxt % 8
                S.wait_fa("gpsimd", fa_wgu[nxt % 2])
                S.dma("gpsimd", wgu[nxt % 2][:, 0:2048].rearrange("p (a f) -> p a f", f=128),
                      wg_b[ne, nh].rearrange("(a p) f -> p a f", p=128), "g")
                S.dma("gpsimd", wgu[nxt % 2][:, 2048:4096].rearrange("p (a f) -> p a f", f=128),
                      wu_b[ne, nh].rearrange("(a p) f -> p a f", p=128), "g")
            for tb4 in range(4):
                tcol = tb4 * TB
                S.wait_fa("tensor", fa_pp[0])
                for dc in range(16):
                    S.emit("tensor", lambda e, dc=dc, wb=wb, tcol=tcol: e.matmul(
                        pp[0][:], wgu[wb][:, dc * 128:(dc + 1) * 128],
                        ARX[:, dc * 2048 + tcol: dc * 2048 + tcol + TB],
                        start=(dc == 0), stop=(dc == 15)),
                        inc=("p" if dc == 15 else None))
                p_g = S.val("p")
                S.wait_fa("tensor", fa_pp[1])
                for dc in range(16):
                    S.emit("tensor", lambda e, dc=dc, wb=wb, tcol=tcol: e.matmul(
                        pp[1][:], wgu[wb][:, 2048 + dc * 128: 2048 + (dc + 1) * 128],
                        ARX[:, dc * 2048 + tcol: dc * 2048 + tcol + TB],
                        start=(dc == 0), stop=(dc == 15)),
                        inc=("p" if dc == 15 else None))
                p_u = S.val("p")
                S.wait("scalar", "p", p_g)
                S.wait_fa("scalar", fa_sg)
                S.emit("scalar", lambda e: e.activation(sg_sb[:], pp[0][:], AF.Silu),
                       inc="a")
                a_sg = S.val("a")
                fa_pp[0] = {"a": a_sg}
                S.wait("vector", "a", a_sg)
                S.wait("vector", "p", p_u)
                S.wait("vector", "ce", ce_now)
                S.wait_fa("vector", fa_icsb)
                S.emit("vector", lambda e: e.tensor_tensor(tmp2_sb[:], sg_sb[:], pp[1][:],
                                                           ALU.mult), inc="v")
                S.wait("vector", "v")
                S.emit("vector", lambda e, tcol=tcol: e.tensor_tensor(
                    ic_sb[:], tmp2_sb[:], ce_sb[:, tcol:tcol + TB], ALU.mult), inc="v")
                S.wait("vector", "v")
                S.emit("vector", lambda e: e.tensor_copy(fl_sb[:], ic_sb[:, 0:8]), inc="v")
                v_ic = S.val("v")
                fa_pp[1] = {"v": v_ic}
                fa_sg = {"v": v_ic}
                S.wait("sync", "v", v_ic)
                S.dma("sync", ic_scr[ee * 1024 + hc * 128: ee * 1024 + (hc + 1) * 128,
                                     tcol:tcol + TB], ic_sb[:], "ic")
                fa_icsb = {"ic": S.val("ic")}
            fa_wgu[wb] = {"p": S.val("p")}
        fa_ce = {"v": S.val("v")}

    S.phase = "D_shared"
    S.dma("gpsimd", wgu[0][:, 0:2048].rearrange("p (a f) -> p a f", f=128),
          swg_b[0].rearrange("(a p) f -> p a f", p=128), "g")
    S.dma("gpsimd", wgu[0][:, 2048:4096].rearrange("p (a f) -> p a f", f=128),
          swu_b[0].rearrange("(a p) f -> p a f", p=128), "g")
    for oc in range(16):
        wb = oc % 2
        S.wait("tensor", "g", S.val("g"))
        if oc < 15:
            S.wait_fa("gpsimd", fa_wgu[(oc + 1) % 2])
            S.dma("gpsimd", wgu[(oc + 1) % 2][:, 0:2048].rearrange("p (a f) -> p a f", f=128),
                  swg_b[oc + 1].rearrange("(a p) f -> p a f", p=128), "g")
            S.dma("gpsimd", wgu[(oc + 1) % 2][:, 2048:4096].rearrange("p (a f) -> p a f", f=128),
                  swu_b[oc + 1].rearrange("(a p) f -> p a f", p=128), "g")
        for tb4 in range(4):
            tcol = tb4 * TB
            S.wait_fa("tensor", fa_pp[0])
            for dc in range(16):
                S.emit("tensor", lambda e, dc=dc, wb=wb, tcol=tcol: e.matmul(
                    pp[0][:], wgu[wb][:, dc * 128:(dc + 1) * 128],
                    ARX[:, dc * 2048 + tcol: dc * 2048 + tcol + TB],
                    start=(dc == 0), stop=(dc == 15)),
                    inc=("p" if dc == 15 else None))
            p_g = S.val("p")
            S.wait_fa("tensor", fa_pp[1])
            for dc in range(16):
                S.emit("tensor", lambda e, dc=dc, wb=wb, tcol=tcol: e.matmul(
                    pp[1][:], wgu[wb][:, 2048 + dc * 128: 2048 + (dc + 1) * 128],
                    ARX[:, dc * 2048 + tcol: dc * 2048 + tcol + TB],
                    start=(dc == 0), stop=(dc == 15)),
                    inc=("p" if dc == 15 else None))
            p_u = S.val("p")
            S.wait("scalar", "p", p_g)
            S.wait_fa("scalar", fa_sg)
            S.emit("scalar", lambda e: e.activation(sg_sb[:], pp[0][:], AF.Silu), inc="a")
            a_sg = S.val("a")
            fa_pp[0] = {"a": a_sg}
            S.wait("vector", "a", a_sg)
            S.wait("vector", "p", p_u)
            S.wait_fa("vector", fa_icsb)
            S.emit("vector", lambda e: e.tensor_tensor(ic_sb[:], sg_sb[:], pp[1][:],
                                                       ALU.mult), inc="v")
            S.wait("vector", "v")
            S.emit("vector", lambda e: e.tensor_copy(fl_sb[:], ic_sb[:, 0:8]), inc="v")
            v_ic = S.val("v")
            fa_pp[1] = {"v": v_ic}
            fa_sg = {"v": v_ic}
            S.wait("sync", "v", v_ic)
            S.dma("sync", ish_scr[oc * 128:(oc + 1) * 128, tcol:tcol + TB], ic_sb[:], "ic")
            fa_icsb = {"ic": S.val("ic")}
        fa_wgu[wb] = {"p": S.val("p")}
    ic_all = S.val("ic")

    S.phase = "D_down"
    for eng in ("sync", "gpsimd", "tensor", "vector", "scalar"):
        S.wait(eng, "ic", ic_all)
        S.wait(eng, "p")
        S.wait(eng, "v")
        S.wait(eng, "a")

    def stripe_ic(k):
        return ARX[:, k * TB:(k + 1) * TB]

    def stripe_sh(k):
        return ARY[:, k * TB:(k + 1) * TB]

    fa_stripe = dict()
    for tb4 in range(4):
        tcol = tb4 * TB
        S.wait_fa("sync", fa_stripe)
        for k in range(64):
            S.dma("sync", stripe_ic(k), ic_scr[k * 128:(k + 1) * 128, tcol:tcol + TB], "s")
        for k in range(16):
            S.dma("sync", stripe_sh(k), ish_scr[k * 128:(k + 1) * 128, tcol:tcol + TB], "s")
        stripe_loaded = S.val("s")
        S.wait("tensor", "s", stripe_loaded)
        for dc in range(16):
            bank = dc % 2
            wb = dc % 2
            S.wait_fa("gpsimd", fa_wdn[wb])
            for ee in range(8):
                S.dma("gpsimd", wdn[wb][:, ee * 1024:(ee + 1) * 1024]
                      .rearrange("p (a f) -> p a f", f=128),
                      wd_b[ee, dc].rearrange("(a p) f -> p a f", p=128), "wl")
            S.dma("gpsimd", wdn[wb][:, 8192:10240].rearrange("p (a f) -> p a f", f=128),
                  swd_b[dc].rearrange("(a p) f -> p a f", p=128), "wl")
            wl_now = S.val("wl")
            S.wait("tensor", "wl", wl_now)
            S.wait_fa("tensor", fa_pp[bank])
            n_mm = 0
            for ee in range(8):
                for hk in range(8):
                    S.emit("tensor", lambda e, ee=ee, hk=hk, wb=wb, bank=bank: e.matmul(
                        pp[bank][:], wdn[wb][:, ee * 1024 + hk * 128: ee * 1024 + (hk + 1) * 128],
                        stripe_ic(ee * 8 + hk),
                        start=(ee == 0 and hk == 0), stop=False))
                    n_mm += 1
            for k in range(16):
                last = (k == 15)
                S.emit("tensor", lambda e, k=k, wb=wb, bank=bank, last=last: e.matmul(
                    pp[bank][:], wdn[wb][:, 8192 + k * 128: 8192 + (k + 1) * 128],
                    stripe_sh(k),
                    start=False, stop=last),
                    inc=("p" if last else None))
            p_dn = S.val("p")
            fa_wdn[wb] = {"p": p_dn}
            S.wait_fa("sync", fa_rtf)
            S.dma("sync", rt_f[:], y3_scr[dc * 128:(dc + 1) * 128, tcol:tcol + TB], "s")
            r_ld = S.val("s")
            S.wait("vector", "p", p_dn)
            S.wait("vector", "s", r_ld)
            S.wait_fa("vector", fa_ytf)
            S.emit("vector", lambda e, bank=bank: e.tensor_tensor(
                yt_f[:], pp[bank][:], rt_f[:], ALU.add), inc="v")
            S.wait("vector", "v")
            S.emit("vector", lambda e: e.tensor_copy(fl_sb[:], yt_f.bitcast(BF16)[:, 0:8]),
                   inc="v")
            v_y = S.val("v")
            fa_pp[bank] = {"v": v_y}
            fa_rtf = {"v": v_y}
            S.wait("sync", "v", v_y)
            S.dma("sync", out_my[dc * 128:(dc + 1) * 128, tcol:tcol + TB], yt_f[:], "ys")
            fa_ytf = {"ys": S.val("ys")}
        fa_stripe = {"p": S.val("p")}

    S.wait("sync", "ys", S.val("ys"))
    S.wait("gpsimd", "ys", S.val("ys"))


def stage_CD(inputs):
    wqkv = np.asarray(inputs["L1_attn_wqkv"], np.float32)
    wo = np.asarray(inputs["L1_attn_wo"], np.float32)
    wqkvT_b = np.ascontiguousarray(wqkv.T.reshape(D, 24, 128).transpose(1, 0, 2))
    woT_b = np.ascontiguousarray(wo.T.reshape(D, 16, 128).transpose(1, 0, 2))
    rw = np.ascontiguousarray(np.asarray(inputs["L1_ffn_router_w"], np.float32))
    wg = np.asarray(inputs["L1_ffn_w_gate"], np.float32)
    wu = np.asarray(inputs["L1_ffn_w_up"], np.float32)
    wdn = np.asarray(inputs["L1_ffn_w_down"], np.float32)
    wg_b = np.ascontiguousarray(wg.reshape(8, D, 8, 128).transpose(0, 2, 1, 3))
    wu_b = np.ascontiguousarray(wu.reshape(8, D, 8, 128).transpose(0, 2, 1, 3))
    wd_b = np.ascontiguousarray(wdn.reshape(8, 1024, 16, 128).transpose(0, 2, 1, 3))
    swg = np.asarray(inputs["L1_ffn_sw_gate"], np.float32)
    swu = np.asarray(inputs["L1_ffn_sw_up"], np.float32)
    swd = np.asarray(inputs["L1_ffn_sw_down"], np.float32)
    swg_b = np.ascontiguousarray(swg.reshape(D, 16, 128).transpose(1, 0, 2))
    swu_b = np.ascontiguousarray(swu.reshape(D, 16, 128).transpose(1, 0, 2))
    swd_b = np.ascontiguousarray(swd.reshape(D, 16, 128).transpose(1, 0, 2))
    ones = np.ones((128, 1), dtype=BF)
    onesr = np.ones((1, 128), dtype=BF)
    e2 = np.zeros((128, 256), dtype=np.float32)
    e2[:, 128] = 1.0
    e2 = e2.astype(BF)
    idb = np.eye(128, dtype=np.float32).astype(BF)
    idf = np.eye(128, dtype=np.float32)
    alib = np.zeros((128, 2), dtype=np.float32)
    n = 16
    m0 = 2.0 ** (-8.0 / n)
    slopes = m0 ** np.arange(1, n + 1, dtype=np.float64)
    pairs10 = [(a, b) for a in range(4) for b in range(a + 1)]
    for stk in range(2):
        for hh in range(8):
            h = stk * 8 + hh
            for pi_, (a, b) in enumerate(pairs10):
                alib[hh * 10 + pi_, stk] = slopes[h] * (-math.sqrt(a - b))

    in_maps = []
    for i in range(NCORES):
        in_maps.append(dict(
            wqkvT_b=wqkvT_b, wo1T_b=woT_b, rw_i=rw,
            wg_b=wg_b, wu_b=wu_b, wd_b=wd_b,
            swg_b=swg_b, swu_b=swu_b, swd_b=swd_b,
            e2_i=e2, idb_i=idb, idf_i=idf,
            alib_i=alib,
        ))
    return in_maps



def build_fused():
    nc = bass.Bass("TRN2", target_bir_lowering=False)
    G = {}
    DT = nc.dram_tensor

    G["wdq"] = DT("wdq", [D, CP], F32, kind="ExternalInput").ap()
    G["wdkv"] = DT("wdkv", [D, CP], F32, kind="ExternalInput").ap()
    G["wuq_m"] = DT("wuq_m", [CP, 256], F32, kind="ExternalInput").ap()
    G["wk_m"] = DT("wk_m", [CP, 256], F32, kind="ExternalInput").ap()
    G["wv_m"] = DT("wv_m", [CP, 256], F32, kind="ExternalInput").ap()
    G["qg_i"] = DT("qg_i", [128, 4], F32, kind="ExternalInput").ap()
    G["qb_i"] = DT("qb_i", [128, 4], F32, kind="ExternalInput").ap()
    G["kg_i"] = DT("kg_i", [128, 4], F32, kind="ExternalInput").ap()
    G["kb_i"] = DT("kb_i", [128, 4], F32, kind="ExternalInput").ap()
    G["ones_i"] = DT("ones_i", [128, 1], BF16, kind="ExternalInput").ap()
    G["onesr_i"] = DT("onesr_i", [1, 128], BF16, kind="ExternalInput").ap()
    G["trim_i"] = DT("trim_i", [4, 128, TB], BF16, kind="ExternalInput").ap()
    G["x_my"] = DT("x_my", [D, TT], F32, kind="ExternalInput").ap()
    G["woT_b"] = DT("woT_b", [16, D, 128], F32, kind="ExternalInput").ap()
    G["pk_g"] = DT("pk_g", [128, 1024, 128], I16, kind="ExternalInput").ap()
    G["pk_u"] = DT("pk_u", [128, 1024, 128], I16, kind="ExternalInput").ap()
    G["pk_d"] = DT("pk_d", [8192, D], I16, kind="ExternalInput").ap()
    G["sc_g"] = DT("sc_g", [64, NMLP], F32, kind="ExternalInput").ap()
    G["sc_u"] = DT("sc_u", [64, NMLP], F32, kind="ExternalInput").ap()
    G["sc_d"] = DT("sc_d", [512, D], F32, kind="ExternalInput").ap()
    G["wqkvT_b"] = DT("wqkvT_b", [24, D, 128], F32, kind="ExternalInput").ap()
    G["wo1T_b"] = DT("wo1T_b", [16, D, 128], F32, kind="ExternalInput").ap()
    G["rw_i"] = DT("rw_i", [D, 8], F32, kind="ExternalInput").ap()
    G["wg_b"] = DT("wg_b", [8, 8, D, 128], F32, kind="ExternalInput").ap()
    G["wu_b"] = DT("wu_b", [8, 8, D, 128], F32, kind="ExternalInput").ap()
    G["wd_b"] = DT("wd_b", [8, 16, 1024, 128], F32, kind="ExternalInput").ap()
    G["swg_b"] = DT("swg_b", [16, D, 128], F32, kind="ExternalInput").ap()
    G["swu_b"] = DT("swu_b", [16, D, 128], F32, kind="ExternalInput").ap()
    G["swd_b"] = DT("swd_b", [16, D, 128], F32, kind="ExternalInput").ap()
    G["e2_i"] = DT("e2_i", [128, 256], BF16, kind="ExternalInput").ap()
    G["idb_i"] = DT("idb_i", [128, 128], BF16, kind="ExternalInput").ap()
    G["idf_i"] = DT("idf_i", [128, 128], F32, kind="ExternalInput").ap()
    G["alib_i"] = DT("alib_i", [128, 2], F32, kind="ExternalInput").ap()

    G["out_my"] = DT("out_my", [D, TT], F32, kind="ExternalOutput").ap()

    G["k_scr"] = DT("k_scr", [256, T], BF16, kind="Internal").ap()
    G["cbounce"] = DT("cbounce", [1, 8], F32, kind="Internal").ap()
    G["v_scr"] = DT("v_scr", [T, 256], BF16, kind="Internal").ap()
    a2a_in_t = DT("a2a_in", [D, TT], BF16, kind="Internal")
    a2a_out_t = DT("a2a_out", [D, TT], BF16, kind="Internal")
    G["a2a_in"] = a2a_in_t.ap()
    G["a2a_out"] = a2a_out_t.ap()
    G["agq_in_t"] = DT("agq_in", [2 * CP, TT], BF16, kind="Internal")
    G["agq_out_t"] = DT("agq_out", [NCORES * 2 * CP, TT], BF16, kind="Internal")
    G["agq_in"] = G["agq_in_t"].ap()
    G["agq_out"] = G["agq_out_t"].ap()
    G["resid_scr"] = DT("resid_scr", [D, TT], F32, kind="Internal").ap()
    G["xeo_scr"] = DT("xeo_scr", [D, TT], BF16, kind="Internal").ap()
    G["inter_scr"] = DT("inter_scr", [NMLP, TT], BF16, kind="Internal").ap()
    G["wd_scr"] = DT("wd_scr", [16, NMLP, 128], BF16, kind="Internal").ap()
    G["y2_scr"] = DT("y2_scr", [D, TT], F32, kind="Internal").ap()
    G["y3_scr"] = DT("y3_scr", [D, TT], F32, kind="Internal").ap()
    G["attn_scr"] = DT("attn_scr", [D, TT], BF16, kind="Internal").ap()
    G["p_scr"] = DT("p_scr", [2, 128, TB], BF16, kind="Internal").ap()
    G["cb_scr"] = DT("cb_scr", [128, 128], F32, kind="Internal").ap()
    G["ic_scr"] = DT("ic_scr", [8 * 1024, TT], BF16, kind="Internal").ap()
    G["ish_scr"] = DT("ish_scr", [D, TT], BF16, kind="Internal").ap()

    A = nc.alloc_sbuf_tensor
    G["ARX"] = A("ARX", [128, 32768], BF16)
    G["ARY"] = A("ARY", [128, 16384], BF16)
    G["ARZW"] = A("ARZW", [128, 16384], BF16)
    G["WU"] = A("WU", [128, 20480], BF16)
    G["S_sb"] = A("S_sb", [128, TB], BF16)
    G["sT_sb"] = A("sT_sb", [128, TB], BF16)
    G["P_sb"] = A("P_sb", [128, TB], BF16)
    G["tmp_sb"] = [A(f"tmp_sb{i}", [128, TB], BF16) for i in range(2)]
    G["pb_sb"] = A("pb_sb", [128, TB], BF16)
    G["acc_sb"] = A("acc_sb", [128, TB], F32)
    G["tmp2_sb"] = A("tmp2_sb", [128, TB], F32)
    G["rt_f"] = A("rt_f", [128, TB], F32)
    G["yt_f"] = A("yt_f", [128, TB], F32)
    G["fl_sb"] = A("fl_sb", [128, 8], BF16)
    G["ones_sb"] = A("ones_sb", [128, 1], BF16)
    G["onesr_sb"] = A("onesr_sb", [1, 128], BF16)
    G["e2_sb"] = A("e2_sb", [128, 256], BF16)
    G["idb_sb"] = A("idb_sb", [128, 128], BF16)
    G["idf_sb"] = A("idf_sb", [128, 128], F32)
    G["alib_sb"] = A("alib_sb", [128, 2], F32)
    G["rw_sb"] = A("rw_sb", [128, 16 * 8], BF16)
    G["nmx"] = A("nmx", [128, 8], F32)
    G["rec1"] = A("rec1", [128, 8], F32)
    G["ro_sb"] = A("ro_sb", [128, 128], F32)
    G["e_sb"] = A("e_sb", [128, 128], F32)
    G["m1_sb"] = A("m1_sb", [128, 16], F32)
    G["m2_sb"] = A("m2_sb", [128, 16], F32)
    G["den_sb"] = A("den_sb", [128, 16], F32)
    G["eq_sb"] = A("eq_sb", [128, 128], F32)
    G["me_sb"] = A("me_sb", [128, 128], F32)
    G["en_sb"] = A("en_sb", [128, 128], F32)
    G["comb_sb"] = A("comb_sb", [128, 128], F32)
    G["combT_sb"] = A("combT_sb", [128, 128], F32)
    G["ce_sb"] = A("ce_sb", [128, TT], F32)
    G["sg_sb"] = A("sg_sb", [128, TB], BF16)
    G["ic_sb"] = A("ic_sb", [128, TB], BF16)
    G["st_f"] = [A(f"st_f{i}", [1, TB], F32) for i in range(3)]
    G["st_f"].append(G["st_f"][2])
    G["st_b"] = [A(f"st_b{i}", [1, TB], BF16) for i in range(2)]
    G["sq_sb"] = G["sg_sb"]
    G["xb_sb"] = G["ic_sb"]
    G["qg_sb"] = A("qg_sb", [128, 4], F32)
    G["qb_sb"] = A("qb_sb", [128, 4], F32)
    G["kg_sb"] = A("kg_sb", [128, 4], F32)
    G["kb_sb"] = A("kb_sb", [128, 4], F32)
    G["qn2"] = A("qn2", [1, 2 * (T // TB)], F32)
    G["kn2"] = A("kn2", [1, 2 * (T // TB)], F32)
    G["sc_sb"] = A("sc_sb", [1, 8], F32)
    G["cb_sb"] = A("cb_sb", [128, 1], F32)

    PT = nc.alloc_psum_tensor
    G["pp"] = [PT(f"pp{i}", [128, TB], F32) for i in range(3)]
    G["bc"] = [PT(f"bc{i}", [128, TB], F32) for i in range(2)]
    G["stp"] = [PT(f"stp{i}", [1, TB], F32) for i in range(2)]
    G["sum_ps"] = PT("sum_ps", [1, TB], F32)
    G["ptb"] = G["pp"][2].bitcast(BF16)[:, 0:128]
    G["ptf"] = G["pp"][2][:, 128:256]

    S = Sched(nc)

    def barrier():
        for eng in Sched.ENGS:
            for name in list(S.cnt):
                S.wait(eng, name)

    S.phase = "A_qkv"
    emit_A(nc, S, G)
    S.phase = "a2a"
    barrier()
    ccs = S.sem("cc")
    S.ops["gpsimd"].append(("a2a",
        lambda e: e.collective_compute(
            "AllToAll", mybir.AluOpType.bypass,
            replica_groups=[list(range(NCORES))],
            ins=[a2a_in_t.ap().opt()],
            outs=[a2a_out_t.ap().opt()],
        ).then_inc(ccs)))
    S.cnt["cc"] += 1
    barrier()
    emit_B1(nc, S, G)
    barrier()
    emit_CD(nc, S, G)
    S.build()
    return nc, S.n_ops()


def stage_fused(inputs):
    x = np.asarray(inputs["L0_attn_x"], dtype=np.float32)
    x_fm = np.ascontiguousarray(x.transpose(2, 0, 1).reshape(D, T))
    maps_a = stage_A(inputs)
    maps_b = stage_B1(inputs, x_fm)
    maps_cd = stage_CD(inputs)
    in_maps = []
    for i in range(NCORES):
        m = dict(maps_a[i])
        m.update(maps_b[i])
        m.update(maps_cd[i])
        in_maps.append(m)
    return in_maps


def kernel(**inputs):
    from concourse.aot_env import allow_ici_all_to_all
    with allow_ici_all_to_all():
        nc, _ = build_fused()
        in_maps = stage_fused(inputs)
        res = run_bass_kernel_spmd(nc, in_maps, core_ids=list(range(NCORES)))
    out = np.zeros((D, T), np.float32)
    for i in range(NCORES):
        out[:, token_cols(i)] = res.results[i]["out_my"]
    return np.ascontiguousarray(
        out.reshape(D, B, SL).transpose(1, 2, 0)).astype(np.float32)
